# version 13
# baseline (speedup 1.0000x reference)
import math
import os
import numpy as np
import ml_dtypes
from contextlib import ExitStack
import concourse.bass as bass
import concourse.mybir as mybir
from concourse.bass_utils import run_bass_kernel_spmd

F32 = mybir.dt.float32
BF16 = mybir.dt.bfloat16
ALU = mybir.AluOpType
AF = mybir.ActivationFunctionType
AX = mybir.AxisListType

B, S, D, H, HD = 4, 4096, 1024, 16, 64
NKV, NQT, NQ, QSTN = 32, 16, 2048, 4
PC = [32, 24, 16, 8]
Q0S = {0: [3584, 2048, 1536, 0], 1: [3072, 2560, 1024, 512]}
EPS = 1e-5
INV1024 = float(np.float32(1.0 / 1024.0))
DEBUG = 0

VE_P0 = 35
VE_P1 = lambda i, st: VE_P0 + 5 * i + st
VE_P1_END = VE_P0 + 5 * NKV
VE_P2 = lambda i, st: VE_P1_END + 7 * i + st
VE_P2_END = VE_P1_END + 7 * NQT
SE_P1 = lambda i, st: 1 + 2 * i + st
SE_P1_END = 1 + 2 * NKV
SE_P2 = lambda i, st: SE_P1_END + 3 * i + st
MM_P0 = 34
MM_P1 = lambda i, st: MM_P0 + 4 * i + st
MM_P1_END = MM_P0 + 4 * NKV
MM_P2 = lambda i, st: MM_P1_END + 4 * i + st
MM_P2_END = MM_P1_END + 4 * NQT
ST_P1_END = 32 * NKV


def _theta_red_hilo(pos, shift):
    i = np.arange(HD // 2, dtype=np.float64)[None, :]
    inv = np.exp(-2.0 * i / HD * math.log(10000.0))
    th = pos[:, None] * inv + shift
    k = np.floor((th + math.pi) / (2 * math.pi))
    thr = th - 2 * math.pi * k
    hi = thr.astype(np.float32)
    lo = (thr - hi.astype(np.float64)).astype(np.float32)
    return hi, lo


def host_prep(inputs):
    x = np.ascontiguousarray(inputs["L0_attn_x"], dtype=np.float32)
    ident = np.eye(128, dtype=ml_dtypes.bfloat16)
    ident8 = (np.eye(128, dtype=np.float32) * 0.125).astype(ml_dtypes.bfloat16)
    pos_kv = np.arange(S, dtype=np.float64)
    tkh_s, tkl_s = _theta_red_hilo(pos_kv, 0.0)
    tkh_c, tkl_c = _theta_red_hilo(pos_kv, math.pi / 2)
    indA = np.zeros((1, 128), dtype=np.float32); indA[0, :64] = 1.0
    indB = np.zeros((1, 128), dtype=np.float32); indB[0, 64:] = 1.0
    shared = {
        "ident": ident, "ident8": ident8,
        "ones_row": np.ones((1, 128), dtype=ml_dtypes.bfloat16),
        "ones_col": np.ones((128, 1), dtype=ml_dtypes.bfloat16),
        "indA": indA, "indB": indB,
        "ident_f": np.eye(128, dtype=np.float32),
        "ones_row_f": np.ones((1, 128), dtype=np.float32),
        "wq": inputs["L0_attn_wq"], "wk": inputs["L0_attn_wk"],
        "wv": inputs["L0_attn_wv"], "wo": inputs["L0_attn_wo"],
        "bq": np.ascontiguousarray(inputs["L0_attn_bq"].reshape(1, D)),
        "bk": np.ascontiguousarray(inputs["L0_attn_bk"].reshape(1, D)),
        "bv": np.ascontiguousarray(inputs["L0_attn_bv"].reshape(1, D)),
        "bo": np.ascontiguousarray(inputs["L0_attn_bo"].reshape(1, D)),
        "gw": inputs["L0_ffn_gate_w"], "uw": inputs["L0_ffn_up_w"],
        "dw": inputs["L0_ffn_down_w"],
        "s_in": np.ascontiguousarray(inputs["L0_ffn_s_in"].reshape(1)),
        "s_dn": np.ascontiguousarray(inputs["L0_ffn_s_in_down"].reshape(1)),
        "tk_sin_hi": tkh_s, "tk_sin_lo": tkl_s, "tk_cos_hi": tkh_c, "tk_cos_lo": tkl_c,
    }
    in_maps = []
    for c in range(8):
        b, sub = c // 2, c % 2
        q0s = Q0S[sub]
        qrows = np.concatenate([np.arange(q0, q0 + 512) for q0 in q0s])
        pos_q = qrows.astype(np.float64)
        tqh_s, tql_s = _theta_red_hilo(pos_q, 0.0)
        tqh_c, tql_c = _theta_red_hilo(pos_q, math.pi / 2)
        masks = np.zeros((QSTN * 8, 128, 512), dtype=np.float32)
        for sl in range(QSTN):
            q0 = q0s[sl]
            for j in range(8):
                kt = PC[sl] - 8 + j
                kpos = 128 * kt + np.arange(128)[:, None]
                qpos = q0 + np.arange(512)[None, :]
                masks[sl * 8 + j] = (kpos <= qpos)
        m = dict(shared)
        m.update({
            "x_q": np.ascontiguousarray(x[b][qrows]), "x_kv": np.ascontiguousarray(x[b]),
            "tq_sin_hi": tqh_s, "tq_sin_lo": tql_s, "tq_cos_hi": tqh_c, "tq_cos_lo": tql_c,
            "masks": masks.astype(ml_dtypes.bfloat16),
        })
        in_maps.append(m)
    return in_maps


def build_program():
    nc = bass.Bass("TRN2", target_bir_lowering=False, detect_race_conditions=False)
    es = ExitStack()
    d = {}

    def din(name, shape, dt=F32):
        d[name] = nc.dram_tensor(name, shape, dt, kind="ExternalInput").ap()

    din("x_q", [NQ, D]); din("x_kv", [S, D])
    for n in ("wq", "wk", "wv", "wo"): din(n, [D, D])
    for n in ("bq", "bk", "bv", "bo"): din(n, [1, D])
    din("gw", [4096, D]); din("uw", [4096, D]); din("dw", [D, 4096])
    din("s_in", [1]); din("s_dn", [1])
    din("ident", [128, 128], BF16); din("ident8", [128, 128], BF16)
    din("ones_row", [1, 128], BF16); din("ones_col", [128, 1], BF16)
    din("indA", [1, 128]); din("indB", [1, 128])
    din("ident_f", [128, 128]); din("ones_row_f", [1, 128])
    for n in ("tq_sin_hi", "tq_sin_lo", "tq_cos_hi", "tq_cos_lo"): din(n, [NQ, HD // 2])
    for n in ("tk_sin_hi", "tk_sin_lo", "tk_cos_hi", "tk_cos_lo"): din(n, [S, HD // 2])
    din("masks", [QSTN * 8, 128, 512], BF16)

    out_d = nc.dram_tensor("out", [NQ, D], F32, kind="ExternalOutput").ap()
    spill_kind = "ExternalOutput" if DEBUG else "Internal"
    kT_sp = nc.dram_tensor("kT_sp", [128, 8, S], BF16, kind=spill_kind).ap()
    v_sp = nc.dram_tensor("v_sp", [128, NKV, D], BF16, kind=spill_kind).ap()
    xqT_sp = nc.dram_tensor("xqT_sp", [128, 8, NQ], BF16, kind=spill_kind).ap()
    s_sp = nc.dram_tensor("s_sp", [NQ, D], F32, kind=spill_kind).ap()
    wgT_sp = nc.dram_tensor("wgT_sp", [128, 8, 4096], BF16, kind=spill_kind).ap()
    wuT_sp = nc.dram_tensor("wuT_sp", [128, 8, 4096], BF16, kind=spill_kind).ap()
    wdT_sp = nc.dram_tensor("wdT_sp", [128, 32, D], BF16, kind=spill_kind).ap()
    qT_sp = nc.dram_tensor("qT_sp", [128, 8, NQ], BF16, kind=spill_kind).ap()
    amax_sp = nc.dram_tensor("amax_sp", [128, 4096], F32, kind=spill_kind).ap()

    SB_BF = 62464
    bfp = es.enter_context(nc.sbuf_tensor("bfp", [128, SB_BF], BF16))

    def bfv(off, shape):
        n = int(np.prod(shape))
        apv = bfp[:, off:off + n]
        if len(shape) > 1:
            letters = "abcdefg"[: len(shape)]
            apv = apv.rearrange(f"p ({' '.join(letters)}) -> p {' '.join(letters)}",
                                **{letters[i]: shape[i] for i in range(len(shape))})
        return apv

    qT = bfv(0, [8, NQ])
    wT = {m: bfv(16384 + i * 8192, [8, 1024])
          for i, m in enumerate(("wq", "wk", "wv", "wo"))}
    xtile_a = bfv(49152, [1024])
    xtile_b = bfv(50176, [1024])
    xq_stage = bfv(51200, [1024])
    xq_par = [bfv(51200, [1024]), bfv(34816, [1024])]
    stage8 = bfv(52224, [2, 1024])
    wbig_bf = bfv(54272, [8, 1024])
    xqT = bfv(0, [8, NQ])
    wg_par = [bfv(16384, [8, 512]), bfv(41472, [8, 512])]
    wu_par = [bfv(20480, [8, 512]), bfv(45568, [8, 512])]
    hqT_t = [bfv(24576 + 4096 * i, [32, 128]) for i in range(4)]
    hqstage = bfv(40960, [512])
    wd_par = [bfv(54272, [4, 1024]), bfv(58368, [4, 1024])]
    kt_par = [bfv(16384, [4096]), bfv(54272, [4096])]
    v_par = [bfv(20480, [4096]), bfv(58368, [4096])]
    maskbuf = bfv(24576, [8, 512])
    accT = bfv(28672, [8, 512])
    expTb = bfv(32768, [2, 1024])

    SB_F32 = 18432
    fpool = es.enter_context(nc.sbuf_tensor("fpool", [128, SB_F32], F32))

    def fv(off, shape):
        n = int(np.prod(shape))
        apv = fpool[:, off:off + n]
        if len(shape) > 1:
            letters = "abcdefg"[: len(shape)]
            apv = apv.rearrange(f"p ({' '.join(letters)}) -> p {' '.join(letters)}",
                                **{letters[i]: shape[i] for i in range(len(shape))})
        return apv

    qs = [fv(i * 1024, [1024]) for i in range(4)]
    lnx_a = fv(4096, [1024]); lnx_b = fv(5120, [1024])
    sqt = fv(6144, [1024])
    xnf = fv(7168, [1024])
    ropet = fv(8192, [6, 512])
    tab_ks = fv(11264, [32, 32]); tab_kc = fv(12288, [32, 32])
    tab_qs = fv(13312, [16, 32]); tab_qc = fv(13824, [16, 32])
    dqsc = fv(14336, [8, 128])
    dqf = fv(14336, [1024])
    stile = fv(17408, [1024])
    amaxc = fv(8192, [2048])
    wf_a = fv(4096, [1024]); wf_b = fv(5120, [1024])
    wf4_a = fv(4096, [1024]); wf4_b = fv(5120, [1024])
    stile2 = fv(10240, [1024])
    xtile2 = fv(11264, [1024])
    gtile2 = fv(12288, [1024])
    hidf4 = fv(4096, [4, 512])
    hsg_par = [fv(6144, [512]), fv(6656, [512])]
    cmbt = fv(13312, [1024])
    sgn_buf = fv(15360, [1024])
    evacf = fv(16384, [1024])

    th_sb = fpool[:, 0:6144]
    bias_sb = fpool[0:1, 6144:6144 + 4 * D]
    scl = es.enter_context(nc.sbuf_tensor("scl", [128, 16], F32))

    def scol(i):
        return scl[:, i:i + 1]

    psf = es.enter_context(nc.psum_tensor("psf", [128, 3584], F32))
    psb = es.enter_context(nc.psum_tensor("psb", [128, 1024], BF16))

    identf_sb = es.enter_context(nc.sbuf_tensor("identf_sb", [128, 128], F32))
    onesrf_sb = es.enter_context(nc.sbuf_tensor("onesrf_sb", [1, 128], F32))
    ident_sb = es.enter_context(nc.sbuf_tensor("ident_sb", [128, 128], BF16))
    ident8_sb = es.enter_context(nc.sbuf_tensor("ident8_sb", [128, 128], BF16))
    onesr_sb = es.enter_context(nc.sbuf_tensor("onesr_sb", [1, 128], BF16))
    onesc_sb = es.enter_context(nc.sbuf_tensor("onesc_sb", [128, 1], BF16))
    indA_sb = es.enter_context(nc.sbuf_tensor("indA_sb", [1, 128], F32))
    indB_sb = es.enter_context(nc.sbuf_tensor("indB_sb", [1, 128], F32))
    biasb_sb = es.enter_context(nc.sbuf_tensor("biasb_sb", [1, 2 * D], BF16))
    beff_sb = es.enter_context(nc.sbuf_tensor("beff_sb", [1, D], BF16))
    bvcolf_sb = es.enter_context(nc.sbuf_tensor("bvcolf_sb", [128, 8], F32))
    bvcol_sb = es.enter_context(nc.sbuf_tensor("bvcol_sb", [128, 8], BF16))
    gs_sb = es.enter_context(nc.sbuf_tensor("gs_sb", [128, 16], F32))
    rl_sb = es.enter_context(nc.sbuf_tensor("rl_sb", [1, 1024], F32))

    sems = {}
    cnt = {}
    for n in ("ld", "gld", "mmA", "veA", "seA", "stB", "fin", "ldD", "mmD", "seD", "veD", "stD",
              "ldE", "mmE", "veE", "seE", "stE", "ldF", "mmF", "veF", "seF", "stF", "outS"):
        sems[n] = es.enter_context(nc.semaphore("s_" + n))
        cnt[n] = 0
    ck = {}

    def inc(name, instr, k=1):
        instr.then_inc(sems[name], k)
        cnt[name] += k
        return cnt[name]

    mmD_b, seD_b, veD_b, ldD_b = {}, {}, {}, {}
    mm_c = se_c = ve_c = ld_c = 0
    for qst in range(QSTN):
        cntk = PC[qst]
        ld_c += 16
        ldD_b[("mask", qst)] = ld_c
        for hg in range(8):
            ld_c += 32
            ldD_b[("kv", qst, hg)] = ld_c
            mmD_b[("base", qst, hg)] = mm_c
            mm_c += 2 * cntk
            mmD_b[("av", qst, hg)] = mm_c
            seD_b[("expb", qst, hg)] = se_c
            se_c += cntk
            seD_b[("exp", qst, hg)] = se_c
            veD_b[("mskb", qst, hg)] = ve_c
            ve_c += 8
            veD_b[("msk", qst, hg)] = ve_c
            ve_c += 1
            veD_b[("rcp", qst, hg)] = ve_c
            mm_c += 1
            mmD_b[("rl", qst, hg)] = mm_c
            ve_c += 1
            veD_b[("accbf", qst, hg)] = ve_c
        mm_c += 4
        mmD_b[("wo", qst)] = mm_c
        ve_c += 4
        veD_b[("sev", qst)] = ve_c
    MM_P4_END, SE_P4_END, VE_P4_END, LD_P4_END = mm_c, se_c, ve_c, ld_c
    ST_P4_END = 16 * 4 * QSTN

    NBLK = 8 * QSTN
    P5U = [("A", 0, r) for r in range(32)] + [("chain", 0)]
    bq = [("B", mi, r) for mi in range(3) for r in range(32)]
    bq_i = 0
    for mi_a in (1, 2):
        ar = 0
        while ar < 32:
            P5U.append(("A", mi_a, ar)); ar += 1
            if ar < 32:
                P5U.append(("A", mi_a, ar)); ar += 1
            if bq_i < len(bq) and bq[bq_i][1] != mi_a:
                P5U.append(bq[bq_i]); bq_i += 1
        P5U.append(("chain", mi_a))
    P5U += bq[bq_i:]

    def p5cost(u):
        if u[0] == "A":
            return 1.0
        if u[0] == "chain":
            return 4.0
        return 24.0 if u[2] % 16 == 0 else 17.0

    blkw = [float(PC[qst]) for qst in range(QSTN) for _ in range(8)]
    totw = sum(blkw); totc = sum(p5cost(u) for u in P5U) * 1.25
    sched = [[] for _ in range(NBLK + 1)]
    bi = 0; used = 0.0; budget = blkw[0] / totw * totc
    nA = nB = 0
    for u in P5U:
        c = p5cost(u)
        while bi < NBLK and (used + c > budget or (u[0] == "A" and nA >= 8)
                             or (u[0] == "B" and nB >= (2 if bi % 8 == 7 else 3))):
            bi += 1
            if bi < NBLK:
                budget += blkw[bi] / totw * totc
                nA = nB = 0
        sched[min(bi, NBLK)].append(u)
        used += c
        if u[0] == "A":
            nA += 1
        elif u[0] == "B":
            nB += 1

    par5 = {}
    lseq = [u for u in P5U if u[0] in ("A", "B")]
    for p, u in enumerate(lseq):
        par5[u] = p % 2
    border = [u for u in P5U if u[0] == "B"]
    bprev = {border[k]: (border[k - 1] if k >= 1 else None) for k in range(len(border))}
    bprev2 = {border[k]: (border[k - 2] if k >= 2 else None) for k in range(len(border))}

    vprog = [[] for _ in range(NBLK + 1)]
    pend = []
    for b in range(NBLK + 1):
        for p in pend:
            vprog[b].append(("ev", p[1], p[2]))
        pend = []
        tail_blk = (b < NBLK and b % 8 == 7)
        for u in sched[b]:
            vprog[b].append(u)
            if u[0] == "B":
                if pend and not tail_blk:
                    for p in pend:
                        vprog[b].append(("ev", p[1], p[2]))
                    pend = []
                pend.append(u)
    for p in pend:
        vprog[NBLK].append(("ev", p[1], p[2]))

    ldE_b, veE_b, mmE_b, seE_b, stE_b = {}, {}, {}, {}, {}
    vc = mc = sc5 = st5 = 0
    for b in range(NBLK + 1):
        for act in vprog[b]:
            vc += 1
            if act[0] == "A":
                veE_b[("A", act[1], act[2])] = vc
            elif act[0] == "chain":
                veE_b[("chain", act[1])] = vc
            elif act[0] == "B":
                veE_b[("q", act[1], act[2])] = vc
            else:
                veE_b[("ev", act[1], act[2])] = vc
    for mi in range(3):
        for r in range(32):
            sc5 += 1; seE_b[("sgn", mi, r)] = sc5
            mc += 1; mmE_b[("tr", mi, r)] = mc
            st5 += 16; stE_b[("st", mi, r)] = st5
    gldseq = []
    for u in P5U:
        if u[0] == "A":
            gldseq.append(u)
        elif u[0] == "chain":
            if u[1] < 2:
                gldseq.append(("spl", u[1]))
        elif u[0] == "B":
            if u[2] % 16 == 0 and u[1] < 2:
                gldseq.append(("rld", u[1], u[2] // 16))
            gldseq.append(u)
    lc = 0
    for it in gldseq:
        lc += 16
        ldE_b[it] = lc
    LDE_END, VEE_END, MME_END, SEE_END, STE_END = lc, vc, mc, sc5, st5
    batches = [(mi, h) for mi in range(3) for h in range(2)]

    stB_blocks = [[a for a in vprog[b] if a[0] == "ev"] for b in range(NBLK)]
    sgn_blocks = [[u for u in sched[b] if u[0] == "B"] for b in range(NBLK)]
    t_blocks = [[u for u in sched[b] if u[0] == "B"] for b in range(NBLK + 1)]
    vsplit = []
    for b in range(NBLK):
        acts = vprog[b]
        i1 = next((i for i, a in enumerate(acts) if a[0] == "B"), None)
        if i1 is None:
            vsplit.append((list(acts), []))
        else:
            vsplit.append((list(acts[:i1]), list(acts[i1:])))

    ldF_b, veF_b, mmF_b, seF_b = {}, {}, {}, {}
    lf = vf = mf = sf = 0
    lf += 16; ldF_b["xqT"] = lf
    for qd in range(4):
        for oc in range(8):
            lf += 32; ldF_b[("gu", qd, oc)] = lf
            for t4 in range(4):
                mf += 1; mmF_b[("g", qd, oc, t4)] = mf
                mf += 1; mmF_b[("u", qd, oc, t4)] = mf
                sf += 1; seF_b[("sil", qd, oc, t4)] = sf
                vf += 1; veF_b[("hid", qd, oc, t4)] = vf
                sf += 1; seF_b[("sgn", qd, oc, t4)] = sf
            vf += 1; veF_b[("dq", qd, oc)] = vf
            for t4 in range(4):
                vf += 1; veF_b[("q", qd, oc, t4)] = vf
                mf += 1; mmF_b[("tr", qd, oc, t4)] = mf
                vf += 1; veF_b[("ev", qd, oc, t4)] = vf
        for pr in range(2):
            for og in range(8):
                lf += 16; ldF_b[("wd", qd, pr, og)] = lf
                for tp in range(2):
                    mf += 1; mmF_b[("dn", qd, pr, og, tp)] = mf
            for tp in range(2):
                sf += 1; seF_b[("g", qd, pr, tp)] = sf
                lf += 32; ldF_b[("sx", qd, pr, tp)] = lf
                vf += 1; veF_b[("cmb", qd, pr, tp)] = vf
    LDF_END, VEF_END, MMF_END, SEF_END = lf, vf, mf, sf

    sgn4s = None
    def mm_sc_val(b, kt):
        return b + (kt + 1 if kt < 2 else 2 * kt)

    def mm_av_val(b, j, cntk):
        return b + (2 * j + 3 if j <= cntk - 2 else 2 * cntk)

    sgn4s = [sgn_buf[:, 0:512], sgn_buf[:, 512:1024], xnf[:, 0:512], xnf[:, 512:1024]]

    def p5_tr(t, mi, r):
        t.wait_ge(sems["veE"], veE_b[("q", mi, r)])
        pb = bprev[("B", mi, r)]
        if pb is not None:
            t.wait_ge(sems["veE"], veE_b[("ev", pb[1], pb[2])])
        xqv = xq_par[r % 2].rearrange("p (a b) -> p a b", a=8)
        for c in range(8):
            ins = t.transpose(psb[:, c * 128:(c + 1) * 128], xqv[:, c, :], ident_sb[:])
        inc("mmE", ins)

    def p5_sgn(sc_, mi, r):
        sc_.wait_ge(sems["ldE"], ldE_b[("B", mi, r)])
        pb = bprev[("B", mi, r)]
        if pb is not None:
            sc_.wait_ge(sems["veE"], veE_b[("q", pb[1], pb[2])])
        wf = wf_a if par5[("B", mi, r)] == 0 else wf_b
        ins = sc_.activation(sgn_buf[:], wf, AF.Sign)
        inc("seE", ins)

    def p5_st(sc_, mi, r):
        sc_.wait_ge(sems["veE"], veE_b[("ev", mi, r)])
        if mi < 2:
            dsp = (wgT_sp if mi == 0 else wuT_sp)[:, :, r * 128:(r + 1) * 128]
        else:
            dsp = wdT_sp[:, (r % 4) * 8:(r % 4 + 1) * 8, (r // 4) * 128:(r // 4 + 1) * 128]
        ins = sc_.dma_start(out=dsp,
                            in_=stage8[:, r % 2, :].rearrange("p (a b) -> p a b", a=8))
        inc("stE", ins, 16)

    blk = es.enter_context(nc.Block())

    @blk.sync
    def _(sy):
        for name, dst in (("ident", ident_sb[:]), ("ident8", ident8_sb[:]),
                          ("ones_row", onesr_sb[:]), ("ones_col", onesc_sb[:]),
                          ("indA", indA_sb[:]), ("indB", indB_sb[:]),
                          ("ident_f", identf_sb[:]), ("ones_row_f", onesrf_sb[:])):
            inc("ld", sy.dma_start(out=dst, in_=d[name]), 16)
        for i, name in enumerate(("bq", "bk", "bv", "bo")):
            inc("ld", sy.dma_start(out=bias_sb[:, i * D:(i + 1) * D], in_=d[name]), 16)
        inc("ld", sy.dma_start(out=bvcolf_sb[:],
                               in_=d["bv"].rearrange("o (a p) -> p o a", p=128).squeeze(1),
                               allow_slow_non_contiguous=True), 16)
        offs = 0
        for name in ("tk_sin_hi", "tk_sin_lo", "tk_cos_hi", "tk_cos_lo"):
            inc("ld", sy.dma_start(
                out=th_sb[:, offs:offs + 1024].rearrange("p (a b) -> p a b", b=32),
                in_=d[name].rearrange("(tl p) i -> p tl i", p=128)), 16)
            offs += 1024
        for name in ("tq_sin_hi", "tq_sin_lo", "tq_cos_hi", "tq_cos_lo"):
            inc("ld", sy.dma_start(
                out=th_sb[:, offs:offs + 512].rearrange("p (a b) -> p a b", b=32),
                in_=d[name].rearrange("(tl p) i -> p tl i", p=128)), 16)
            offs += 512
        inc("ld", sy.dma_start(out=gs_sb[:, 0:1],
                               in_=d["s_in"].partition_broadcast(128).unsqueeze(-1)), 16)
        inc("ld", sy.dma_start(out=gs_sb[:, 1:2],
                               in_=d["s_dn"].partition_broadcast(128).unsqueeze(-1)), 16)
        ck["ld_const"] = cnt["ld"]

        for i in range(NKV):
            if i < 2:
                sy.wait_ge(sems["veA"], VE_P0)
            else:
                sy.wait_ge(sems["veA"], VE_P1(i - 2, 2))
                sy.wait_ge(sems["seA"], SE_P1(i - 2, 1))
            dst = lnx_a if i % 2 == 0 else lnx_b
            inc("ld", sy.dma_start(out=dst, in_=d["x_kv"][i * 128:(i + 1) * 128, :]), 16)
            ck[f"p1_ld_{i}"] = cnt["ld"]
        for i in range(NQT):
            if i < 2:
                sy.wait_ge(sems["veA"], VE_P1(NKV - 2 + i, 2))
                sy.wait_ge(sems["seA"], SE_P1(NKV - 2 + i, 1))
            else:
                sy.wait_ge(sems["veA"], VE_P2(i - 2, 2))
                sy.wait_ge(sems["seA"], SE_P2(i - 2, 1))
            dst = lnx_a if i % 2 == 0 else lnx_b
            inc("ld", sy.dma_start(out=dst, in_=d["x_q"][i * 128:(i + 1) * 128, :]), 16)
            ck[f"p2_ld_{i}"] = cnt["ld"]

        sy.wait_ge(sems["stB"], 16 * 2 * NKV)
        sy.wait_ge(sems["veA"], VE_P2_END)
        for qst in range(QSTN):
            cntk = PC[qst]
            if qst >= 1:
                sy.wait_ge(sems["veD"], veD_b[("accbf", qst - 1, 7)])
            inc("ldD", sy.dma_start(out=maskbuf,
                                    in_=d["masks"][qst * 8:(qst + 1) * 8].transpose([1, 0, 2])), 16)
            for hg in range(8):
                bi = qst * 8 + hg
                if bi >= 2:
                    p2 = ((bi - 2) // 8, (bi - 2) % 8)
                    sy.wait_ge(sems["mmD"], mmD_b[("av", p2[0], p2[1])])
                par = bi % 2
                inc("ldD", sy.dma_start(out=kt_par[par][:, 0:cntk * 128],
                                        in_=kT_sp[:, hg, 0:cntk * 128]), 16)
                inc("ldD", sy.dma_start(out=v_par[par][:, 0:cntk * 128].rearrange("p (a b) -> p a b", b=128),
                                        in_=v_sp[:, 0:cntk, hg * 128:(hg + 1) * 128]), 16)
        sy.wait_ge(sems["stD"], ST_P4_END)
        inc("ldF", sy.dma_start(out=xqT, in_=xqT_sp), 16)
        gu_seq = [(qd, oc) for qd in range(4) for oc in range(8)]
        wd_seq = [(qd, pr, og) for qd in range(4) for pr in range(2) for og in range(8)]
        for qd in range(4):
            for oc in range(8):
                gi = gu_seq.index((qd, oc))
                if gi == 0:
                    sy.wait_ge(sems["stE"], stE_b[("st", 0, 31)])
                    sy.wait_ge(sems["stE"], stE_b[("st", 1, 31)])
                elif gi >= 2:
                    p2 = gu_seq[gi - 2]
                    sy.wait_ge(sems["mmF"], mmF_b[("u", p2[0], p2[1], 3)])
                par = oc % 2
                inc("ldF", sy.dma_start(out=wg_par[par], in_=wgT_sp[:, :, oc * 512:(oc + 1) * 512]), 16)
                inc("ldF", sy.dma_start(out=wu_par[par], in_=wuT_sp[:, :, oc * 512:(oc + 1) * 512]), 16)
            for pr in range(2):
                for og in range(8):
                    wi = wd_seq.index((qd, pr, og))
                    if wi == 0:
                        sy.wait_ge(sems["stE"], stE_b[("st", 2, 31)])
                    elif wi >= 2:
                        pk = wd_seq[wi - 2]
                        sy.wait_ge(sems["mmF"], mmF_b[("dn", pk[0], pk[1], pk[2], 1)])
                    inc("ldF", sy.dma_start(out=wd_par[og % 2],
                                            in_=wdT_sp[:, og * 4:(og + 1) * 4, :]), 16)
                for tp in range(2):
                    j = 4 * qd + 2 * pr + tp
                    if j >= 1:
                        sy.wait_ge(sems["outS"], 16 * j)
                    inc("ldF", sy.dma_start(out=stile2, in_=s_sp[j * 128:(j + 1) * 128, :]), 16)
                    inc("ldF", sy.dma_start(out=xtile2, in_=d["x_q"][j * 128:(j + 1) * 128, :]), 16)
        sy.wait_ge(sems["stB"], 16 * (2 * NKV + NQT))
        if DEBUG:
            inc("ld", sy.dma_start(out=qT_sp, in_=qT), 16)
        sy.wait_ge(sems["outS"], 16 * NQT)

    @blk.gpsimd
    def _(g):
        for mi, m in enumerate(("wq", "wk", "wv", "wo")):
            if mi >= 1:
                g.wait_ge(sems["mmA"], 8 * mi)
            inc("gld", g.dma_start(out=wbig_bf, in_=d[m].rearrange("(r p) h -> p r h", p=128)), 16)
            ck[f"p0_gld_w_{mi}"] = cnt["gld"]

        def wslice(mi, r):
            mn = ("gw", "uw", "dw")[mi]
            if mn != "dw":
                return d[mn][r * 128:(r + 1) * 128, :]
            return d["dw"][(r // 4) * 128:(r // 4 + 1) * 128, (r % 4) * 1024:(r % 4 + 1) * 1024]
        g.wait_ge(sems["veA"], VE_P2_END)
        lpos = 0
        for it in gldseq:
            if it[0] == "spl":
                g.wait_ge(sems["veE"], veE_b[("chain", it[1])])
                inc("ldE", g.dma_start(out=amax_sp[:, it[1] * 2048:(it[1] + 1) * 2048],
                                       in_=amaxc[:]), 16)
            elif it[0] == "rld":
                kb = batches.index((it[1], it[2]))
                if kb >= 1:
                    pmi, ph = batches[kb - 1]
                    g.wait_ge(sems["veE"], veE_b[("q", pmi, ph * 16)])
                g.wait_ge(sems["ldE"], ldE_b[("spl", it[1])])
                off = it[1] * 2048 + it[2] * 1024
                inc("ldE", g.dma_start(out=dqf, in_=amax_sp[:, off:off + 1024]), 16)
            else:
                if lpos >= 2:
                    pu = lseq[lpos - 2]
                    if pu[0] == "A":
                        g.wait_ge(sems["veE"], veE_b[("A", pu[1], pu[2])])
                    else:
                        g.wait_ge(sems["veE"], veE_b[("q", pu[1], pu[2])])
                        g.wait_ge(sems["seE"], seE_b[("sgn", pu[1], pu[2])])
                dst = wf_a if lpos % 2 == 0 else wf_b
                inc("ldE", g.dma_start(out=dst, in_=wslice(it[1], it[2])), 16)
                lpos += 1

    @blk.tensor
    def _(t):
        t.wait_ge(sems["ld"], ck["ld_const"])
        for mi, m in enumerate(("wq", "wk", "wv", "wo")):
            t.wait_ge(sems["gld"], ck[f"p0_gld_w_{mi}"])
            idn = ident_sb
            for r in range(8):
                if mi * 8 + r >= 1:
                    t.wait_ge(sems["veA"], mi * 8 + r)
                for c in range(8):
                    ins = t.transpose(psb[:, c * 128:(c + 1) * 128],
                                      wbig_bf[:, r, c * 128:(c + 1) * 128], idn[:])
                inc("mmA", ins)
        t.wait_ge(sems["veA"], 33)
        for ch in range(2):
            for kt in range(8):
                ins = t.matmul(psf[0:1, 2048 + ch * 512: 2048 + (ch + 1) * 512],
                               bvcol_sb[:, kt:kt + 1], wT["wo"][:, kt, ch * 512:(ch + 1) * 512],
                               start=(kt == 0), stop=(kt == 7))
            inc("mmA", ins)
        assert cnt["mmA"] == MM_P0

        for i in range(NKV):
            xtv = (xtile_a if i % 2 == 0 else xtile_b).rearrange("p (a b) -> p a b", a=8)
            t.wait_ge(sems["veA"], VE_P1(i, 2))
            for c in range(8):
                ins = t.transpose(psb[:, c * 128:(c + 1) * 128], xtv[:, c, :], ident_sb[:])
            inc("mmA", ins)
            t.wait_ge(sems["veA"], VE_P1(i, 3))
            xnT = stage8[:, 0, :].rearrange("p (a b) -> p a b", a=8)
            for ch in range(2):
                for kt in range(8):
                    t.matmul(psf[:, ch * 512:(ch + 1) * 512], xnT[:, kt, :],
                             wT["wk"][:, kt, ch * 512:(ch + 1) * 512],
                             start=(kt == 0), stop=False)
                ins = t.matmul(psf[:, ch * 512:(ch + 1) * 512], onesr_sb[:],
                               biasb_sb[:, D + ch * 512: D + (ch + 1) * 512],
                               start=False, stop=True)
            inc("mmA", ins)
            for ch in range(2):
                for kt in range(8):
                    ins = t.matmul(psf[:, 1024 + ch * 512: 1024 + (ch + 1) * 512], xnT[:, kt, :],
                                   wT["wv"][:, kt, ch * 512:(ch + 1) * 512],
                                   start=(kt == 0), stop=(kt == 7))
            inc("mmA", ins)
            t.wait_ge(sems["veA"], VE_P1(i, 4))
            ktv = (xtile_a if i % 2 == 0 else xtile_b).rearrange("p (a b) -> p a b", a=8)
            for c in range(8):
                ins = t.transpose(psb[:, c * 128:(c + 1) * 128], ktv[:, c, :], ident_sb[:])
            inc("mmA", ins)
        assert cnt["mmA"] == MM_P1_END

        for i in range(NQT):
            xtv = (xtile_a if i % 2 == 0 else xtile_b).rearrange("p (a b) -> p a b", a=8)
            t.wait_ge(sems["veA"], VE_P2(i, 2))
            for c in range(8):
                ins = t.transpose(psb[:, c * 128:(c + 1) * 128], xtv[:, c, :], ident_sb[:])
            inc("mmA", ins)
            t.wait_ge(sems["veA"], VE_P2(i, 3))
            xnT = stage8[:, 0, :].rearrange("p (a b) -> p a b", a=8)
            for ch in range(2):
                for kt in range(8):
                    t.matmul(psf[:, ch * 512:(ch + 1) * 512], xnT[:, kt, :],
                             wT["wq"][:, kt, ch * 512:(ch + 1) * 512],
                             start=(kt == 0), stop=False)
                ins = t.matmul(psf[:, ch * 512:(ch + 1) * 512], onesr_sb[:],
                               biasb_sb[:, ch * 512:(ch + 1) * 512], start=False, stop=True)
            inc("mmA", ins)
            t.wait_ge(sems["veA"], VE_P2(i, 4))
            qtv = (xtile_a if i % 2 == 0 else xtile_b).rearrange("p (a b) -> p a b", a=8)
            for c in range(8):
                ins = t.transpose(psb[:, c * 128:(c + 1) * 128], qtv[:, c, :], ident_sb[:])
            inc("mmA", ins)
            t.wait_ge(sems["veA"], VE_P2(i, 6))
            xqv = xq_stage.rearrange("p (a b) -> p a b", a=8)
            for c in range(8):
                ins = t.transpose(psb[:, c * 128:(c + 1) * 128], xqv[:, c, :], ident_sb[:])
            inc("mmA", ins)
        assert cnt["mmA"] == MM_P2_END

        t.wait_ge(sems["veA"], VE_P2_END)
        for qst in range(QSTN):
            cntk = PC[qst]
            for hg in range(8):
                t.wait_ge(sems["ldD"], ldD_b[("kv", qst, hg)])
                if qst >= 1 or hg >= 1:
                    phg = (qst, hg - 1) if hg >= 1 else (qst - 1, 7)
                    t.wait_ge(sems["veD"], veD_b[("accbf", phg[0], phg[1])])
                if hg == 0 and qst >= 1:
                    t.wait_ge(sems["veD"], veD_b[("sev", qst - 1)])
                qA = qT[0:64, hg, qst * 512:(qst + 1) * 512]
                qB = qT[64:128, hg, qst * 512:(qst + 1) * 512]
                nmask_start = cntk - 8
                eb = seD_b[("expb", qst, hg)]
                mb = veD_b[("mskb", qst, hg)]
                par4 = (qst * 8 + hg) % 2
                ktall = kt_par[par4]
                vt = v_par[par4][:, 0:cntk * 128].rearrange("p (a b) -> p a b", b=128)

                def emit_sc(kt):
                    buf = kt % 2
                    off = buf * 1024
                    if kt >= 2:
                        t.wait_ge(sems["seD"], eb + kt - 1)
                    ktile = ktall[:, kt * 128:(kt + 1) * 128]
                    t.matmul(psf[:, off:off + 512], ktile[0:64, :], qA,
                             start=True, stop=True, tile_position=(0, 0))
                    ins = t.matmul(psf[:, off + 512:off + 1024], ktile[64:128, :], qB,
                                   start=True, stop=True, tile_position=(64, 0))
                    inc("mmD", ins)

                def emit_av(kt):
                    buf = kt % 2
                    t.wait_ge(sems["seD"], eb + kt + 1)
                    if kt >= nmask_start:
                        t.wait_ge(sems["veD"], mb + (kt - nmask_start) + 1)
                    ex = expTb[:, buf, :]
                    t.matmul(psf[0:1, 2560:3072], onesc_sb[:], ex[:, 0:512],
                             start=(kt == 0), stop=(kt == cntk - 1))
                    t.matmul(psf[0:1, 3072:3584], onesc_sb[:], ex[:, 512:1024],
                             start=(kt == 0), stop=(kt == cntk - 1))
                    t.matmul(psf[0:64, 2048:2560], vt[:, kt, 0:64], ex[:, 0:512],
                             start=(kt == 0), stop=(kt == cntk - 1), tile_position=(0, 0),
                             skip_group_check=True)
                    ins = t.matmul(psf[64:128, 2048:2560], vt[:, kt, 64:128], ex[:, 512:1024],
                                   start=(kt == 0), stop=(kt == cntk - 1), tile_position=(0, 64),
                                   skip_group_check=True)
                    inc("mmD", ins)

                emit_sc(0)
                emit_sc(1)
                for kt in range(2, cntk):
                    emit_av(kt - 2)
                    emit_sc(kt)
                emit_av(cntk - 2)
                emit_av(cntk - 1)
                t.wait_ge(sems["veD"], veD_b[("rcp", qst, hg)])
                t.matmul(psf[:, 1024:1536], indA_sb[:], rl_sb[0:1, 0:512], start=True, stop=False)
                ins = t.matmul(psf[:, 1024:1536], indB_sb[:], rl_sb[0:1, 512:1024], start=False, stop=True)
                inc("mmD", ins)
                if hg < 7:
                    for u in t_blocks[qst * 8 + hg]:
                        p5_tr(t, u[1], u[2])
            t.wait_ge(sems["veD"], veD_b[("accbf", qst, 7)])
            for tq in range(4):
                if qst > 0 or tq > 0:
                    t.wait_ge(sems["veD"], veD_b[("sev", qst)] - 4 + tq if tq > 0
                              else veD_b[("sev", qst - 1)])
                for ch in range(2):
                    for et in range(8):
                        t.matmul(psf[:, ch * 512:(ch + 1) * 512],
                                 accT[:, et, tq * 128:(tq + 1) * 128],
                                 wT["wo"][:, et, ch * 512:(ch + 1) * 512],
                                 start=(et == 0), stop=False)
                    ins = t.matmul(psf[:, ch * 512:(ch + 1) * 512], onesr_sb[:],
                                   beff_sb[0:1, ch * 512:(ch + 1) * 512],
                                   start=False, stop=True)
                inc("mmD", ins)
            for u in t_blocks[qst * 8 + 7]:
                p5_tr(t, u[1], u[2])
        assert cnt["mmD"] == MM_P4_END

        for u in t_blocks[NBLK]:
            p5_tr(t, u[1], u[2])
        assert cnt["mmE"] == MME_END

        t.wait_ge(sems["ldF"], ldF_b["xqT"])
        for qd in range(4):
            for oc in range(8):
                t.wait_ge(sems["ldF"], ldF_b[("gu", qd, oc)])
                par = oc % 2
                for t4 in range(4):
                    j = 4 * qd + t4
                    gch = (qd * 8 + oc) * 4 + t4
                    if gch >= 2:
                        g2 = gch - 2
                        pk = (g2 // 32, (g2 // 4) % 8, g2 % 4)
                        t.wait_ge(sems["seF"], seF_b[("sil", pk[0], pk[1], pk[2])])
                        t.wait_ge(sems["veF"], veF_b[("hid", pk[0], pk[1], pk[2])])
                    if qd > 0 and oc == 0 and t4 == 1:
                        t.wait_ge(sems["seF"], seF_b[("g", qd - 1, 1, 0)])
                    gb = 1024 * (t4 % 2)
                    for kt in range(8):
                        ins = t.matmul(psf[:, gb:gb + 512], xqT[:, kt, j * 128:(j + 1) * 128],
                                       wg_par[par][:, kt, :], start=(kt == 0), stop=(kt == 7))
                    inc("mmF", ins)
                    for kt in range(8):
                        ins = t.matmul(psf[:, gb + 512:gb + 1024], xqT[:, kt, j * 128:(j + 1) * 128],
                                       wu_par[par][:, kt, :], start=(kt == 0), stop=(kt == 7))
                    inc("mmF", ins)
                for t4 in range(4):
                    t.wait_ge(sems["veF"], veF_b[("q", qd, oc, t4)])
                    hqv = hqstage.rearrange("p (a b) -> p a b", a=4)
                    for c in range(4):
                        ins = t.transpose(psb[:, c * 128:(c + 1) * 128], hqv[:, c, :], ident_sb[:])
                    inc("mmF", ins)
            for pr in range(2):
                for og in range(8):
                    t.wait_ge(sems["ldF"], ldF_b[("wd", qd, pr, og)])
                    for tp in range(2):
                        t4 = 2 * pr + tp
                        j = 4 * qd + t4
                        if og == 0:
                            t.wait_ge(sems["veF"], veF_b[("ev", qd, 7, t4)])
                            ppr = (qd, pr - 1) if pr > 0 else ((qd - 1, 1) if qd > 0 else None)
                            if ppr is not None:
                                t.wait_ge(sems["seF"], seF_b[("g", ppr[0], ppr[1], tp)])
                        base = 1024 + 1024 * tp
                        for ot in range(4):
                            go = og * 4 + ot
                            for ch in range(2):
                                ins = t.matmul(psf[:, base + ch * 512:base + (ch + 1) * 512],
                                               hqT_t[t4][:, go, :],
                                               wd_par[og % 2][:, ot, ch * 512:(ch + 1) * 512],
                                               start=(go == 0), stop=(go == 31))
                        inc("mmF", ins)
        assert cnt["mmF"] == MMF_END

    def emit_ln(v, se_sq, se_sqrt, lnx, out_bf, want_f32):
        v.tensor_reduce(scol(0), lnx, AX.X, ALU.add)
        v.wait_ge(sems["seA"], se_sq)
        v.tensor_reduce(scol(1), sqt[:], AX.X, ALU.add)
        v.drain()
        v.tensor_single_scalar(scol(2), scol(0), INV1024, ALU.mult)
        v.tensor_single_scalar(scol(4), scol(1), INV1024, ALU.mult)
        v.drain()
        v.tensor_scalar(sqt[:], lnx, scol(2), None, ALU.subtract)
        v.tensor_tensor(scol(3), scol(2), scol(2), ALU.mult)
        v.drain()
        v.scalar_tensor_tensor(scol(5), scol(3), -1.0, scol(4), ALU.mult, ALU.add)
        v.drain()
        v.tensor_single_scalar(scol(6), scol(5), EPS, ALU.add)
        v.drain()
        ins = v.reciprocal(scol(7), scol(6))
        inc("veA", ins)
        v.wait_ge(sems["seA"], se_sqrt)
        if want_f32:
            v.tensor_scalar(xnf[:], sqt[:], scol(8), None, ALU.mult)
        ins = v.tensor_scalar(out_bf, sqt[:], scol(8), None, ALU.mult)
        inc("veA", ins)

    def emit_rope(v, psrc_off, tabs, tabc, tabi, out_bf):
        v.tensor_copy(ropet[:, 0, :].rearrange("p (a b) -> p a b", b=32),
                      tabc[:, tabi, :].unsqueeze(1).broadcast_to([128, 16, 32]))
        v.tensor_copy(ropet[:, 1, :].rearrange("p (a b) -> p a b", b=32),
                      tabs[:, tabi, :].unsqueeze(1).broadcast_to([128, 16, 32]))
        v.tensor_copy(evacf[:], psf[:, psrc_off:psrc_off + 1024])
        v.drain()
        ev = evacf[:].rearrange("p (a two) -> p a two", two=2)
        e1 = ev[:, :, 0]; e2 = ev[:, :, 1]
        ob = out_bf.rearrange("p (a two) -> p a two", two=2)
        v.tensor_tensor(ropet[:, 2, :], e1, ropet[:, 0, :], ALU.mult)
        v.tensor_tensor(ropet[:, 3, :], e2, ropet[:, 1, :], ALU.mult)
        v.tensor_tensor(ropet[:, 4, :], e1, ropet[:, 1, :], ALU.mult)
        v.tensor_tensor(ropet[:, 5, :], e2, ropet[:, 0, :], ALU.mult)
        v.scalar_tensor_tensor(ob[:, :, 0], ropet[:, 3, :], -1.0, ropet[:, 2, :], ALU.mult, ALU.add)
        ins = v.scalar_tensor_tensor(ob[:, :, 1], ropet[:, 4, :], 1.0, ropet[:, 5, :], ALU.mult, ALU.add)
        return ins


    def emit_dq_chain(v, jobs):
        def both(f):
            for j in jobs:
                f(j)
        both(lambda j: v.tensor_scalar(j["sc"], j["am"], j["gs6"], 448.0, ALU.mult, ALU.min))
        both(lambda j: v.memset(j["pe"], 2.0 ** -6))
        for mlev in (8, 4, 2, 1):
            both(lambda j: v.tensor_single_scalar(j["th"], j["pe"], float(2.0 ** mlev), ALU.mult))
            both(lambda j: v.tensor_tensor(j["cc"], j["sc"], j["th"], ALU.is_ge))
            both(lambda j: v.tensor_scalar(j["fac"], j["cc"], float(2.0 ** mlev - 1), 1.0,
                                           ALU.mult, ALU.add))
            both(lambda j: v.tensor_tensor(j["pe"], j["pe"], j["fac"], ALU.mult))
        both(lambda j: v.reciprocal(j["th"], j["pe"]))
        both(lambda j: v.tensor_tensor(j["rr"], j["sc"], j["th"], ALU.mult))
        both(lambda j: v.tensor_single_scalar(j["f16"], j["rr"], 1.0 / 16.0, ALU.is_ge))
        both(lambda j: v.tensor_single_scalar(j["cc"], j["rr"], 3.0 / 16.0, ALU.is_ge))
        for jj in range(2, 16):
            tgt = "f16" if jj % 2 == 0 else "cc"
            both(lambda j, tgt=tgt, jj=jj: v.scalar_tensor_tensor(
                j[tgt], j["rr"], float((2 * jj + 1) / 16.0), j[tgt], ALU.is_ge, ALU.add))
        both(lambda j: v.tensor_tensor(j["f16"], j["f16"], j["cc"], ALU.add))
        both(lambda j: v.tensor_single_scalar(j["th"], j["pe"], 0.125, ALU.mult))
        both(lambda j: v.tensor_tensor(j["sc"], j["f16"], j["th"], ALU.mult))
        both(lambda j: v.tensor_scalar(j["am2"], j["sc"], j["invgs"], None, ALU.mult))
        both(lambda j: v.reciprocal(j["th"], j["am2"]))
        both(lambda j: v.tensor_single_scalar(j["rr"], j["am2"], 0.5, ALU.mult))
        ins = None
        for j in jobs:
            ins = v.tensor_single_scalar(j["th"], j["th"], 1e30, ALU.min)
        return ins

    def emit_main_chain(v, jobs, sign_wait):
        def both(f):
            for j in jobs:
                f(j)
        both(lambda j: v.tensor_tensor(j["a0"].rearrange("p (a b) -> p a b", b=16),
                                       j["src"].rearrange("p (a b) -> p a b", b=16),
                                       j["th"].unsqueeze(-1).broadcast_to(
                                           [128, j["nb"], 16]), ALU.mult))
        both(lambda j: v.scalar_tensor_tensor(j["ax"], j["a0"], -1.0, j["a0"], ALU.mult, ALU.max))
        both(lambda j: v.tensor_single_scalar(j["ca"], j["ax"], 0.25, ALU.is_ge))
        both(lambda j: v.tensor_single_scalar(j["cb"], j["ax"], 1.25, ALU.is_ge))
        both(lambda j: v.scalar_tensor_tensor(j["ca"], j["ax"], 0.75, j["ca"], ALU.is_ge, ALU.add))
        both(lambda j: v.scalar_tensor_tensor(j["cb"], j["ax"], 1.75, j["cb"], ALU.is_ge, ALU.add))
        both(lambda j: v.tensor_tensor(j["ca"], j["ca"], j["cb"], ALU.add))
        both(lambda j: v.tensor_single_scalar(j["cb"], j["ax"], 2.5, ALU.is_ge))
        both(lambda j: v.scalar_tensor_tensor(j["cb"], j["ax"], 3.5, j["cb"], ALU.is_ge, ALU.add))
        both(lambda j: v.scalar_tensor_tensor(j["ca"], j["cb"], 2.0, j["ca"], ALU.mult, ALU.add))
        both(lambda j: v.tensor_single_scalar(j["cb"], j["ax"], 5.0, ALU.is_ge))
        both(lambda j: v.scalar_tensor_tensor(j["ca"], j["cb"], 4.0, j["ca"], ALU.mult, ALU.add))
        both(lambda j: v.tensor_tensor(j["ca"].rearrange("p (a b) -> p a b", b=16),
                                       j["ca"].rearrange("p (a b) -> p a b", b=16),
                                       j["rr"].unsqueeze(-1).broadcast_to(
                                           [128, j["nb"], 16]), ALU.mult))
        sign_wait()
        ins = None
        for j in jobs:
            ins = v.tensor_tensor(j["out"], j["ca"], j["sgn"], ALU.mult)
        return ins

    p5rows_s = dict(sc=sqt, pe=xnf, th=stile2, cc=xtile2, fac=gtile2, rr=cmbt, f16=evacf)

    def p5_vact(v, act):
        kind = act[0]
        if kind == "A":
            _, mi, r = act
            v.wait_ge(sems["ldE"], ldE_b[("A", mi, r)])
            if r == 0 and mi > 0:
                v.wait_ge(sems["ldE"], ldE_b[("spl", mi - 1)])
            wf = wf_a if par5[act] == 0 else wf_b
            ins = v.tensor_reduce(amaxc[:, r * 64:(r + 1) * 64],
                                  wf.rearrange("p (a b) -> p a b", b=16),
                                  AX.X, ALU.max, apply_absolute_value=True)
            inc("veE", ins)
        elif kind == "chain":
            mi = act[1]
            v.drain()
            v.tensor_reduce(sqt[:, 0:1], amaxc[:], AX.X, ALU.max)
            v.drain()
            v.transpose(sqt[:, 32:64], sqt[:, 0:32])
            v.drain()
            v.tensor_reduce(sqt[:, 64:65], sqt[:, 32:64], AX.X, ALU.max)
            v.drain()
            v.stream_shuffle(sqt[0:32, 65:66], sqt[0:32, 64:65], [0] * 32)
            v.stream_shuffle(sqt[0:32, 66:67], sqt[32:64, 64:65], [0] * 32)
            v.stream_shuffle(sqt[0:32, 67:68], sqt[64:96, 64:65], [0] * 32)
            v.stream_shuffle(sqt[0:32, 68:69], sqt[96:128, 64:65], [0] * 32)
            v.drain()
            v.tensor_reduce(sqt[:, 69:70], sqt[:, 65:69], AX.X, ALU.max)
            v.drain()
            v.stream_shuffle(sqt[32:64, 69:70], sqt[0:32, 69:70], list(range(32)))
            v.stream_shuffle(sqt[64:96, 69:70], sqt[0:32, 69:70], list(range(32)))
            v.stream_shuffle(sqt[96:128, 69:70], sqt[0:32, 69:70], list(range(32)))
            v.drain()
            v.reciprocal(sqt[:, 70:71], sqt[:, 69:70])
            v.drain()
            v.tensor_single_scalar(gs_sb[:, 6 + 2 * mi:7 + 2 * mi], sqt[:, 70:71],
                                   448.0, ALU.mult)
            ins = v.tensor_single_scalar(gs_sb[:, 7 + 2 * mi:8 + 2 * mi], sqt[:, 69:70],
                                         float(1.0 / 2688.0), ALU.mult)
            inc("veE", ins)
        elif kind == "B":
            _, mi, r = act
            if r % 16 == 0:
                half = r // 16
                if mi < 2:
                    v.wait_ge(sems["ldE"], ldE_b[("rld", mi, half)])
                jobs = []
                for ji in range(2):
                    co = ji * 512
                    jd = {k: ap[:, co:co + 512] for k, ap in p5rows_s.items()}
                    jd["am"] = (dqf[:, co:co + 512] if mi < 2
                                else amaxc[:, half * 1024 + co: half * 1024 + co + 512])
                    jd["am2"] = jd["fac"]
                    jd["gs6"] = gs_sb[:, 6 + 2 * mi:7 + 2 * mi]
                    jd["invgs"] = gs_sb[:, 7 + 2 * mi:8 + 2 * mi]
                    jobs.append(jd)
                emit_dq_chain(v, jobs)
            v.wait_ge(sems["ldE"], ldE_b[("B", mi, r)])
            p2 = bprev2[("B", mi, r)]
            if p2 is not None:
                v.wait_ge(sems["mmE"], mmE_b[("tr", p2[1], p2[2])])
            wf = wf_a if par5[act] == 0 else wf_b
            off = (r % 16) * 64
            mjobs = []
            for ji in range(2):
                sl = slice(ji * 512, (ji + 1) * 512)
                mjobs.append(dict(
                    src=wf[:, sl], out=xq_par[r % 2][:, sl], sgn=sgn_buf[:, sl],
                    th=p5rows_s["th"][:, off + ji * 32: off + (ji + 1) * 32],
                    rr=p5rows_s["rr"][:, off + ji * 32: off + (ji + 1) * 32], nb=32,
                    a0=qs[0][:, sl], ax=qs[1][:, sl], ca=qs[2][:, sl], cb=qs[3][:, sl],
                ))

            def _sw(mi=mi, r=r):
                v.wait_ge(sems["seE"], seE_b[("sgn", mi, r)])
            ins = emit_main_chain(v, mjobs, _sw)
            inc("veE", ins)
        else:
            _, mi, r = act
            v.wait_ge(sems["mmE"], mmE_b[("tr", mi, r)])
            p2 = bprev2[("B", mi, r)]
            if p2 is not None:
                v.wait_ge(sems["stE"], stE_b[("st", p2[1], p2[2])])
            ins = v.tensor_copy(stage8[:, r % 2, :], psb[:])
            inc("veE", ins)

    def emit_quant(v, src, out_bf, W, gs6_col, invgs_col, sign_wait, amax_ap=None,
                   sgn_ap=None):
        if sgn_ap is None:
            sgn_ap = sgn_buf[:, 0:W]
        Wh = W // 2
        nb = Wh // 16
        jobs = []
        for ji in range(2):
            co = ji * 64
            sl = slice(ji * Wh, (ji + 1) * Wh)
            qsl = slice(ji * Wh, (ji + 1) * Wh) if W <= 1024 else None
            jobs.append(dict(
                src=src[:, sl], out=out_bf[:, sl], sgn=sgn_ap[:, sl],
                am=(dqsc[:, 0, co:co + nb] if amax_ap is None else amax_ap[:, ji * nb:(ji + 1) * nb]),
                amr=dqsc[:, 0, co:co + nb],
                sc=dqsc[:, 1, co:co + nb], pe=dqsc[:, 2, co:co + nb],
                th=dqsc[:, 3, co:co + nb], cc=dqsc[:, 4, co:co + nb],
                fac=dqsc[:, 5, co:co + nb], rr=dqsc[:, 6, co:co + nb],
                f16=dqsc[:, 7, co:co + nb],
                a0=qs[0][:, sl], ax=qs[1][:, sl], ca=qs[2][:, sl], cb=qs[3][:, sl],
            ))

        def both(f):
            for j in jobs:
                f(j)

        if amax_ap is None:
            both(lambda j: v.tensor_reduce(j["am"], j["src"].rearrange("p (a b) -> p a b", b=16),
                                           AX.X, ALU.max, apply_absolute_value=True))
        both(lambda j: v.tensor_scalar(j["sc"], j["am"], gs6_col, 448.0, ALU.mult, ALU.min))
        both(lambda j: v.memset(j["pe"], 2.0 ** -6))
        for mlev in (8, 4, 2, 1):
            both(lambda j: v.tensor_single_scalar(j["th"], j["pe"], float(2.0 ** mlev), ALU.mult))
            both(lambda j: v.tensor_tensor(j["cc"], j["sc"], j["th"], ALU.is_ge))
            both(lambda j: v.tensor_scalar(j["fac"], j["cc"], float(2.0 ** mlev - 1), 1.0,
                                           ALU.mult, ALU.add))
            both(lambda j: v.tensor_tensor(j["pe"], j["pe"], j["fac"], ALU.mult))
        both(lambda j: v.reciprocal(j["th"], j["pe"]))
        both(lambda j: v.tensor_tensor(j["rr"], j["sc"], j["th"], ALU.mult))
        both(lambda j: v.tensor_single_scalar(j["f16"], j["rr"], 1.0 / 16.0, ALU.is_ge))
        both(lambda j: v.tensor_single_scalar(j["cc"], j["rr"], 3.0 / 16.0, ALU.is_ge))
        for jj in range(2, 16):
            tgt = "f16" if jj % 2 == 0 else "cc"
            both(lambda j, tgt=tgt, jj=jj: v.scalar_tensor_tensor(
                j[tgt], j["rr"], float((2 * jj + 1) / 16.0), j[tgt], ALU.is_ge, ALU.add))
        both(lambda j: v.tensor_tensor(j["f16"], j["f16"], j["cc"], ALU.add))
        both(lambda j: v.tensor_single_scalar(j["th"], j["pe"], 0.125, ALU.mult))
        both(lambda j: v.tensor_tensor(j["sc"], j["f16"], j["th"], ALU.mult))
        both(lambda j: v.tensor_scalar(j["amr"], j["sc"], invgs_col, None, ALU.mult))
        both(lambda j: v.reciprocal(j["th"], j["amr"]))
        both(lambda j: v.tensor_single_scalar(j["rr"], j["amr"], 0.5, ALU.mult))
        both(lambda j: v.tensor_single_scalar(j["th"], j["th"], 1e30, ALU.min))
        both(lambda j: v.tensor_tensor(j["a0"].rearrange("p (a b) -> p a b", b=16),
                                       j["src"].rearrange("p (a b) -> p a b", b=16),
                                       j["th"].unsqueeze(-1).broadcast_to([128, nb, 16]),
                                       ALU.mult))
        both(lambda j: v.scalar_tensor_tensor(j["ax"], j["a0"], -1.0, j["a0"], ALU.mult, ALU.max))
        both(lambda j: v.tensor_single_scalar(j["ca"], j["ax"], 0.25, ALU.is_ge))
        both(lambda j: v.tensor_single_scalar(j["cb"], j["ax"], 1.25, ALU.is_ge))
        both(lambda j: v.scalar_tensor_tensor(j["ca"], j["ax"], 0.75, j["ca"], ALU.is_ge, ALU.add))
        both(lambda j: v.scalar_tensor_tensor(j["cb"], j["ax"], 1.75, j["cb"], ALU.is_ge, ALU.add))
        both(lambda j: v.tensor_tensor(j["ca"], j["ca"], j["cb"], ALU.add))
        both(lambda j: v.tensor_single_scalar(j["cb"], j["ax"], 2.5, ALU.is_ge))
        both(lambda j: v.scalar_tensor_tensor(j["cb"], j["ax"], 3.5, j["cb"], ALU.is_ge, ALU.add))
        both(lambda j: v.scalar_tensor_tensor(j["ca"], j["cb"], 2.0, j["ca"], ALU.mult, ALU.add))
        both(lambda j: v.tensor_single_scalar(j["cb"], j["ax"], 5.0, ALU.is_ge))
        both(lambda j: v.scalar_tensor_tensor(j["ca"], j["cb"], 4.0, j["ca"], ALU.mult, ALU.add))
        both(lambda j: v.tensor_tensor(j["ca"].rearrange("p (a b) -> p a b", b=16),
                                       j["ca"].rearrange("p (a b) -> p a b", b=16),
                                       j["rr"].unsqueeze(-1).broadcast_to([128, nb, 16]),
                                       ALU.mult))
        sign_wait()
        v.tensor_tensor(jobs[0]["out"], jobs[0]["ca"], jobs[0]["sgn"], ALU.mult)
        ins = v.tensor_tensor(jobs[1]["out"], jobs[1]["ca"], jobs[1]["sgn"], ALU.mult)
        return ins

    @blk.vector
    def _(v):
        v.wait_ge(sems["ld"], ck["ld_const"])
        for mi in range(4):
            for r in range(8):
                v.wait_ge(sems["mmA"], mi * 8 + r + 1)
                ins = v.tensor_copy(wT[("wq", "wk", "wv", "wo")[mi]][:, :, r * 128:(r + 1) * 128],
                                    psb[:].rearrange("p (a b) -> p a b", a=8))
                inc("veA", ins)
        v.tensor_copy(biasb_sb[:, 0:D], bias_sb[:, 0:D])
        v.tensor_copy(biasb_sb[:, D:2 * D], bias_sb[:, D:2 * D])
        v.reciprocal(gs_sb[:, 3:4], gs_sb[:, 0:1])
        v.reciprocal(gs_sb[:, 5:6], gs_sb[:, 1:2])
        ins = v.tensor_copy(bvcol_sb[:], bvcolf_sb[:])
        inc("veA", ins)
        v.wait_ge(sems["mmA"], MM_P0)
        ins = v.tensor_tensor(beff_sb[:], psf[0:1, 2048:3072], bias_sb[:, 3 * D:4 * D], ALU.add)
        inc("veA", ins)
        v.tensor_tensor(tab_ks[:].rearrange("p a b -> p (a b)"),
                        th_sb[:, 0:1024], th_sb[:, 1024:2048], ALU.add)
        v.tensor_tensor(tab_kc[:].rearrange("p a b -> p (a b)"),
                        th_sb[:, 2048:3072], th_sb[:, 3072:4096], ALU.add)
        v.tensor_tensor(tab_qs[:].rearrange("p a b -> p (a b)"),
                        th_sb[:, 4096:4608], th_sb[:, 4608:5120], ALU.add)
        ins = v.tensor_tensor(tab_qc[:].rearrange("p a b -> p (a b)"),
                              th_sb[:, 5120:5632], th_sb[:, 5632:6144], ALU.add)
        inc("veA", ins)
        assert cnt["veA"] == VE_P0

        for i in range(NKV):
            lnx = lnx_a if i % 2 == 0 else lnx_b
            xt = xtile_a if i % 2 == 0 else xtile_b
            v.wait_ge(sems["ld"], ck[f"p1_ld_{i}"])
            v.wait_ge(sems["seA"], 1)
            emit_ln(v, SE_P1(i, 1), SE_P1(i, 2), lnx, xt[:], want_f32=False)
            v.wait_ge(sems["mmA"], MM_P1(i, 1))
            v.wait_ge(sems["stB"], 32 * i)
            ins = v.tensor_copy(stage8[:, 0, :], psb[:])
            inc("veA", ins)
            v.wait_ge(sems["mmA"], MM_P1(i, 2))
            ins = emit_rope(v, 0, tab_ks, tab_kc, i, xt[:])
            inc("veA", ins)
            v.wait_ge(sems["mmA"], MM_P1(i, 3))
            v.tensor_copy(stage8[:, 1, :], psf[:, 1024:2048])
            v.wait_ge(sems["mmA"], MM_P1(i, 4))
            ins = v.tensor_copy(stage8[:, 0, :], psb[:])
            inc("veA", ins)
        assert cnt["veA"] == VE_P1_END

        for i in range(NQT):
            lnx = lnx_a if i % 2 == 0 else lnx_b
            xt = xtile_a if i % 2 == 0 else xtile_b
            v.wait_ge(sems["ld"], ck[f"p2_ld_{i}"])
            emit_ln(v, SE_P2(i, 1), SE_P2(i, 2), lnx, xt[:], want_f32=True)
            v.wait_ge(sems["mmA"], MM_P2(i, 1))
            v.wait_ge(sems["stB"], ST_P1_END + 16 * i)
            ins = v.tensor_copy(stage8[:, 0, :], psb[:])
            inc("veA", ins)
            v.wait_ge(sems["mmA"], MM_P2(i, 2))
            ins = emit_rope(v, 0, tab_qs, tab_qc, i, xt[:])
            inc("veA", ins)
            v.wait_ge(sems["mmA"], MM_P2(i, 3))
            ins = v.tensor_copy(qT[:, :, i * 128:(i + 1) * 128], psb[:].rearrange("p (a b) -> p a b", a=8))
            inc("veA", ins)
            def _sw(i=i):
                v.wait_ge(sems["seA"], SE_P2(i, 3))
            ins = emit_quant(v, xnf, xq_stage, 1024, gs_sb[:, 2:3], gs_sb[:, 3:4], _sw,
                             sgn_ap=sgn_buf)
            inc("veA", ins)
            v.wait_ge(sems["mmA"], MM_P2(i, 4))
            ins = v.tensor_copy(stage8[:, 0, :], psb[:])
            inc("veA", ins)
        assert cnt["veA"] == VE_P2_END

        for qst in range(QSTN):
            cntk = PC[qst]
            pump_tail = None
            for hg in range(8):
                b = qst * 8 + hg
                part1, part2 = vsplit[b]
                state = {"p1": list(part1), "p2": list(part2), "k": 0, "npts": 10}
                while state["p1"] and state["p1"][0][0] == "ev":
                    p5_vact(v, state["p1"].pop(0))

                def pump(state=state):
                    state["k"] += 1
                    k = state["k"]
                    if k <= 7:
                        src, rem = state["p1"], 7 - k
                    else:
                        if state["p1"]:
                            state["p2"] = state["p1"] + state["p2"]
                            state["p1"] = []
                        src, rem = state["p2"], state["npts"] - k
                    n = len(src)
                    take = n if rem <= 0 else -(-n // (rem + 1))
                    for _ in range(take):
                        p5_vact(v, src.pop(0))
                if hg == 7:
                    pump_tail = pump
                nmask_start = cntk - 8
                for j in range(8):
                    kt = nmask_start + j
                    buf = kt % 2
                    v.wait_ge(sems["seD"], seD_b[("expb", qst, hg)] + kt + 1)
                    ex = expTb[:, buf, :]
                    v.tensor_tensor(ex[:, 0:512], ex[:, 0:512], maskbuf[:, j, :], ALU.mult)
                    ins = v.tensor_tensor(ex[:, 512:1024], ex[:, 512:1024], maskbuf[:, j, :], ALU.mult)
                    inc("veD", ins)
                    pump()
                v.wait_ge(sems["mmD"], mmD_b[("av", qst, hg)])
                v.reciprocal(rl_sb[0:1, 0:512], psf[0:1, 2560:3072])
                ins = v.reciprocal(rl_sb[0:1, 512:1024], psf[0:1, 3072:3584])
                inc("veD", ins)
                pump()
                v.wait_ge(sems["mmD"], mmD_b[("rl", qst, hg)])
                v.tensor_copy(evacf[:, 0:512], psf[:, 1024:1536])
                v.drain()
                ins = v.tensor_tensor(accT[:, hg, :], psf[:, 2048:2560],
                                      evacf[:, 0:512], ALU.mult)
                inc("veD", ins)
                pump()
            for tq in range(4):
                v.wait_ge(sems["mmD"], mmD_b[("wo", qst)] - 4 + tq + 1)
                v.wait_ge(sems["stD"], 16 * (4 * qst + tq))
                ins = v.tensor_copy(stile[:], psf[:, 0:1024])
                inc("veD", ins)
                pump_tail()
        assert cnt["veD"] == VE_P4_END
        for act in vprog[NBLK]:
            p5_vact(v, act)
        assert cnt["veE"] == VEE_END


        for qd in range(4):
            for oc in range(8):
                pend_amax = []
                for t4 in range(4):
                    v.wait_ge(sems["mmF"], mmF_b[("u", qd, oc, t4)])
                    v.wait_ge(sems["seF"], seF_b[("sil", qd, oc, t4)])
                    ins = v.tensor_tensor(hidf4[:, t4, :], hsg_par[t4 % 2][:],
                                          psf[:, 1024 * (t4 % 2) + 512:1024 * (t4 % 2) + 1024],
                                          ALU.mult)
                    inc("veF", ins)
                    pend_amax.append(t4)
                    if len(pend_amax) >= 2:
                        ta = pend_amax.pop(0)
                        v.tensor_reduce(dqsc[:, 0, ta * 32:(ta + 1) * 32],
                                        hidf4[:, ta, :].rearrange("p (a b) -> p a b", b=16),
                                        AX.X, ALU.max, apply_absolute_value=True)
                for ta in pend_amax:
                    v.tensor_reduce(dqsc[:, 0, ta * 32:(ta + 1) * 32],
                                    hidf4[:, ta, :].rearrange("p (a b) -> p a b", b=16),
                                    AX.X, ALU.max, apply_absolute_value=True)
                jobs = []
                for ji in range(2):
                    co = ji * 64
                    jobs.append(dict(
                        am=dqsc[:, 0, co:co + 64], am2=dqsc[:, 0, co:co + 64],
                        sc=dqsc[:, 1, co:co + 64], pe=dqsc[:, 2, co:co + 64],
                        th=dqsc[:, 3, co:co + 64], cc=dqsc[:, 4, co:co + 64],
                        fac=dqsc[:, 5, co:co + 64], rr=dqsc[:, 6, co:co + 64],
                        f16=dqsc[:, 7, co:co + 64],
                        gs6=gs_sb[:, 4:5], invgs=gs_sb[:, 5:6],
                    ))
                ins = emit_dq_chain(v, jobs)
                inc("veF", ins)
                sgn4 = [sgn_buf[:, 0:512], sgn_buf[:, 512:1024], xnf[:, 0:512], xnf[:, 512:1024]]
                for t4 in range(4):
                    if (qd, oc, t4) != (0, 0, 0):
                        pk = (qd, oc, t4 - 1) if t4 > 0 else ((qd, oc - 1, 3) if oc > 0
                                                              else (qd - 1, 7, 3))
                        v.wait_ge(sems["mmF"], mmF_b[("tr", pk[0], pk[1], pk[2])])
                    mjobs = []
                    for ji in range(2):
                        co = t4 * 32 + ji * 16
                        sl = slice(ji * 256, (ji + 1) * 256)
                        mjobs.append(dict(
                            src=hidf4[:, t4, sl], out=hqstage[:, sl], sgn=sgn4[t4][:, sl],
                            th=dqsc[:, 3, co:co + 16], rr=dqsc[:, 6, co:co + 16], nb=16,
                            a0=qs[0][:, sl], ax=qs[1][:, sl], ca=qs[2][:, sl], cb=qs[3][:, sl],
                        ))

                    def _sw2(qd=qd, oc=oc, t4=t4):
                        v.wait_ge(sems["seF"], seF_b[("sgn", qd, oc, t4)])
                    ins = emit_main_chain(v, mjobs, _sw2)
                    inc("veF", ins)
                    v.wait_ge(sems["mmF"], mmF_b[("tr", qd, oc, t4)])
                    ins = v.tensor_copy(hqT_t[t4][:, oc * 4:(oc + 1) * 4, :],
                                        psb[:, 0:512].rearrange("p (a b) -> p a b", a=4))
                    inc("veF", ins)
            for pr in range(2):
                for tp in range(2):
                    v.wait_ge(sems["seF"], seF_b[("g", qd, pr, tp)])
                    v.wait_ge(sems["ldF"], ldF_b[("sx", qd, pr, tp)])
                    v.tensor_tensor(cmbt[:, 0:512], stile2[:, 0:512], gtile2[:, 0:512], ALU.mult)
                    v.tensor_tensor(cmbt[:, 512:1024], stile2[:, 512:1024], gtile2[:, 512:1024], ALU.mult)
                    v.tensor_tensor(stile2[:, 0:512], cmbt[:, 0:512], xtile2[:, 0:512], ALU.add)
                    ins = v.tensor_tensor(stile2[:, 512:1024], cmbt[:, 512:1024], xtile2[:, 512:1024], ALU.add)
                    inc("veF", ins)
        assert cnt["veF"] == VEF_END

    @blk.scalar
    def _(sc_):
        sc_.wait_ge(sems["ld"], ck["ld_const"])
        sc_.mul(gs_sb[:, 2:3], gs_sb[:, 0:1], 1.0 / 6.0)
        sc_.mul(gs_sb[:, 4:5], gs_sb[:, 1:2], 1.0 / 6.0)
        sc_.wait_ge(sems["veA"], VE_P0)
        sc_.activation(tab_ks[:].rearrange("p a b -> p (a b)"),
                       tab_ks[:].rearrange("p a b -> p (a b)"), AF.Sin)
        sc_.activation(tab_kc[:].rearrange("p a b -> p (a b)"),
                       tab_kc[:].rearrange("p a b -> p (a b)"), AF.Sin)
        sc_.activation(tab_qs[:].rearrange("p a b -> p (a b)"),
                       tab_qs[:].rearrange("p a b -> p (a b)"), AF.Sin)
        sc_.activation(tab_qc[:].rearrange("p a b -> p (a b)"),
                       tab_qc[:].rearrange("p a b -> p (a b)"), AF.Sin)
        sc_.mul(tab_qs[:].rearrange("p a b -> p (a b)"),
                tab_qs[:].rearrange("p a b -> p (a b)"), 0.125)
        ins = sc_.mul(tab_qc[:].rearrange("p a b -> p (a b)"),
                      tab_qc[:].rearrange("p a b -> p (a b)"), 0.125)
        inc("seA", ins)
        for i in range(NKV):
            sc_.wait_ge(sems["ld"], ck[f"p1_ld_{i}"])
            if i >= 1:
                sc_.wait_ge(sems["veA"], VE_P1(i - 1, 2))
            lnx = lnx_a if i % 2 == 0 else lnx_b
            ins = sc_.activation(sqt[:], lnx, AF.Square)
            inc("seA", ins)
            sc_.wait_ge(sems["veA"], VE_P1(i, 1))
            ins = sc_.activation(scol(8), scol(7), AF.Sqrt)
            inc("seA", ins)
            sc_.wait_ge(sems["veA"], VE_P1(i, 5))
            inc("stB", sc_.dma_start(out=kT_sp[:, :, i * 128:(i + 1) * 128],
                                     in_=stage8[:, 0, :].rearrange("p (a b) -> p a b", a=8)), 16)
            inc("stB", sc_.dma_start(out=v_sp[:, i, :], in_=stage8[:, 1, :]), 16)
        for i in range(NQT):
            sc_.wait_ge(sems["ld"], ck[f"p2_ld_{i}"])
            sc_.wait_ge(sems["veA"], VE_P1(NKV - 1, 2) if i == 0 else VE_P2(i - 1, 2))
            lnx = lnx_a if i % 2 == 0 else lnx_b
            ins = sc_.activation(sqt[:], lnx, AF.Square)
            inc("seA", ins)
            sc_.wait_ge(sems["veA"], VE_P2(i, 1))
            ins = sc_.activation(scol(8), scol(7), AF.Sqrt)
            inc("seA", ins)
            sc_.wait_ge(sems["veA"], VE_P2(i, 2))
            ins = sc_.activation(sgn_buf[:], xnf[:], AF.Sign)
            inc("seA", ins)
            sc_.wait_ge(sems["veA"], VE_P2(i, 7))
            inc("stB", sc_.dma_start(out=xqT_sp[:, :, i * 128:(i + 1) * 128],
                                     in_=stage8[:, 0, :].rearrange("p (a b) -> p a b", a=8)), 16)

        for qst in range(QSTN):
            cntk = PC[qst]
            for hg in range(8):
                b = qst * 8 + hg
                nmask_start = cntk - 8
                bqh = mmD_b[("base", qst, hg)]
                for kt in range(cntk):
                    buf = kt % 2
                    sc_.wait_ge(sems["mmD"], mm_sc_val(bqh, kt))
                    if kt >= 2:
                        sc_.wait_ge(sems["mmD"], mm_av_val(bqh, kt - 2, cntk))
                        if kt - 2 >= nmask_start:
                            sc_.wait_ge(sems["veD"],
                                        veD_b[("mskb", qst, hg)] + (kt - 2 - nmask_start) + 1)
                    ins = sc_.activation(expTb[:, buf, :], psf[:, buf * 1024:(buf + 1) * 1024], AF.Exp)
                    inc("seD", ins)
                for act in vprog[b]:
                    if act[0] == "B":
                        p5_sgn(sc_, act[1], act[2])
                    elif act[0] == "ev":
                        p5_st(sc_, act[1], act[2])
            for tq in range(4):
                sc_.wait_ge(sems["veD"], veD_b[("sev", qst)] - 4 + tq + 1)
                row = qst * 512 + tq * 128
                inc("stD", sc_.dma_start(out=s_sp[row:row + 128, :], in_=stile[:]), 16)

        for act in vprog[NBLK]:
            if act[0] == "B":
                p5_sgn(sc_, act[1], act[2])
            elif act[0] == "ev":
                p5_st(sc_, act[1], act[2])
        assert cnt["seE"] == SEE_END and cnt["stE"] == STE_END

        for qd in range(4):
            for oc in range(8):
                for t4 in range(4):
                    sc_.wait_ge(sems["mmF"], mmF_b[("g", qd, oc, t4)])
                    if (qd, oc, t4) != (0, 0, 0):
                        pk = (qd, oc, t4 - 2) if t4 >= 2 else ((qd, oc - 1, t4 + 2) if oc > 0
                                                               else (qd - 1, 7, t4 + 2))
                        if pk[2] <= 3 and pk[0] >= 0:
                            sc_.wait_ge(sems["veF"], veF_b[("hid", pk[0], pk[1], pk[2])])
                    ins = sc_.activation(hsg_par[t4 % 2][:],
                                         psf[:, 1024 * (t4 % 2):1024 * (t4 % 2) + 512], AF.Silu)
                    inc("seF", ins)
                    sc_.wait_ge(sems["veF"], veF_b[("hid", qd, oc, t4)])
                    ins = sc_.activation(sgn4s[t4], hidf4[:, t4, :], AF.Sign)
                    inc("seF", ins)
            for pr in range(2):
                for tp in range(2):
                    t4 = 2 * pr + tp
                    j = 4 * qd + t4
                    sc_.wait_ge(sems["mmF"], mmF_b[("dn", qd, pr, 7, tp)])
                    ins = sc_.activation(gtile2[:], psf[:, 1024 + 1024 * tp:2048 + 1024 * tp], AF.Silu)
                    inc("seF", ins)
                    sc_.wait_ge(sems["veF"], veF_b[("cmb", qd, pr, tp)])
                    ins = sc_.dma_start(out=out_d[j * 128:(j + 1) * 128, :], in_=stile2[:])
                    inc("outS", ins, 16)

        es.close()
    return nc




_PROGRAM_CACHE = {}


def kernel(**inputs):
    in_maps = host_prep(inputs)
    if "nc" not in _PROGRAM_CACHE:
        _PROGRAM_CACHE["nc"] = build_program()
    nc = _PROGRAM_CACHE["nc"]
    res = run_bass_kernel_spmd(nc, in_maps, core_ids=list(range(8)))
    out = np.empty((B, S, D), dtype=np.float32)
    for c in range(8):
        b, sub = c // 2, c % 2
        q0s = Q0S[sub]
        oc = np.asarray(res.results[c]["out"])
        for sl, q0 in enumerate(q0s):
            out[b, q0:q0 + 512, :] = oc[sl * 512:(sl + 1) * 512, :]
    return out


# version 14
# speedup vs baseline: 1.0264x; 1.0264x over previous
import math
import os
import numpy as np
import ml_dtypes
from contextlib import ExitStack
import concourse.bass as bass
import concourse.mybir as mybir
from concourse.bass_utils import run_bass_kernel_spmd

F32 = mybir.dt.float32
BF16 = mybir.dt.bfloat16
ALU = mybir.AluOpType
AF = mybir.ActivationFunctionType
AX = mybir.AxisListType

B, S, D, H, HD = 4, 4096, 1024, 16, 64
NKV, NQT, NQ, QSTN = 32, 16, 2048, 4
PC = [32, 24, 16, 8]
Q0S = {0: [3584, 2048, 1536, 0], 1: [3072, 2560, 1024, 512]}
EPS = 1e-5
INV1024 = float(np.float32(1.0 / 1024.0))
DEBUG = 0

VE_P0 = 35
VE_P1 = lambda i, st: VE_P0 + 5 * i + st
VE_P1_END = VE_P0 + 5 * NKV
VE_P2 = lambda i, st: VE_P1_END + 7 * i + st
VE_P2_END = VE_P1_END + 7 * NQT
SE_P1 = lambda i, st: 1 + 2 * i + st
SE_P1_END = 1 + 2 * NKV
SE_P2 = lambda i, st: SE_P1_END + 3 * i + st
MM_P0 = 34
MM_P1 = lambda i, st: MM_P0 + 4 * i + st
MM_P1_END = MM_P0 + 4 * NKV
MM_P2 = lambda i, st: MM_P1_END + 4 * i + st
MM_P2_END = MM_P1_END + 4 * NQT
ST_P1_END = 32 * NKV


def _theta_red_hilo(pos, shift):
    i = np.arange(HD // 2, dtype=np.float64)[None, :]
    inv = np.exp(-2.0 * i / HD * math.log(10000.0))
    th = pos[:, None] * inv + shift
    k = np.floor((th + math.pi) / (2 * math.pi))
    thr = th - 2 * math.pi * k
    hi = thr.astype(np.float32)
    lo = (thr - hi.astype(np.float64)).astype(np.float32)
    return hi, lo


def host_prep(inputs):
    x = np.ascontiguousarray(inputs["L0_attn_x"], dtype=np.float32)
    ident = np.eye(128, dtype=ml_dtypes.bfloat16)
    ident8 = (np.eye(128, dtype=np.float32) * 0.125).astype(ml_dtypes.bfloat16)
    pos_kv = np.arange(S, dtype=np.float64)
    tkh_s, tkl_s = _theta_red_hilo(pos_kv, 0.0)
    tkh_c, tkl_c = _theta_red_hilo(pos_kv, math.pi / 2)
    indA = np.zeros((1, 128), dtype=np.float32); indA[0, :64] = 1.0
    indB = np.zeros((1, 128), dtype=np.float32); indB[0, 64:] = 1.0
    shared = {
        "ident": ident, "ident8": ident8,
        "ones_row": np.ones((1, 128), dtype=ml_dtypes.bfloat16),
        "ones_col": np.ones((128, 1), dtype=ml_dtypes.bfloat16),
        "indA": indA, "indB": indB,
        "ident_f": np.eye(128, dtype=np.float32),
        "ones_row_f": np.ones((1, 128), dtype=np.float32),
        "wq": inputs["L0_attn_wq"], "wk": inputs["L0_attn_wk"],
        "wv": inputs["L0_attn_wv"], "wo": inputs["L0_attn_wo"],
        "bq": np.ascontiguousarray(inputs["L0_attn_bq"].reshape(1, D)),
        "bk": np.ascontiguousarray(inputs["L0_attn_bk"].reshape(1, D)),
        "bv": np.ascontiguousarray(inputs["L0_attn_bv"].reshape(1, D)),
        "bo": np.ascontiguousarray(inputs["L0_attn_bo"].reshape(1, D)),
        "gw": inputs["L0_ffn_gate_w"], "uw": inputs["L0_ffn_up_w"],
        "dw": inputs["L0_ffn_down_w"],
        "s_in": np.ascontiguousarray(inputs["L0_ffn_s_in"].reshape(1)),
        "s_dn": np.ascontiguousarray(inputs["L0_ffn_s_in_down"].reshape(1)),
        "tk_sin_hi": tkh_s, "tk_sin_lo": tkl_s, "tk_cos_hi": tkh_c, "tk_cos_lo": tkl_c,
    }
    in_maps = []
    for c in range(8):
        b, sub = c // 2, c % 2
        q0s = Q0S[sub]
        qrows = np.concatenate([np.arange(q0, q0 + 512) for q0 in q0s])
        pos_q = qrows.astype(np.float64)
        tqh_s, tql_s = _theta_red_hilo(pos_q, 0.0)
        tqh_c, tql_c = _theta_red_hilo(pos_q, math.pi / 2)
        masks = np.zeros((QSTN * 8, 128, 512), dtype=np.float32)
        for sl in range(QSTN):
            q0 = q0s[sl]
            for j in range(8):
                kt = PC[sl] - 8 + j
                kpos = 128 * kt + np.arange(128)[:, None]
                qpos = q0 + np.arange(512)[None, :]
                masks[sl * 8 + j] = (kpos <= qpos)
        m = dict(shared)
        m.update({
            "x_q": np.ascontiguousarray(x[b][qrows]), "x_kv": np.ascontiguousarray(x[b]),
            "tq_sin_hi": tqh_s, "tq_sin_lo": tql_s, "tq_cos_hi": tqh_c, "tq_cos_lo": tql_c,
            "masks": masks.astype(ml_dtypes.bfloat16),
        })
        in_maps.append(m)
    return in_maps


def build_program():
    nc = bass.Bass("TRN2", target_bir_lowering=False, detect_race_conditions=False)
    es = ExitStack()
    d = {}

    def din(name, shape, dt=F32):
        d[name] = nc.dram_tensor(name, shape, dt, kind="ExternalInput").ap()

    din("x_q", [NQ, D]); din("x_kv", [S, D])
    for n in ("wq", "wk", "wv", "wo"): din(n, [D, D])
    for n in ("bq", "bk", "bv", "bo"): din(n, [1, D])
    din("gw", [4096, D]); din("uw", [4096, D]); din("dw", [D, 4096])
    din("s_in", [1]); din("s_dn", [1])
    din("ident", [128, 128], BF16); din("ident8", [128, 128], BF16)
    din("ones_row", [1, 128], BF16); din("ones_col", [128, 1], BF16)
    din("indA", [1, 128]); din("indB", [1, 128])
    din("ident_f", [128, 128]); din("ones_row_f", [1, 128])
    for n in ("tq_sin_hi", "tq_sin_lo", "tq_cos_hi", "tq_cos_lo"): din(n, [NQ, HD // 2])
    for n in ("tk_sin_hi", "tk_sin_lo", "tk_cos_hi", "tk_cos_lo"): din(n, [S, HD // 2])
    din("masks", [QSTN * 8, 128, 512], BF16)

    out_d = nc.dram_tensor("out", [NQ, D], F32, kind="ExternalOutput").ap()
    spill_kind = "ExternalOutput" if DEBUG else "Internal"
    kT_sp = nc.dram_tensor("kT_sp", [128, 8, S], BF16, kind=spill_kind).ap()
    v_sp = nc.dram_tensor("v_sp", [128, NKV, D], BF16, kind=spill_kind).ap()
    xqT_sp = nc.dram_tensor("xqT_sp", [128, 8, NQ], BF16, kind=spill_kind).ap()
    s_sp = nc.dram_tensor("s_sp", [NQ, D], F32, kind=spill_kind).ap()
    wgT_sp = nc.dram_tensor("wgT_sp", [128, 8, 4096], BF16, kind=spill_kind).ap()
    wuT_sp = nc.dram_tensor("wuT_sp", [128, 8, 4096], BF16, kind=spill_kind).ap()
    wdT_sp = nc.dram_tensor("wdT_sp", [128, 32, D], BF16, kind=spill_kind).ap()
    qT_sp = nc.dram_tensor("qT_sp", [128, 8, NQ], BF16, kind=spill_kind).ap()
    amax_sp = nc.dram_tensor("amax_sp", [128, 4096], F32, kind=spill_kind).ap()

    SB_BF = 62464
    bfp = es.enter_context(nc.sbuf_tensor("bfp", [128, SB_BF], BF16))

    def bfv(off, shape):
        n = int(np.prod(shape))
        apv = bfp[:, off:off + n]
        if len(shape) > 1:
            letters = "abcdefg"[: len(shape)]
            apv = apv.rearrange(f"p ({' '.join(letters)}) -> p {' '.join(letters)}",
                                **{letters[i]: shape[i] for i in range(len(shape))})
        return apv

    qT = bfv(0, [8, NQ])
    wT = {m: bfv(16384 + i * 8192, [8, 1024])
          for i, m in enumerate(("wq", "wk", "wv", "wo"))}
    xtile_a = bfv(49152, [1024])
    xtile_b = bfv(50176, [1024])
    xq_stage = bfv(51200, [1024])
    xq_par = [bfv(51200, [1024]), bfv(34816, [1024])]
    stage8 = bfv(52224, [2, 1024])
    wbig_bf = bfv(54272, [8, 1024])
    xqT = bfv(0, [8, NQ])
    wg_par = [bfv(16384, [8, 512]), bfv(41472, [8, 512])]
    wu_par = [bfv(20480, [8, 512]), bfv(45568, [8, 512])]
    hqT_t = [bfv(24576 + 4096 * i, [32, 128]) for i in range(4)]
    hqstage = bfv(40960, [512])
    wd_par = [bfv(54272, [4, 1024]), bfv(58368, [4, 1024])]
    kt_par = [bfv(16384, [4096]), bfv(54272, [4096])]
    v_par = [bfv(20480, [4096]), bfv(58368, [4096])]
    maskbuf = bfv(24576, [8, 512])
    accT = bfv(28672, [8, 512])
    expTb = bfv(32768, [2, 1024])

    SB_F32 = 18432
    fpool = es.enter_context(nc.sbuf_tensor("fpool", [128, SB_F32], F32))

    def fv(off, shape):
        n = int(np.prod(shape))
        apv = fpool[:, off:off + n]
        if len(shape) > 1:
            letters = "abcdefg"[: len(shape)]
            apv = apv.rearrange(f"p ({' '.join(letters)}) -> p {' '.join(letters)}",
                                **{letters[i]: shape[i] for i in range(len(shape))})
        return apv

    qs = [fv(i * 1024, [1024]) for i in range(4)]
    lnx_a = fv(4096, [1024]); lnx_b = fv(5120, [1024])
    sqt = fv(6144, [1024])
    xnf = fv(7168, [1024])
    ropet = fv(8192, [6, 512])
    tab_ks = fv(11264, [32, 32]); tab_kc = fv(12288, [32, 32])
    tab_qs = fv(13312, [16, 32]); tab_qc = fv(13824, [16, 32])
    dqsc = fv(14336, [8, 128])
    dqf = fv(14336, [1024])
    stile = fv(17408, [1024])
    amaxc = fv(8192, [2048])
    wf_a = fv(4096, [1024]); wf_b = fv(5120, [1024])
    wf4_a = fv(4096, [1024]); wf4_b = fv(5120, [1024])
    stile2 = fv(10240, [1024])
    xtile2 = fv(11264, [1024])
    gtile2 = fv(12288, [1024])
    hidf4 = fv(4096, [4, 512])
    hsg_par = [fv(6144, [512]), fv(6656, [512])]
    cmbt = fv(13312, [1024])
    sgn_buf = fv(15360, [1024])
    evacf = fv(16384, [1024])

    th_sb = fpool[:, 0:6144]
    bias_sb = fpool[0:1, 6144:6144 + 4 * D]
    scl = es.enter_context(nc.sbuf_tensor("scl", [128, 16], F32))

    def scol(i):
        return scl[:, i:i + 1]

    psf = es.enter_context(nc.psum_tensor("psf", [128, 3584], F32))
    psb = es.enter_context(nc.psum_tensor("psb", [128, 1024], BF16))

    identf_sb = es.enter_context(nc.sbuf_tensor("identf_sb", [128, 128], F32))
    onesrf_sb = es.enter_context(nc.sbuf_tensor("onesrf_sb", [1, 128], F32))
    ident_sb = es.enter_context(nc.sbuf_tensor("ident_sb", [128, 128], BF16))
    ident8_sb = es.enter_context(nc.sbuf_tensor("ident8_sb", [128, 128], BF16))
    onesr_sb = es.enter_context(nc.sbuf_tensor("onesr_sb", [1, 128], BF16))
    onesc_sb = es.enter_context(nc.sbuf_tensor("onesc_sb", [128, 1], BF16))
    indA_sb = es.enter_context(nc.sbuf_tensor("indA_sb", [1, 128], F32))
    indB_sb = es.enter_context(nc.sbuf_tensor("indB_sb", [1, 128], F32))
    biasb_sb = es.enter_context(nc.sbuf_tensor("biasb_sb", [1, 2 * D], BF16))
    beff_sb = es.enter_context(nc.sbuf_tensor("beff_sb", [1, D], BF16))
    bvcolf_sb = es.enter_context(nc.sbuf_tensor("bvcolf_sb", [128, 8], F32))
    bvcol_sb = es.enter_context(nc.sbuf_tensor("bvcol_sb", [128, 8], BF16))
    gs_sb = es.enter_context(nc.sbuf_tensor("gs_sb", [128, 16], F32))
    rl_sb = es.enter_context(nc.sbuf_tensor("rl_sb", [1, 1024], F32))

    sems = {}
    cnt = {}
    for n in ("ld", "gld", "mmA", "veA", "seA", "stB", "fin", "ldD", "mmD", "seD", "veD", "stD",
              "ldE", "mmE", "veE", "seE", "stE", "ldF", "mmF", "veF", "seF", "stF", "outS"):
        sems[n] = es.enter_context(nc.semaphore("s_" + n))
        cnt[n] = 0
    ck = {}

    def inc(name, instr, k=1):
        instr.then_inc(sems[name], k)
        cnt[name] += k
        return cnt[name]

    mmD_b, seD_b, veD_b, ldD_b = {}, {}, {}, {}
    mm_c = se_c = ve_c = ld_c = 0
    for qst in range(QSTN):
        cntk = PC[qst]
        ld_c += 16
        ldD_b[("mask", qst)] = ld_c
        for hg in range(8):
            ld_c += 32
            ldD_b[("kv", qst, hg)] = ld_c
            mmD_b[("base", qst, hg)] = mm_c
            mm_c += 2 * cntk
            mmD_b[("av", qst, hg)] = mm_c
            seD_b[("expb", qst, hg)] = se_c
            se_c += cntk
            seD_b[("exp", qst, hg)] = se_c
            veD_b[("mskb", qst, hg)] = ve_c
            ve_c += 8
            veD_b[("msk", qst, hg)] = ve_c
            ve_c += 1
            veD_b[("rcp", qst, hg)] = ve_c
            mm_c += 1
            mmD_b[("rl", qst, hg)] = mm_c
            ve_c += 1
            veD_b[("accbf", qst, hg)] = ve_c
        mm_c += 4
        mmD_b[("wo", qst)] = mm_c
        ve_c += 4
        veD_b[("sev", qst)] = ve_c
    MM_P4_END, SE_P4_END, VE_P4_END, LD_P4_END = mm_c, se_c, ve_c, ld_c
    ST_P4_END = 16 * 4 * QSTN

    NBLK = 8 * QSTN
    P5U = [("A", 0, r) for r in range(32)] + [("chain", 0)]
    bq = [("B", mi, r) for mi in range(3) for r in range(32)]
    bq_i = 0
    for mi_a in (1, 2):
        ar = 0
        while ar < 32:
            P5U.append(("A", mi_a, ar)); ar += 1
            if ar < 32:
                P5U.append(("A", mi_a, ar)); ar += 1
            if bq_i < len(bq) and bq[bq_i][1] != mi_a:
                P5U.append(bq[bq_i]); bq_i += 1
        P5U.append(("chain", mi_a))
    P5U += bq[bq_i:]

    def p5cost(u):
        if u[0] == "A":
            return 1.0
        if u[0] == "chain":
            return 4.0
        return 24.0 if u[2] % 16 == 0 else 17.0

    blkw = [float(PC[qst]) for qst in range(QSTN) for _ in range(8)]
    totw = sum(blkw); totc = sum(p5cost(u) for u in P5U)
    sched = [[] for _ in range(NBLK + 1)]
    bi = 0; used = 0.0; budget = blkw[0] / totw * totc
    nA = nB = 0
    for u in P5U:
        c = p5cost(u)
        while bi < NBLK and (used + c > budget or (u[0] == "A" and nA >= 8)
                             or (u[0] == "B" and nB >= 2)):
            bi += 1
            if bi < NBLK:
                budget += blkw[bi] / totw * totc
                nA = nB = 0
        sched[min(bi, NBLK)].append(u)
        used += c
        if u[0] == "A":
            nA += 1
        elif u[0] == "B":
            nB += 1

    par5 = {}
    lseq = [u for u in P5U if u[0] in ("A", "B")]
    for p, u in enumerate(lseq):
        par5[u] = p % 2
    border = [u for u in P5U if u[0] == "B"]
    bprev = {border[k]: (border[k - 1] if k >= 1 else None) for k in range(len(border))}
    bprev2 = {border[k]: (border[k - 2] if k >= 2 else None) for k in range(len(border))}

    vprog = [[] for _ in range(NBLK + 1)]
    pend = []
    for b in range(NBLK + 1):
        for p in pend:
            vprog[b].append(("ev", p[1], p[2]))
        pend = []
        tail_blk = (b < NBLK and b % 8 == 7)
        for u in sched[b]:
            vprog[b].append(u)
            if u[0] == "B":
                if pend and not tail_blk:
                    for p in pend:
                        vprog[b].append(("ev", p[1], p[2]))
                    pend = []
                pend.append(u)
    for p in pend:
        vprog[NBLK].append(("ev", p[1], p[2]))

    ldE_b, veE_b, mmE_b, seE_b, stE_b = {}, {}, {}, {}, {}
    vc = mc = sc5 = st5 = 0
    for b in range(NBLK + 1):
        for act in vprog[b]:
            vc += 1
            if act[0] == "A":
                veE_b[("A", act[1], act[2])] = vc
            elif act[0] == "chain":
                veE_b[("chain", act[1])] = vc
            elif act[0] == "B":
                veE_b[("q", act[1], act[2])] = vc
            else:
                veE_b[("ev", act[1], act[2])] = vc
    for mi in range(3):
        for r in range(32):
            sc5 += 1; seE_b[("sgn", mi, r)] = sc5
            mc += 1; mmE_b[("tr", mi, r)] = mc
            st5 += 16; stE_b[("st", mi, r)] = st5
    gldseq = []
    for u in P5U:
        if u[0] == "A":
            gldseq.append(u)
        elif u[0] == "chain":
            if u[1] < 2:
                gldseq.append(("spl", u[1]))
        elif u[0] == "B":
            if u[2] % 16 == 0 and u[1] < 2:
                gldseq.append(("rld", u[1], u[2] // 16))
            gldseq.append(u)
    lc = 0
    for it in gldseq:
        lc += 16
        ldE_b[it] = lc
    LDE_END, VEE_END, MME_END, SEE_END, STE_END = lc, vc, mc, sc5, st5
    batches = [(mi, h) for mi in range(3) for h in range(2)]

    stB_blocks = [[a for a in vprog[b] if a[0] == "ev"] for b in range(NBLK)]
    sgn_blocks = [[u for u in sched[b] if u[0] == "B"] for b in range(NBLK)]
    t_blocks = [[u for u in sched[b] if u[0] == "B"] for b in range(NBLK + 1)]
    vsplit = []
    for b in range(NBLK):
        acts = vprog[b]
        i1 = next((i for i, a in enumerate(acts) if a[0] == "B"), None)
        if i1 is None:
            vsplit.append((list(acts), []))
        else:
            vsplit.append((list(acts[:i1]), list(acts[i1:])))

    ldF_b, veF_b, mmF_b, seF_b = {}, {}, {}, {}
    lf = vf = mf = sf = 0
    lf += 16; ldF_b["xqT"] = lf
    for qd in range(4):
        for oc in range(8):
            lf += 32; ldF_b[("gu", qd, oc)] = lf
            for t4 in range(4):
                mf += 1; mmF_b[("g", qd, oc, t4)] = mf
                mf += 1; mmF_b[("u", qd, oc, t4)] = mf
                sf += 1; seF_b[("sil", qd, oc, t4)] = sf
                vf += 1; veF_b[("hid", qd, oc, t4)] = vf
                sf += 1; seF_b[("sgn", qd, oc, t4)] = sf
            vf += 1; veF_b[("dq", qd, oc)] = vf
            for t4 in range(4):
                vf += 1; veF_b[("q", qd, oc, t4)] = vf
                mf += 1; mmF_b[("tr", qd, oc, t4)] = mf
                vf += 1; veF_b[("ev", qd, oc, t4)] = vf
        for pr in range(2):
            for og in range(8):
                lf += 16; ldF_b[("wd", qd, pr, og)] = lf
                for tp in range(2):
                    mf += 1; mmF_b[("dn", qd, pr, og, tp)] = mf
            for tp in range(2):
                sf += 1; seF_b[("g", qd, pr, tp)] = sf
                lf += 32; ldF_b[("sx", qd, pr, tp)] = lf
                vf += 1; veF_b[("cmb", qd, pr, tp)] = vf
    LDF_END, VEF_END, MMF_END, SEF_END = lf, vf, mf, sf

    sgn4s = None
    def mm_sc_val(b, kt):
        return b + (kt + 1 if kt < 2 else 2 * kt)

    def mm_av_val(b, j, cntk):
        return b + (2 * j + 3 if j <= cntk - 2 else 2 * cntk)

    sgn4s = [sgn_buf[:, 0:512], sgn_buf[:, 512:1024], xnf[:, 0:512], xnf[:, 512:1024]]

    def p5_tr(t, mi, r):
        t.wait_ge(sems["veE"], veE_b[("q", mi, r)])
        pb = bprev[("B", mi, r)]
        if pb is not None:
            t.wait_ge(sems["veE"], veE_b[("ev", pb[1], pb[2])])
        xqv = xq_par[r % 2].rearrange("p (a b) -> p a b", a=8)
        for c in range(8):
            ins = t.transpose(psb[:, c * 128:(c + 1) * 128], xqv[:, c, :], ident_sb[:])
        inc("mmE", ins)

    def p5_sgn(sc_, mi, r):
        sc_.wait_ge(sems["ldE"], ldE_b[("B", mi, r)])
        pb = bprev[("B", mi, r)]
        if pb is not None:
            sc_.wait_ge(sems["veE"], veE_b[("q", pb[1], pb[2])])
        wf = wf_a if par5[("B", mi, r)] == 0 else wf_b
        ins = sc_.activation(sgn_buf[:], wf, AF.Sign)
        inc("seE", ins)

    def p5_st(sc_, mi, r):
        sc_.wait_ge(sems["veE"], veE_b[("ev", mi, r)])
        if mi < 2:
            dsp = (wgT_sp if mi == 0 else wuT_sp)[:, :, r * 128:(r + 1) * 128]
        else:
            dsp = wdT_sp[:, (r % 4) * 8:(r % 4 + 1) * 8, (r // 4) * 128:(r // 4 + 1) * 128]
        ins = sc_.dma_start(out=dsp,
                            in_=stage8[:, r % 2, :].rearrange("p (a b) -> p a b", a=8))
        inc("stE", ins, 16)

    blk = es.enter_context(nc.Block())

    @blk.sync
    def _(sy):
        for name, dst in (("ident", ident_sb[:]), ("ident8", ident8_sb[:]),
                          ("ones_row", onesr_sb[:]), ("ones_col", onesc_sb[:]),
                          ("indA", indA_sb[:]), ("indB", indB_sb[:]),
                          ("ident_f", identf_sb[:]), ("ones_row_f", onesrf_sb[:])):
            inc("ld", sy.dma_start(out=dst, in_=d[name]), 16)
        for i, name in enumerate(("bq", "bk", "bv", "bo")):
            inc("ld", sy.dma_start(out=bias_sb[:, i * D:(i + 1) * D], in_=d[name]), 16)
        inc("ld", sy.dma_start(out=bvcolf_sb[:],
                               in_=d["bv"].rearrange("o (a p) -> p o a", p=128).squeeze(1),
                               allow_slow_non_contiguous=True), 16)
        offs = 0
        for name in ("tk_sin_hi", "tk_sin_lo", "tk_cos_hi", "tk_cos_lo"):
            inc("ld", sy.dma_start(
                out=th_sb[:, offs:offs + 1024].rearrange("p (a b) -> p a b", b=32),
                in_=d[name].rearrange("(tl p) i -> p tl i", p=128)), 16)
            offs += 1024
        for name in ("tq_sin_hi", "tq_sin_lo", "tq_cos_hi", "tq_cos_lo"):
            inc("ld", sy.dma_start(
                out=th_sb[:, offs:offs + 512].rearrange("p (a b) -> p a b", b=32),
                in_=d[name].rearrange("(tl p) i -> p tl i", p=128)), 16)
            offs += 512
        inc("ld", sy.dma_start(out=gs_sb[:, 0:1],
                               in_=d["s_in"].partition_broadcast(128).unsqueeze(-1)), 16)
        inc("ld", sy.dma_start(out=gs_sb[:, 1:2],
                               in_=d["s_dn"].partition_broadcast(128).unsqueeze(-1)), 16)
        ck["ld_const"] = cnt["ld"]

        for i in range(NKV):
            if i < 2:
                sy.wait_ge(sems["veA"], VE_P0)
            else:
                sy.wait_ge(sems["veA"], VE_P1(i - 2, 2))
                sy.wait_ge(sems["seA"], SE_P1(i - 2, 1))
            dst = lnx_a if i % 2 == 0 else lnx_b
            inc("ld", sy.dma_start(out=dst, in_=d["x_kv"][i * 128:(i + 1) * 128, :]), 16)
            ck[f"p1_ld_{i}"] = cnt["ld"]
        for i in range(NQT):
            if i < 2:
                sy.wait_ge(sems["veA"], VE_P1(NKV - 2 + i, 2))
                sy.wait_ge(sems["seA"], SE_P1(NKV - 2 + i, 1))
            else:
                sy.wait_ge(sems["veA"], VE_P2(i - 2, 2))
                sy.wait_ge(sems["seA"], SE_P2(i - 2, 1))
            dst = lnx_a if i % 2 == 0 else lnx_b
            inc("ld", sy.dma_start(out=dst, in_=d["x_q"][i * 128:(i + 1) * 128, :]), 16)
            ck[f"p2_ld_{i}"] = cnt["ld"]

        sy.wait_ge(sems["stB"], 16 * 2 * NKV)
        sy.wait_ge(sems["veA"], VE_P2_END)
        for qst in range(QSTN):
            cntk = PC[qst]
            if qst >= 1:
                sy.wait_ge(sems["veD"], veD_b[("accbf", qst - 1, 7)])
            inc("ldD", sy.dma_start(out=maskbuf,
                                    in_=d["masks"][qst * 8:(qst + 1) * 8].transpose([1, 0, 2])), 16)
            for hg in range(8):
                bi = qst * 8 + hg
                if bi >= 2:
                    p2 = ((bi - 2) // 8, (bi - 2) % 8)
                    sy.wait_ge(sems["mmD"], mmD_b[("av", p2[0], p2[1])])
                par = bi % 2
                inc("ldD", sy.dma_start(out=kt_par[par][:, 0:cntk * 128],
                                        in_=kT_sp[:, hg, 0:cntk * 128]), 16)
                inc("ldD", sy.dma_start(out=v_par[par][:, 0:cntk * 128].rearrange("p (a b) -> p a b", b=128),
                                        in_=v_sp[:, 0:cntk, hg * 128:(hg + 1) * 128]), 16)
        sy.wait_ge(sems["stD"], ST_P4_END)
        inc("ldF", sy.dma_start(out=xqT, in_=xqT_sp), 16)
        gu_seq = [(qd, oc) for qd in range(4) for oc in range(8)]
        wd_seq = [(qd, pr, og) for qd in range(4) for pr in range(2) for og in range(8)]
        for qd in range(4):
            for oc in range(8):
                gi = gu_seq.index((qd, oc))
                if gi == 0:
                    sy.wait_ge(sems["stE"], stE_b[("st", 0, 31)])
                    sy.wait_ge(sems["stE"], stE_b[("st", 1, 31)])
                elif gi >= 2:
                    p2 = gu_seq[gi - 2]
                    sy.wait_ge(sems["mmF"], mmF_b[("u", p2[0], p2[1], 3)])
                par = oc % 2
                inc("ldF", sy.dma_start(out=wg_par[par], in_=wgT_sp[:, :, oc * 512:(oc + 1) * 512]), 16)
                inc("ldF", sy.dma_start(out=wu_par[par], in_=wuT_sp[:, :, oc * 512:(oc + 1) * 512]), 16)
            for pr in range(2):
                for og in range(8):
                    wi = wd_seq.index((qd, pr, og))
                    if wi == 0:
                        sy.wait_ge(sems["stE"], stE_b[("st", 2, 31)])
                    elif wi >= 2:
                        pk = wd_seq[wi - 2]
                        sy.wait_ge(sems["mmF"], mmF_b[("dn", pk[0], pk[1], pk[2], 1)])
                    inc("ldF", sy.dma_start(out=wd_par[og % 2],
                                            in_=wdT_sp[:, og * 4:(og + 1) * 4, :]), 16)
                for tp in range(2):
                    j = 4 * qd + 2 * pr + tp
                    if j >= 1:
                        sy.wait_ge(sems["outS"], 16 * j)
                    inc("ldF", sy.dma_start(out=stile2, in_=s_sp[j * 128:(j + 1) * 128, :]), 16)
                    inc("ldF", sy.dma_start(out=xtile2, in_=d["x_q"][j * 128:(j + 1) * 128, :]), 16)
        sy.wait_ge(sems["stB"], 16 * (2 * NKV + NQT))
        if DEBUG:
            inc("ld", sy.dma_start(out=qT_sp, in_=qT), 16)
        sy.wait_ge(sems["outS"], 16 * NQT)

    @blk.gpsimd
    def _(g):
        for mi, m in enumerate(("wq", "wk", "wv", "wo")):
            if mi >= 1:
                g.wait_ge(sems["mmA"], 8 * mi)
            inc("gld", g.dma_start(out=wbig_bf, in_=d[m].rearrange("(r p) h -> p r h", p=128)), 16)
            ck[f"p0_gld_w_{mi}"] = cnt["gld"]

        def wslice(mi, r):
            mn = ("gw", "uw", "dw")[mi]
            if mn != "dw":
                return d[mn][r * 128:(r + 1) * 128, :]
            return d["dw"][(r // 4) * 128:(r // 4 + 1) * 128, (r % 4) * 1024:(r % 4 + 1) * 1024]
        g.wait_ge(sems["veA"], VE_P2_END)
        lpos = 0
        for it in gldseq:
            if it[0] == "spl":
                g.wait_ge(sems["veE"], veE_b[("chain", it[1])])
                inc("ldE", g.dma_start(out=amax_sp[:, it[1] * 2048:(it[1] + 1) * 2048],
                                       in_=amaxc[:]), 16)
            elif it[0] == "rld":
                kb = batches.index((it[1], it[2]))
                if kb >= 1:
                    pmi, ph = batches[kb - 1]
                    g.wait_ge(sems["veE"], veE_b[("q", pmi, ph * 16)])
                g.wait_ge(sems["ldE"], ldE_b[("spl", it[1])])
                off = it[1] * 2048 + it[2] * 1024
                inc("ldE", g.dma_start(out=dqf, in_=amax_sp[:, off:off + 1024]), 16)
            else:
                if lpos >= 2:
                    pu = lseq[lpos - 2]
                    if pu[0] == "A":
                        g.wait_ge(sems["veE"], veE_b[("A", pu[1], pu[2])])
                    else:
                        g.wait_ge(sems["veE"], veE_b[("q", pu[1], pu[2])])
                        g.wait_ge(sems["seE"], seE_b[("sgn", pu[1], pu[2])])
                dst = wf_a if lpos % 2 == 0 else wf_b
                inc("ldE", g.dma_start(out=dst, in_=wslice(it[1], it[2])), 16)
                lpos += 1

    @blk.tensor
    def _(t):
        t.wait_ge(sems["ld"], ck["ld_const"])
        for mi, m in enumerate(("wq", "wk", "wv", "wo")):
            t.wait_ge(sems["gld"], ck[f"p0_gld_w_{mi}"])
            idn = ident_sb
            for r in range(8):
                if mi * 8 + r >= 1:
                    t.wait_ge(sems["veA"], mi * 8 + r)
                for c in range(8):
                    ins = t.transpose(psb[:, c * 128:(c + 1) * 128],
                                      wbig_bf[:, r, c * 128:(c + 1) * 128], idn[:])
                inc("mmA", ins)
        t.wait_ge(sems["veA"], 33)
        for ch in range(2):
            for kt in range(8):
                ins = t.matmul(psf[0:1, 2048 + ch * 512: 2048 + (ch + 1) * 512],
                               bvcol_sb[:, kt:kt + 1], wT["wo"][:, kt, ch * 512:(ch + 1) * 512],
                               start=(kt == 0), stop=(kt == 7))
            inc("mmA", ins)
        assert cnt["mmA"] == MM_P0

        for i in range(NKV):
            xtv = (xtile_a if i % 2 == 0 else xtile_b).rearrange("p (a b) -> p a b", a=8)
            t.wait_ge(sems["veA"], VE_P1(i, 2))
            for c in range(8):
                ins = t.transpose(psb[:, c * 128:(c + 1) * 128], xtv[:, c, :], ident_sb[:])
            inc("mmA", ins)
            t.wait_ge(sems["veA"], VE_P1(i, 3))
            xnT = stage8[:, 0, :].rearrange("p (a b) -> p a b", a=8)
            for ch in range(2):
                for kt in range(8):
                    t.matmul(psf[:, ch * 512:(ch + 1) * 512], xnT[:, kt, :],
                             wT["wk"][:, kt, ch * 512:(ch + 1) * 512],
                             start=(kt == 0), stop=False)
                ins = t.matmul(psf[:, ch * 512:(ch + 1) * 512], onesr_sb[:],
                               biasb_sb[:, D + ch * 512: D + (ch + 1) * 512],
                               start=False, stop=True)
            inc("mmA", ins)
            for ch in range(2):
                for kt in range(8):
                    ins = t.matmul(psf[:, 1024 + ch * 512: 1024 + (ch + 1) * 512], xnT[:, kt, :],
                                   wT["wv"][:, kt, ch * 512:(ch + 1) * 512],
                                   start=(kt == 0), stop=(kt == 7))
            inc("mmA", ins)
            t.wait_ge(sems["veA"], VE_P1(i, 4))
            ktv = (xtile_a if i % 2 == 0 else xtile_b).rearrange("p (a b) -> p a b", a=8)
            for c in range(8):
                ins = t.transpose(psb[:, c * 128:(c + 1) * 128], ktv[:, c, :], ident_sb[:])
            inc("mmA", ins)
        assert cnt["mmA"] == MM_P1_END

        for i in range(NQT):
            xtv = (xtile_a if i % 2 == 0 else xtile_b).rearrange("p (a b) -> p a b", a=8)
            t.wait_ge(sems["veA"], VE_P2(i, 2))
            for c in range(8):
                ins = t.transpose(psb[:, c * 128:(c + 1) * 128], xtv[:, c, :], ident_sb[:])
            inc("mmA", ins)
            t.wait_ge(sems["veA"], VE_P2(i, 3))
            xnT = stage8[:, 0, :].rearrange("p (a b) -> p a b", a=8)
            for ch in range(2):
                for kt in range(8):
                    t.matmul(psf[:, ch * 512:(ch + 1) * 512], xnT[:, kt, :],
                             wT["wq"][:, kt, ch * 512:(ch + 1) * 512],
                             start=(kt == 0), stop=False)
                ins = t.matmul(psf[:, ch * 512:(ch + 1) * 512], onesr_sb[:],
                               biasb_sb[:, ch * 512:(ch + 1) * 512], start=False, stop=True)
            inc("mmA", ins)
            t.wait_ge(sems["veA"], VE_P2(i, 4))
            qtv = (xtile_a if i % 2 == 0 else xtile_b).rearrange("p (a b) -> p a b", a=8)
            for c in range(8):
                ins = t.transpose(psb[:, c * 128:(c + 1) * 128], qtv[:, c, :], ident_sb[:])
            inc("mmA", ins)
            t.wait_ge(sems["veA"], VE_P2(i, 6))
            xqv = xq_stage.rearrange("p (a b) -> p a b", a=8)
            for c in range(8):
                ins = t.transpose(psb[:, c * 128:(c + 1) * 128], xqv[:, c, :], ident_sb[:])
            inc("mmA", ins)
        assert cnt["mmA"] == MM_P2_END

        t.wait_ge(sems["veA"], VE_P2_END)
        for qst in range(QSTN):
            cntk = PC[qst]
            for hg in range(8):
                t.wait_ge(sems["ldD"], ldD_b[("kv", qst, hg)])
                if qst >= 1 or hg >= 1:
                    phg = (qst, hg - 1) if hg >= 1 else (qst - 1, 7)
                    t.wait_ge(sems["veD"], veD_b[("accbf", phg[0], phg[1])])
                if hg == 0 and qst >= 1:
                    t.wait_ge(sems["veD"], veD_b[("sev", qst - 1)])
                qA = qT[0:64, hg, qst * 512:(qst + 1) * 512]
                qB = qT[64:128, hg, qst * 512:(qst + 1) * 512]
                nmask_start = cntk - 8
                eb = seD_b[("expb", qst, hg)]
                mb = veD_b[("mskb", qst, hg)]
                par4 = (qst * 8 + hg) % 2
                ktall = kt_par[par4]
                vt = v_par[par4][:, 0:cntk * 128].rearrange("p (a b) -> p a b", b=128)

                def emit_sc(kt):
                    buf = kt % 2
                    off = buf * 1024
                    if kt >= 2:
                        t.wait_ge(sems["seD"], eb + kt - 1)
                    ktile = ktall[:, kt * 128:(kt + 1) * 128]
                    t.matmul(psf[:, off:off + 512], ktile[0:64, :], qA,
                             start=True, stop=True, tile_position=(0, 0))
                    ins = t.matmul(psf[:, off + 512:off + 1024], ktile[64:128, :], qB,
                                   start=True, stop=True, tile_position=(64, 0))
                    inc("mmD", ins)

                def emit_av(kt):
                    buf = kt % 2
                    t.wait_ge(sems["seD"], eb + kt + 1)
                    if kt >= nmask_start:
                        t.wait_ge(sems["veD"], mb + (kt - nmask_start) + 1)
                    ex = expTb[:, buf, :]
                    t.matmul(psf[0:1, 2560:3072], onesc_sb[:], ex[:, 0:512],
                             start=(kt == 0), stop=(kt == cntk - 1))
                    t.matmul(psf[0:1, 3072:3584], onesc_sb[:], ex[:, 512:1024],
                             start=(kt == 0), stop=(kt == cntk - 1))
                    t.matmul(psf[0:64, 2048:2560], vt[:, kt, 0:64], ex[:, 0:512],
                             start=(kt == 0), stop=(kt == cntk - 1), tile_position=(0, 0),
                             skip_group_check=True)
                    ins = t.matmul(psf[64:128, 2048:2560], vt[:, kt, 64:128], ex[:, 512:1024],
                                   start=(kt == 0), stop=(kt == cntk - 1), tile_position=(0, 64),
                                   skip_group_check=True)
                    inc("mmD", ins)

                emit_sc(0)
                emit_sc(1)
                for kt in range(2, cntk):
                    emit_av(kt - 2)
                    emit_sc(kt)
                emit_av(cntk - 2)
                emit_av(cntk - 1)
                t.wait_ge(sems["veD"], veD_b[("rcp", qst, hg)])
                t.matmul(psf[:, 1024:1536], indA_sb[:], rl_sb[0:1, 0:512], start=True, stop=False)
                ins = t.matmul(psf[:, 1024:1536], indB_sb[:], rl_sb[0:1, 512:1024], start=False, stop=True)
                inc("mmD", ins)
                if hg < 7:
                    for u in t_blocks[qst * 8 + hg]:
                        p5_tr(t, u[1], u[2])
            t.wait_ge(sems["veD"], veD_b[("accbf", qst, 7)])
            for tq in range(4):
                if qst > 0 or tq > 0:
                    t.wait_ge(sems["veD"], veD_b[("sev", qst)] - 4 + tq if tq > 0
                              else veD_b[("sev", qst - 1)])
                for ch in range(2):
                    for et in range(8):
                        t.matmul(psf[:, ch * 512:(ch + 1) * 512],
                                 accT[:, et, tq * 128:(tq + 1) * 128],
                                 wT["wo"][:, et, ch * 512:(ch + 1) * 512],
                                 start=(et == 0), stop=False)
                    ins = t.matmul(psf[:, ch * 512:(ch + 1) * 512], onesr_sb[:],
                                   beff_sb[0:1, ch * 512:(ch + 1) * 512],
                                   start=False, stop=True)
                inc("mmD", ins)
            for u in t_blocks[qst * 8 + 7]:
                p5_tr(t, u[1], u[2])
        assert cnt["mmD"] == MM_P4_END

        for u in t_blocks[NBLK]:
            p5_tr(t, u[1], u[2])
        assert cnt["mmE"] == MME_END

        t.wait_ge(sems["ldF"], ldF_b["xqT"])
        for qd in range(4):
            for oc in range(8):
                t.wait_ge(sems["ldF"], ldF_b[("gu", qd, oc)])
                par = oc % 2
                for t4 in range(4):
                    j = 4 * qd + t4
                    gch = (qd * 8 + oc) * 4 + t4
                    if gch >= 2:
                        g2 = gch - 2
                        pk = (g2 // 32, (g2 // 4) % 8, g2 % 4)
                        t.wait_ge(sems["seF"], seF_b[("sil", pk[0], pk[1], pk[2])])
                        t.wait_ge(sems["veF"], veF_b[("hid", pk[0], pk[1], pk[2])])
                    if qd > 0 and oc == 0 and t4 == 1:
                        t.wait_ge(sems["seF"], seF_b[("g", qd - 1, 1, 0)])
                    gb = 1024 * (t4 % 2)
                    for kt in range(8):
                        ins = t.matmul(psf[:, gb:gb + 512], xqT[:, kt, j * 128:(j + 1) * 128],
                                       wg_par[par][:, kt, :], start=(kt == 0), stop=(kt == 7))
                    inc("mmF", ins)
                    for kt in range(8):
                        ins = t.matmul(psf[:, gb + 512:gb + 1024], xqT[:, kt, j * 128:(j + 1) * 128],
                                       wu_par[par][:, kt, :], start=(kt == 0), stop=(kt == 7))
                    inc("mmF", ins)
                for t4 in range(4):
                    t.wait_ge(sems["veF"], veF_b[("q", qd, oc, t4)])
                    hqv = hqstage.rearrange("p (a b) -> p a b", a=4)
                    for c in range(4):
                        ins = t.transpose(psb[:, c * 128:(c + 1) * 128], hqv[:, c, :], ident_sb[:])
                    inc("mmF", ins)
            for pr in range(2):
                for og in range(8):
                    t.wait_ge(sems["ldF"], ldF_b[("wd", qd, pr, og)])
                    for tp in range(2):
                        t4 = 2 * pr + tp
                        j = 4 * qd + t4
                        if og == 0:
                            t.wait_ge(sems["veF"], veF_b[("ev", qd, 7, t4)])
                            ppr = (qd, pr - 1) if pr > 0 else ((qd - 1, 1) if qd > 0 else None)
                            if ppr is not None:
                                t.wait_ge(sems["seF"], seF_b[("g", ppr[0], ppr[1], tp)])
                        base = 1024 + 1024 * tp
                        for ot in range(4):
                            go = og * 4 + ot
                            for ch in range(2):
                                ins = t.matmul(psf[:, base + ch * 512:base + (ch + 1) * 512],
                                               hqT_t[t4][:, go, :],
                                               wd_par[og % 2][:, ot, ch * 512:(ch + 1) * 512],
                                               start=(go == 0), stop=(go == 31))
                        inc("mmF", ins)
        assert cnt["mmF"] == MMF_END

    def emit_ln(v, se_sq, se_sqrt, lnx, out_bf, want_f32):
        v.tensor_reduce(scol(0), lnx, AX.X, ALU.add)
        v.wait_ge(sems["seA"], se_sq)
        v.tensor_reduce(scol(1), sqt[:], AX.X, ALU.add)
        v.drain()
        v.tensor_single_scalar(scol(2), scol(0), INV1024, ALU.mult)
        v.tensor_single_scalar(scol(4), scol(1), INV1024, ALU.mult)
        v.drain()
        v.tensor_scalar(sqt[:], lnx, scol(2), None, ALU.subtract)
        v.tensor_tensor(scol(3), scol(2), scol(2), ALU.mult)
        v.drain()
        v.scalar_tensor_tensor(scol(5), scol(3), -1.0, scol(4), ALU.mult, ALU.add)
        v.drain()
        v.tensor_single_scalar(scol(6), scol(5), EPS, ALU.add)
        v.drain()
        ins = v.reciprocal(scol(7), scol(6))
        inc("veA", ins)
        v.wait_ge(sems["seA"], se_sqrt)
        if want_f32:
            v.tensor_scalar(xnf[:], sqt[:], scol(8), None, ALU.mult)
        ins = v.tensor_scalar(out_bf, sqt[:], scol(8), None, ALU.mult)
        inc("veA", ins)

    def emit_rope(v, psrc_off, tabs, tabc, tabi, out_bf):
        v.tensor_copy(ropet[:, 0, :].rearrange("p (a b) -> p a b", b=32),
                      tabc[:, tabi, :].unsqueeze(1).broadcast_to([128, 16, 32]))
        v.tensor_copy(ropet[:, 1, :].rearrange("p (a b) -> p a b", b=32),
                      tabs[:, tabi, :].unsqueeze(1).broadcast_to([128, 16, 32]))
        v.tensor_copy(evacf[:], psf[:, psrc_off:psrc_off + 1024])
        v.drain()
        ev = evacf[:].rearrange("p (a two) -> p a two", two=2)
        e1 = ev[:, :, 0]; e2 = ev[:, :, 1]
        ob = out_bf.rearrange("p (a two) -> p a two", two=2)
        v.tensor_tensor(ropet[:, 2, :], e1, ropet[:, 0, :], ALU.mult)
        v.tensor_tensor(ropet[:, 3, :], e2, ropet[:, 1, :], ALU.mult)
        v.tensor_tensor(ropet[:, 4, :], e1, ropet[:, 1, :], ALU.mult)
        v.tensor_tensor(ropet[:, 5, :], e2, ropet[:, 0, :], ALU.mult)
        v.scalar_tensor_tensor(ob[:, :, 0], ropet[:, 3, :], -1.0, ropet[:, 2, :], ALU.mult, ALU.add)
        ins = v.scalar_tensor_tensor(ob[:, :, 1], ropet[:, 4, :], 1.0, ropet[:, 5, :], ALU.mult, ALU.add)
        return ins


    def emit_dq_chain(v, jobs):
        def both(f):
            for j in jobs:
                f(j)
        both(lambda j: v.tensor_scalar(j["sc"], j["am"], j["gs6"], 448.0, ALU.mult, ALU.min))
        both(lambda j: v.memset(j["pe"], 2.0 ** -6))
        for mlev in (8, 4, 2, 1):
            both(lambda j: v.tensor_single_scalar(j["th"], j["pe"], float(2.0 ** mlev), ALU.mult))
            both(lambda j: v.tensor_tensor(j["cc"], j["sc"], j["th"], ALU.is_ge))
            both(lambda j: v.tensor_scalar(j["fac"], j["cc"], float(2.0 ** mlev - 1), 1.0,
                                           ALU.mult, ALU.add))
            both(lambda j: v.tensor_tensor(j["pe"], j["pe"], j["fac"], ALU.mult))
        both(lambda j: v.reciprocal(j["th"], j["pe"]))
        both(lambda j: v.tensor_tensor(j["rr"], j["sc"], j["th"], ALU.mult))
        both(lambda j: v.tensor_single_scalar(j["f16"], j["rr"], 1.0 / 16.0, ALU.is_ge))
        both(lambda j: v.tensor_single_scalar(j["cc"], j["rr"], 3.0 / 16.0, ALU.is_ge))
        for jj in range(2, 16):
            tgt = "f16" if jj % 2 == 0 else "cc"
            both(lambda j, tgt=tgt, jj=jj: v.scalar_tensor_tensor(
                j[tgt], j["rr"], float((2 * jj + 1) / 16.0), j[tgt], ALU.is_ge, ALU.add))
        both(lambda j: v.tensor_tensor(j["f16"], j["f16"], j["cc"], ALU.add))
        both(lambda j: v.tensor_single_scalar(j["th"], j["pe"], 0.125, ALU.mult))
        both(lambda j: v.tensor_tensor(j["sc"], j["f16"], j["th"], ALU.mult))
        both(lambda j: v.tensor_scalar(j["am2"], j["sc"], j["invgs"], None, ALU.mult))
        both(lambda j: v.reciprocal(j["th"], j["am2"]))
        both(lambda j: v.tensor_single_scalar(j["rr"], j["am2"], 0.5, ALU.mult))
        ins = None
        for j in jobs:
            ins = v.tensor_single_scalar(j["th"], j["th"], 1e30, ALU.min)
        return ins

    def emit_main_chain(v, jobs, sign_wait):
        def both(f):
            for j in jobs:
                f(j)
        both(lambda j: v.tensor_tensor(j["a0"].rearrange("p (a b) -> p a b", b=16),
                                       j["src"].rearrange("p (a b) -> p a b", b=16),
                                       j["th"].unsqueeze(-1).broadcast_to(
                                           [128, j["nb"], 16]), ALU.mult))
        both(lambda j: v.scalar_tensor_tensor(j["ax"], j["a0"], -1.0, j["a0"], ALU.mult, ALU.max))
        both(lambda j: v.tensor_single_scalar(j["ca"], j["ax"], 0.25, ALU.is_ge))
        both(lambda j: v.tensor_single_scalar(j["cb"], j["ax"], 1.25, ALU.is_ge))
        both(lambda j: v.scalar_tensor_tensor(j["ca"], j["ax"], 0.75, j["ca"], ALU.is_ge, ALU.add))
        both(lambda j: v.scalar_tensor_tensor(j["cb"], j["ax"], 1.75, j["cb"], ALU.is_ge, ALU.add))
        both(lambda j: v.tensor_tensor(j["ca"], j["ca"], j["cb"], ALU.add))
        both(lambda j: v.tensor_single_scalar(j["cb"], j["ax"], 2.5, ALU.is_ge))
        both(lambda j: v.scalar_tensor_tensor(j["cb"], j["ax"], 3.5, j["cb"], ALU.is_ge, ALU.add))
        both(lambda j: v.scalar_tensor_tensor(j["ca"], j["cb"], 2.0, j["ca"], ALU.mult, ALU.add))
        both(lambda j: v.tensor_single_scalar(j["cb"], j["ax"], 5.0, ALU.is_ge))
        both(lambda j: v.scalar_tensor_tensor(j["ca"], j["cb"], 4.0, j["ca"], ALU.mult, ALU.add))
        both(lambda j: v.tensor_tensor(j["ca"].rearrange("p (a b) -> p a b", b=16),
                                       j["ca"].rearrange("p (a b) -> p a b", b=16),
                                       j["rr"].unsqueeze(-1).broadcast_to(
                                           [128, j["nb"], 16]), ALU.mult))
        sign_wait()
        ins = None
        for j in jobs:
            ins = v.tensor_tensor(j["out"], j["ca"], j["sgn"], ALU.mult)
        return ins

    p5rows_s = dict(sc=sqt, pe=xnf, th=stile2, cc=xtile2, fac=gtile2, rr=cmbt, f16=evacf)

    def p5_vact(v, act):
        kind = act[0]
        if kind == "A":
            _, mi, r = act
            v.wait_ge(sems["ldE"], ldE_b[("A", mi, r)])
            if r == 0 and mi > 0:
                v.wait_ge(sems["ldE"], ldE_b[("spl", mi - 1)])
            wf = wf_a if par5[act] == 0 else wf_b
            ins = v.tensor_reduce(amaxc[:, r * 64:(r + 1) * 64],
                                  wf.rearrange("p (a b) -> p a b", b=16),
                                  AX.X, ALU.max, apply_absolute_value=True)
            inc("veE", ins)
        elif kind == "chain":
            mi = act[1]
            v.drain()
            v.tensor_reduce(sqt[:, 0:1], amaxc[:], AX.X, ALU.max)
            v.drain()
            v.transpose(sqt[:, 32:64], sqt[:, 0:32])
            v.drain()
            v.tensor_reduce(sqt[:, 64:65], sqt[:, 32:64], AX.X, ALU.max)
            v.drain()
            v.stream_shuffle(sqt[0:32, 65:66], sqt[0:32, 64:65], [0] * 32)
            v.stream_shuffle(sqt[0:32, 66:67], sqt[32:64, 64:65], [0] * 32)
            v.stream_shuffle(sqt[0:32, 67:68], sqt[64:96, 64:65], [0] * 32)
            v.stream_shuffle(sqt[0:32, 68:69], sqt[96:128, 64:65], [0] * 32)
            v.drain()
            v.tensor_reduce(sqt[:, 69:70], sqt[:, 65:69], AX.X, ALU.max)
            v.drain()
            v.stream_shuffle(sqt[32:64, 69:70], sqt[0:32, 69:70], list(range(32)))
            v.stream_shuffle(sqt[64:96, 69:70], sqt[0:32, 69:70], list(range(32)))
            v.stream_shuffle(sqt[96:128, 69:70], sqt[0:32, 69:70], list(range(32)))
            v.drain()
            v.reciprocal(sqt[:, 70:71], sqt[:, 69:70])
            v.drain()
            v.tensor_single_scalar(gs_sb[:, 6 + 2 * mi:7 + 2 * mi], sqt[:, 70:71],
                                   448.0, ALU.mult)
            ins = v.tensor_single_scalar(gs_sb[:, 7 + 2 * mi:8 + 2 * mi], sqt[:, 69:70],
                                         float(1.0 / 2688.0), ALU.mult)
            inc("veE", ins)
        elif kind == "B":
            _, mi, r = act
            if r % 16 == 0:
                half = r // 16
                if mi < 2:
                    v.wait_ge(sems["ldE"], ldE_b[("rld", mi, half)])
                jobs = []
                for ji in range(2):
                    co = ji * 512
                    jd = {k: ap[:, co:co + 512] for k, ap in p5rows_s.items()}
                    jd["am"] = (dqf[:, co:co + 512] if mi < 2
                                else amaxc[:, half * 1024 + co: half * 1024 + co + 512])
                    jd["am2"] = jd["fac"]
                    jd["gs6"] = gs_sb[:, 6 + 2 * mi:7 + 2 * mi]
                    jd["invgs"] = gs_sb[:, 7 + 2 * mi:8 + 2 * mi]
                    jobs.append(jd)
                emit_dq_chain(v, jobs)
            v.wait_ge(sems["ldE"], ldE_b[("B", mi, r)])
            p2 = bprev2[("B", mi, r)]
            if p2 is not None:
                v.wait_ge(sems["mmE"], mmE_b[("tr", p2[1], p2[2])])
            wf = wf_a if par5[act] == 0 else wf_b
            off = (r % 16) * 64
            mjobs = []
            for ji in range(2):
                sl = slice(ji * 512, (ji + 1) * 512)
                mjobs.append(dict(
                    src=wf[:, sl], out=xq_par[r % 2][:, sl], sgn=sgn_buf[:, sl],
                    th=p5rows_s["th"][:, off + ji * 32: off + (ji + 1) * 32],
                    rr=p5rows_s["rr"][:, off + ji * 32: off + (ji + 1) * 32], nb=32,
                    a0=qs[0][:, sl], ax=qs[1][:, sl], ca=qs[2][:, sl], cb=qs[3][:, sl],
                ))

            def _sw(mi=mi, r=r):
                v.wait_ge(sems["seE"], seE_b[("sgn", mi, r)])
            ins = emit_main_chain(v, mjobs, _sw)
            inc("veE", ins)
        else:
            _, mi, r = act
            v.wait_ge(sems["mmE"], mmE_b[("tr", mi, r)])
            p2 = bprev2[("B", mi, r)]
            if p2 is not None:
                v.wait_ge(sems["stE"], stE_b[("st", p2[1], p2[2])])
            ins = v.tensor_copy(stage8[:, r % 2, :], psb[:])
            inc("veE", ins)

    def emit_quant(v, src, out_bf, W, gs6_col, invgs_col, sign_wait, amax_ap=None,
                   sgn_ap=None):
        if sgn_ap is None:
            sgn_ap = sgn_buf[:, 0:W]
        Wh = W // 2
        nb = Wh // 16
        jobs = []
        for ji in range(2):
            co = ji * 64
            sl = slice(ji * Wh, (ji + 1) * Wh)
            qsl = slice(ji * Wh, (ji + 1) * Wh) if W <= 1024 else None
            jobs.append(dict(
                src=src[:, sl], out=out_bf[:, sl], sgn=sgn_ap[:, sl],
                am=(dqsc[:, 0, co:co + nb] if amax_ap is None else amax_ap[:, ji * nb:(ji + 1) * nb]),
                amr=dqsc[:, 0, co:co + nb],
                sc=dqsc[:, 1, co:co + nb], pe=dqsc[:, 2, co:co + nb],
                th=dqsc[:, 3, co:co + nb], cc=dqsc[:, 4, co:co + nb],
                fac=dqsc[:, 5, co:co + nb], rr=dqsc[:, 6, co:co + nb],
                f16=dqsc[:, 7, co:co + nb],
                a0=qs[0][:, sl], ax=qs[1][:, sl], ca=qs[2][:, sl], cb=qs[3][:, sl],
            ))

        def both(f):
            for j in jobs:
                f(j)

        if amax_ap is None:
            both(lambda j: v.tensor_reduce(j["am"], j["src"].rearrange("p (a b) -> p a b", b=16),
                                           AX.X, ALU.max, apply_absolute_value=True))
        both(lambda j: v.tensor_scalar(j["sc"], j["am"], gs6_col, 448.0, ALU.mult, ALU.min))
        both(lambda j: v.memset(j["pe"], 2.0 ** -6))
        for mlev in (8, 4, 2, 1):
            both(lambda j: v.tensor_single_scalar(j["th"], j["pe"], float(2.0 ** mlev), ALU.mult))
            both(lambda j: v.tensor_tensor(j["cc"], j["sc"], j["th"], ALU.is_ge))
            both(lambda j: v.tensor_scalar(j["fac"], j["cc"], float(2.0 ** mlev - 1), 1.0,
                                           ALU.mult, ALU.add))
            both(lambda j: v.tensor_tensor(j["pe"], j["pe"], j["fac"], ALU.mult))
        both(lambda j: v.reciprocal(j["th"], j["pe"]))
        both(lambda j: v.tensor_tensor(j["rr"], j["sc"], j["th"], ALU.mult))
        both(lambda j: v.tensor_single_scalar(j["f16"], j["rr"], 1.0 / 16.0, ALU.is_ge))
        both(lambda j: v.tensor_single_scalar(j["cc"], j["rr"], 3.0 / 16.0, ALU.is_ge))
        for jj in range(2, 16):
            tgt = "f16" if jj % 2 == 0 else "cc"
            both(lambda j, tgt=tgt, jj=jj: v.scalar_tensor_tensor(
                j[tgt], j["rr"], float((2 * jj + 1) / 16.0), j[tgt], ALU.is_ge, ALU.add))
        both(lambda j: v.tensor_tensor(j["f16"], j["f16"], j["cc"], ALU.add))
        both(lambda j: v.tensor_single_scalar(j["th"], j["pe"], 0.125, ALU.mult))
        both(lambda j: v.tensor_tensor(j["sc"], j["f16"], j["th"], ALU.mult))
        both(lambda j: v.tensor_scalar(j["amr"], j["sc"], invgs_col, None, ALU.mult))
        both(lambda j: v.reciprocal(j["th"], j["amr"]))
        both(lambda j: v.tensor_single_scalar(j["rr"], j["amr"], 0.5, ALU.mult))
        both(lambda j: v.tensor_single_scalar(j["th"], j["th"], 1e30, ALU.min))
        both(lambda j: v.tensor_tensor(j["a0"].rearrange("p (a b) -> p a b", b=16),
                                       j["src"].rearrange("p (a b) -> p a b", b=16),
                                       j["th"].unsqueeze(-1).broadcast_to([128, nb, 16]),
                                       ALU.mult))
        both(lambda j: v.scalar_tensor_tensor(j["ax"], j["a0"], -1.0, j["a0"], ALU.mult, ALU.max))
        both(lambda j: v.tensor_single_scalar(j["ca"], j["ax"], 0.25, ALU.is_ge))
        both(lambda j: v.tensor_single_scalar(j["cb"], j["ax"], 1.25, ALU.is_ge))
        both(lambda j: v.scalar_tensor_tensor(j["ca"], j["ax"], 0.75, j["ca"], ALU.is_ge, ALU.add))
        both(lambda j: v.scalar_tensor_tensor(j["cb"], j["ax"], 1.75, j["cb"], ALU.is_ge, ALU.add))
        both(lambda j: v.tensor_tensor(j["ca"], j["ca"], j["cb"], ALU.add))
        both(lambda j: v.tensor_single_scalar(j["cb"], j["ax"], 2.5, ALU.is_ge))
        both(lambda j: v.scalar_tensor_tensor(j["cb"], j["ax"], 3.5, j["cb"], ALU.is_ge, ALU.add))
        both(lambda j: v.scalar_tensor_tensor(j["ca"], j["cb"], 2.0, j["ca"], ALU.mult, ALU.add))
        both(lambda j: v.tensor_single_scalar(j["cb"], j["ax"], 5.0, ALU.is_ge))
        both(lambda j: v.scalar_tensor_tensor(j["ca"], j["cb"], 4.0, j["ca"], ALU.mult, ALU.add))
        both(lambda j: v.tensor_tensor(j["ca"].rearrange("p (a b) -> p a b", b=16),
                                       j["ca"].rearrange("p (a b) -> p a b", b=16),
                                       j["rr"].unsqueeze(-1).broadcast_to([128, nb, 16]),
                                       ALU.mult))
        sign_wait()
        v.tensor_tensor(jobs[0]["out"], jobs[0]["ca"], jobs[0]["sgn"], ALU.mult)
        ins = v.tensor_tensor(jobs[1]["out"], jobs[1]["ca"], jobs[1]["sgn"], ALU.mult)
        return ins

    @blk.vector
    def _(v):
        v.wait_ge(sems["ld"], ck["ld_const"])
        for mi in range(4):
            for r in range(8):
                v.wait_ge(sems["mmA"], mi * 8 + r + 1)
                ins = v.tensor_copy(wT[("wq", "wk", "wv", "wo")[mi]][:, :, r * 128:(r + 1) * 128],
                                    psb[:].rearrange("p (a b) -> p a b", a=8))
                inc("veA", ins)
        v.tensor_copy(biasb_sb[:, 0:D], bias_sb[:, 0:D])
        v.tensor_copy(biasb_sb[:, D:2 * D], bias_sb[:, D:2 * D])
        v.reciprocal(gs_sb[:, 3:4], gs_sb[:, 0:1])
        v.reciprocal(gs_sb[:, 5:6], gs_sb[:, 1:2])
        ins = v.tensor_copy(bvcol_sb[:], bvcolf_sb[:])
        inc("veA", ins)
        v.wait_ge(sems["mmA"], MM_P0)
        ins = v.tensor_tensor(beff_sb[:], psf[0:1, 2048:3072], bias_sb[:, 3 * D:4 * D], ALU.add)
        inc("veA", ins)
        v.tensor_tensor(tab_ks[:].rearrange("p a b -> p (a b)"),
                        th_sb[:, 0:1024], th_sb[:, 1024:2048], ALU.add)
        v.tensor_tensor(tab_kc[:].rearrange("p a b -> p (a b)"),
                        th_sb[:, 2048:3072], th_sb[:, 3072:4096], ALU.add)
        v.tensor_tensor(tab_qs[:].rearrange("p a b -> p (a b)"),
                        th_sb[:, 4096:4608], th_sb[:, 4608:5120], ALU.add)
        ins = v.tensor_tensor(tab_qc[:].rearrange("p a b -> p (a b)"),
                              th_sb[:, 5120:5632], th_sb[:, 5632:6144], ALU.add)
        inc("veA", ins)
        assert cnt["veA"] == VE_P0

        for i in range(NKV):
            lnx = lnx_a if i % 2 == 0 else lnx_b
            xt = xtile_a if i % 2 == 0 else xtile_b
            v.wait_ge(sems["ld"], ck[f"p1_ld_{i}"])
            v.wait_ge(sems["seA"], 1)
            emit_ln(v, SE_P1(i, 1), SE_P1(i, 2), lnx, xt[:], want_f32=False)
            v.wait_ge(sems["mmA"], MM_P1(i, 1))
            v.wait_ge(sems["stB"], 32 * i)
            ins = v.tensor_copy(stage8[:, 0, :], psb[:])
            inc("veA", ins)
            v.wait_ge(sems["mmA"], MM_P1(i, 2))
            ins = emit_rope(v, 0, tab_ks, tab_kc, i, xt[:])
            inc("veA", ins)
            v.wait_ge(sems["mmA"], MM_P1(i, 3))
            v.tensor_copy(stage8[:, 1, :], psf[:, 1024:2048])
            v.wait_ge(sems["mmA"], MM_P1(i, 4))
            ins = v.tensor_copy(stage8[:, 0, :], psb[:])
            inc("veA", ins)
        assert cnt["veA"] == VE_P1_END

        for i in range(NQT):
            lnx = lnx_a if i % 2 == 0 else lnx_b
            xt = xtile_a if i % 2 == 0 else xtile_b
            v.wait_ge(sems["ld"], ck[f"p2_ld_{i}"])
            emit_ln(v, SE_P2(i, 1), SE_P2(i, 2), lnx, xt[:], want_f32=True)
            v.wait_ge(sems["mmA"], MM_P2(i, 1))
            v.wait_ge(sems["stB"], ST_P1_END + 16 * i)
            ins = v.tensor_copy(stage8[:, 0, :], psb[:])
            inc("veA", ins)
            v.wait_ge(sems["mmA"], MM_P2(i, 2))
            ins = emit_rope(v, 0, tab_qs, tab_qc, i, xt[:])
            inc("veA", ins)
            v.wait_ge(sems["mmA"], MM_P2(i, 3))
            ins = v.tensor_copy(qT[:, :, i * 128:(i + 1) * 128], psb[:].rearrange("p (a b) -> p a b", a=8))
            inc("veA", ins)
            def _sw(i=i):
                v.wait_ge(sems["seA"], SE_P2(i, 3))
            ins = emit_quant(v, xnf, xq_stage, 1024, gs_sb[:, 2:3], gs_sb[:, 3:4], _sw,
                             sgn_ap=sgn_buf)
            inc("veA", ins)
            v.wait_ge(sems["mmA"], MM_P2(i, 4))
            ins = v.tensor_copy(stage8[:, 0, :], psb[:])
            inc("veA", ins)
        assert cnt["veA"] == VE_P2_END

        for qst in range(QSTN):
            cntk = PC[qst]
            pump_tail = None
            for hg in range(8):
                b = qst * 8 + hg
                part1, part2 = vsplit[b]
                state = {"p1": list(part1), "p2": list(part2), "k": 0, "npts": 10}
                while state["p1"] and state["p1"][0][0] == "ev":
                    p5_vact(v, state["p1"].pop(0))

                def pump(state=state):
                    state["k"] += 1
                    k = state["k"]
                    if k <= 7:
                        src, rem = state["p1"], 7 - k
                    else:
                        if state["p1"]:
                            state["p2"] = state["p1"] + state["p2"]
                            state["p1"] = []
                        src, rem = state["p2"], state["npts"] - k
                    n = len(src)
                    take = n if rem <= 0 else -(-n // (rem + 1))
                    for _ in range(take):
                        p5_vact(v, src.pop(0))
                if hg == 7:
                    pump_tail = pump
                nmask_start = cntk - 8
                for j in range(8):
                    kt = nmask_start + j
                    buf = kt % 2
                    v.wait_ge(sems["seD"], seD_b[("expb", qst, hg)] + kt + 1)
                    ex = expTb[:, buf, :]
                    v.tensor_tensor(ex[:, 0:512], ex[:, 0:512], maskbuf[:, j, :], ALU.mult)
                    ins = v.tensor_tensor(ex[:, 512:1024], ex[:, 512:1024], maskbuf[:, j, :], ALU.mult)
                    inc("veD", ins)
                    pump()
                v.wait_ge(sems["mmD"], mmD_b[("av", qst, hg)])
                v.reciprocal(rl_sb[0:1, 0:512], psf[0:1, 2560:3072])
                ins = v.reciprocal(rl_sb[0:1, 512:1024], psf[0:1, 3072:3584])
                inc("veD", ins)
                pump()
                v.wait_ge(sems["mmD"], mmD_b[("rl", qst, hg)])
                v.tensor_copy(evacf[:, 0:512], psf[:, 1024:1536])
                v.drain()
                ins = v.tensor_tensor(accT[:, hg, :], psf[:, 2048:2560],
                                      evacf[:, 0:512], ALU.mult)
                inc("veD", ins)
                pump()
            for tq in range(4):
                v.wait_ge(sems["mmD"], mmD_b[("wo", qst)] - 4 + tq + 1)
                v.wait_ge(sems["stD"], 16 * (4 * qst + tq))
                ins = v.tensor_copy(stile[:], psf[:, 0:1024])
                inc("veD", ins)
                pump_tail()
        assert cnt["veD"] == VE_P4_END
        for act in vprog[NBLK]:
            p5_vact(v, act)
        assert cnt["veE"] == VEE_END


        for qd in range(4):
            for oc in range(8):
                pend_amax = []
                for t4 in range(4):
                    v.wait_ge(sems["mmF"], mmF_b[("u", qd, oc, t4)])
                    v.wait_ge(sems["seF"], seF_b[("sil", qd, oc, t4)])
                    ins = v.tensor_tensor(hidf4[:, t4, :], hsg_par[t4 % 2][:],
                                          psf[:, 1024 * (t4 % 2) + 512:1024 * (t4 % 2) + 1024],
                                          ALU.mult)
                    inc("veF", ins)
                    pend_amax.append(t4)
                    if len(pend_amax) >= 2:
                        ta = pend_amax.pop(0)
                        v.tensor_reduce(dqsc[:, 0, ta * 32:(ta + 1) * 32],
                                        hidf4[:, ta, :].rearrange("p (a b) -> p a b", b=16),
                                        AX.X, ALU.max, apply_absolute_value=True)
                for ta in pend_amax:
                    v.tensor_reduce(dqsc[:, 0, ta * 32:(ta + 1) * 32],
                                    hidf4[:, ta, :].rearrange("p (a b) -> p a b", b=16),
                                    AX.X, ALU.max, apply_absolute_value=True)
                jobs = []
                for ji in range(2):
                    co = ji * 64
                    jobs.append(dict(
                        am=dqsc[:, 0, co:co + 64], am2=dqsc[:, 0, co:co + 64],
                        sc=dqsc[:, 1, co:co + 64], pe=dqsc[:, 2, co:co + 64],
                        th=dqsc[:, 3, co:co + 64], cc=dqsc[:, 4, co:co + 64],
                        fac=dqsc[:, 5, co:co + 64], rr=dqsc[:, 6, co:co + 64],
                        f16=dqsc[:, 7, co:co + 64],
                        gs6=gs_sb[:, 4:5], invgs=gs_sb[:, 5:6],
                    ))
                ins = emit_dq_chain(v, jobs)
                inc("veF", ins)
                sgn4 = [sgn_buf[:, 0:512], sgn_buf[:, 512:1024], xnf[:, 0:512], xnf[:, 512:1024]]
                for t4 in range(4):
                    if (qd, oc, t4) != (0, 0, 0):
                        pk = (qd, oc, t4 - 1) if t4 > 0 else ((qd, oc - 1, 3) if oc > 0
                                                              else (qd - 1, 7, 3))
                        v.wait_ge(sems["mmF"], mmF_b[("tr", pk[0], pk[1], pk[2])])
                    mjobs = []
                    for ji in range(2):
                        co = t4 * 32 + ji * 16
                        sl = slice(ji * 256, (ji + 1) * 256)
                        mjobs.append(dict(
                            src=hidf4[:, t4, sl], out=hqstage[:, sl], sgn=sgn4[t4][:, sl],
                            th=dqsc[:, 3, co:co + 16], rr=dqsc[:, 6, co:co + 16], nb=16,
                            a0=qs[0][:, sl], ax=qs[1][:, sl], ca=qs[2][:, sl], cb=qs[3][:, sl],
                        ))

                    def _sw2(qd=qd, oc=oc, t4=t4):
                        v.wait_ge(sems["seF"], seF_b[("sgn", qd, oc, t4)])
                    ins = emit_main_chain(v, mjobs, _sw2)
                    inc("veF", ins)
                    v.wait_ge(sems["mmF"], mmF_b[("tr", qd, oc, t4)])
                    ins = v.tensor_copy(hqT_t[t4][:, oc * 4:(oc + 1) * 4, :],
                                        psb[:, 0:512].rearrange("p (a b) -> p a b", a=4))
                    inc("veF", ins)
            for pr in range(2):
                for tp in range(2):
                    v.wait_ge(sems["seF"], seF_b[("g", qd, pr, tp)])
                    v.wait_ge(sems["ldF"], ldF_b[("sx", qd, pr, tp)])
                    v.tensor_tensor(cmbt[:, 0:512], stile2[:, 0:512], gtile2[:, 0:512], ALU.mult)
                    v.tensor_tensor(cmbt[:, 512:1024], stile2[:, 512:1024], gtile2[:, 512:1024], ALU.mult)
                    v.tensor_tensor(stile2[:, 0:512], cmbt[:, 0:512], xtile2[:, 0:512], ALU.add)
                    ins = v.tensor_tensor(stile2[:, 512:1024], cmbt[:, 512:1024], xtile2[:, 512:1024], ALU.add)
                    inc("veF", ins)
        assert cnt["veF"] == VEF_END

    @blk.scalar
    def _(sc_):
        sc_.wait_ge(sems["ld"], ck["ld_const"])
        sc_.mul(gs_sb[:, 2:3], gs_sb[:, 0:1], 1.0 / 6.0)
        sc_.mul(gs_sb[:, 4:5], gs_sb[:, 1:2], 1.0 / 6.0)
        sc_.wait_ge(sems["veA"], VE_P0)
        sc_.activation(tab_ks[:].rearrange("p a b -> p (a b)"),
                       tab_ks[:].rearrange("p a b -> p (a b)"), AF.Sin)
        sc_.activation(tab_kc[:].rearrange("p a b -> p (a b)"),
                       tab_kc[:].rearrange("p a b -> p (a b)"), AF.Sin)
        sc_.activation(tab_qs[:].rearrange("p a b -> p (a b)"),
                       tab_qs[:].rearrange("p a b -> p (a b)"), AF.Sin)
        sc_.activation(tab_qc[:].rearrange("p a b -> p (a b)"),
                       tab_qc[:].rearrange("p a b -> p (a b)"), AF.Sin)
        sc_.mul(tab_qs[:].rearrange("p a b -> p (a b)"),
                tab_qs[:].rearrange("p a b -> p (a b)"), 0.125)
        ins = sc_.mul(tab_qc[:].rearrange("p a b -> p (a b)"),
                      tab_qc[:].rearrange("p a b -> p (a b)"), 0.125)
        inc("seA", ins)
        for i in range(NKV):
            sc_.wait_ge(sems["ld"], ck[f"p1_ld_{i}"])
            if i >= 1:
                sc_.wait_ge(sems["veA"], VE_P1(i - 1, 2))
            lnx = lnx_a if i % 2 == 0 else lnx_b
            ins = sc_.activation(sqt[:], lnx, AF.Square)
            inc("seA", ins)
            sc_.wait_ge(sems["veA"], VE_P1(i, 1))
            ins = sc_.activation(scol(8), scol(7), AF.Sqrt)
            inc("seA", ins)
            sc_.wait_ge(sems["veA"], VE_P1(i, 5))
            inc("stB", sc_.dma_start(out=kT_sp[:, :, i * 128:(i + 1) * 128],
                                     in_=stage8[:, 0, :].rearrange("p (a b) -> p a b", a=8)), 16)
            inc("stB", sc_.dma_start(out=v_sp[:, i, :], in_=stage8[:, 1, :]), 16)
        for i in range(NQT):
            sc_.wait_ge(sems["ld"], ck[f"p2_ld_{i}"])
            sc_.wait_ge(sems["veA"], VE_P1(NKV - 1, 2) if i == 0 else VE_P2(i - 1, 2))
            lnx = lnx_a if i % 2 == 0 else lnx_b
            ins = sc_.activation(sqt[:], lnx, AF.Square)
            inc("seA", ins)
            sc_.wait_ge(sems["veA"], VE_P2(i, 1))
            ins = sc_.activation(scol(8), scol(7), AF.Sqrt)
            inc("seA", ins)
            sc_.wait_ge(sems["veA"], VE_P2(i, 2))
            ins = sc_.activation(sgn_buf[:], xnf[:], AF.Sign)
            inc("seA", ins)
            sc_.wait_ge(sems["veA"], VE_P2(i, 7))
            inc("stB", sc_.dma_start(out=xqT_sp[:, :, i * 128:(i + 1) * 128],
                                     in_=stage8[:, 0, :].rearrange("p (a b) -> p a b", a=8)), 16)

        for qst in range(QSTN):
            cntk = PC[qst]
            for hg in range(8):
                b = qst * 8 + hg
                nmask_start = cntk - 8
                bqh = mmD_b[("base", qst, hg)]
                for kt in range(cntk):
                    buf = kt % 2
                    sc_.wait_ge(sems["mmD"], mm_sc_val(bqh, kt))
                    if kt >= 2:
                        sc_.wait_ge(sems["mmD"], mm_av_val(bqh, kt - 2, cntk))
                        if kt - 2 >= nmask_start:
                            sc_.wait_ge(sems["veD"],
                                        veD_b[("mskb", qst, hg)] + (kt - 2 - nmask_start) + 1)
                    ins = sc_.activation(expTb[:, buf, :], psf[:, buf * 1024:(buf + 1) * 1024], AF.Exp)
                    inc("seD", ins)
                for act in vprog[b]:
                    if act[0] == "B":
                        p5_sgn(sc_, act[1], act[2])
                    elif act[0] == "ev":
                        p5_st(sc_, act[1], act[2])
            for tq in range(4):
                sc_.wait_ge(sems["veD"], veD_b[("sev", qst)] - 4 + tq + 1)
                row = qst * 512 + tq * 128
                inc("stD", sc_.dma_start(out=s_sp[row:row + 128, :], in_=stile[:]), 16)

        for act in vprog[NBLK]:
            if act[0] == "B":
                p5_sgn(sc_, act[1], act[2])
            elif act[0] == "ev":
                p5_st(sc_, act[1], act[2])
        assert cnt["seE"] == SEE_END and cnt["stE"] == STE_END

        for qd in range(4):
            for oc in range(8):
                for t4 in range(4):
                    sc_.wait_ge(sems["mmF"], mmF_b[("g", qd, oc, t4)])
                    if (qd, oc, t4) != (0, 0, 0):
                        pk = (qd, oc, t4 - 2) if t4 >= 2 else ((qd, oc - 1, t4 + 2) if oc > 0
                                                               else (qd - 1, 7, t4 + 2))
                        if pk[2] <= 3 and pk[0] >= 0:
                            sc_.wait_ge(sems["veF"], veF_b[("hid", pk[0], pk[1], pk[2])])
                    ins = sc_.activation(hsg_par[t4 % 2][:],
                                         psf[:, 1024 * (t4 % 2):1024 * (t4 % 2) + 512], AF.Silu)
                    inc("seF", ins)
                    sc_.wait_ge(sems["veF"], veF_b[("hid", qd, oc, t4)])
                    ins = sc_.activation(sgn4s[t4], hidf4[:, t4, :], AF.Sign)
                    inc("seF", ins)
            for pr in range(2):
                for tp in range(2):
                    t4 = 2 * pr + tp
                    j = 4 * qd + t4
                    sc_.wait_ge(sems["mmF"], mmF_b[("dn", qd, pr, 7, tp)])
                    ins = sc_.activation(gtile2[:], psf[:, 1024 + 1024 * tp:2048 + 1024 * tp], AF.Silu)
                    inc("seF", ins)
                    sc_.wait_ge(sems["veF"], veF_b[("cmb", qd, pr, tp)])
                    ins = sc_.dma_start(out=out_d[j * 128:(j + 1) * 128, :], in_=stile2[:])
                    inc("outS", ins, 16)

        es.close()
    return nc




_PROGRAM_CACHE = {}


def kernel(**inputs):
    in_maps = host_prep(inputs)
    if "nc" not in _PROGRAM_CACHE:
        _PROGRAM_CACHE["nc"] = build_program()
    nc = _PROGRAM_CACHE["nc"]
    res = run_bass_kernel_spmd(nc, in_maps, core_ids=list(range(8)))
    out = np.empty((B, S, D), dtype=np.float32)
    for c in range(8):
        b, sub = c // 2, c % 2
        q0s = Q0S[sub]
        oc = np.asarray(res.results[c]["out"])
        for sl, q0 in enumerate(q0s):
            out[b, q0:q0 + 512, :] = oc[sl * 512:(sl + 1) * 512, :]
    return out


# version 15
# speedup vs baseline: 1.0483x; 1.0213x over previous
import math
import os
import numpy as np
import ml_dtypes
from contextlib import ExitStack
import concourse.bass as bass
import concourse.mybir as mybir
from concourse.bass_utils import run_bass_kernel_spmd

F32 = mybir.dt.float32
BF16 = mybir.dt.bfloat16
ALU = mybir.AluOpType
AF = mybir.ActivationFunctionType
AX = mybir.AxisListType

B, S, D, H, HD = 4, 4096, 1024, 16, 64
NKV, NQT, NQ, QSTN = 32, 16, 2048, 4
PC = [32, 24, 16, 8]
Q0S = {0: [3584, 2048, 1536, 0], 1: [3072, 2560, 1024, 512]}
EPS = 1e-5
INV1024 = float(np.float32(1.0 / 1024.0))
DEBUG = 0

VE_P0 = 35
VE_P1 = lambda i, st: VE_P0 + 5 * i + st
VE_P1_END = VE_P0 + 5 * NKV
VE_P2 = lambda i, st: VE_P1_END + 7 * i + st
VE_P2_END = VE_P1_END + 7 * NQT
SE_P1 = lambda i, st: 1 + 2 * i + st
SE_P1_END = 1 + 2 * NKV
SE_P2 = lambda i, st: SE_P1_END + 3 * i + st
MM_P0 = 34
MM_P1 = lambda i, st: MM_P0 + 4 * i + st
MM_P1_END = MM_P0 + 4 * NKV
MM_P2 = lambda i, st: MM_P1_END + 4 * i + st
MM_P2_END = MM_P1_END + 4 * NQT
ST_P1_END = 32 * NKV


def _theta_red_hilo(pos, shift):
    i = np.arange(HD // 2, dtype=np.float64)[None, :]
    inv = np.exp(-2.0 * i / HD * math.log(10000.0))
    th = pos[:, None] * inv + shift
    k = np.floor((th + math.pi) / (2 * math.pi))
    thr = th - 2 * math.pi * k
    hi = thr.astype(np.float32)
    lo = (thr - hi.astype(np.float64)).astype(np.float32)
    return hi, lo


def host_prep(inputs):
    x = np.ascontiguousarray(inputs["L0_attn_x"], dtype=np.float32)
    ident = np.eye(128, dtype=ml_dtypes.bfloat16)
    ident8 = (np.eye(128, dtype=np.float32) * 0.125).astype(ml_dtypes.bfloat16)
    pos_kv = np.arange(S, dtype=np.float64)
    tkh_s, tkl_s = _theta_red_hilo(pos_kv, 0.0)
    tkh_c, tkl_c = _theta_red_hilo(pos_kv, math.pi / 2)
    indA = np.zeros((1, 128), dtype=np.float32); indA[0, :64] = 1.0
    indB = np.zeros((1, 128), dtype=np.float32); indB[0, 64:] = 1.0
    shared = {
        "ident": ident, "ident8": ident8,
        "ones_row": np.ones((1, 128), dtype=ml_dtypes.bfloat16),
        "ones_col": np.ones((128, 1), dtype=ml_dtypes.bfloat16),
        "indA": indA, "indB": indB,
        "ident_f": np.eye(128, dtype=np.float32),
        "ones_row_f": np.ones((1, 128), dtype=np.float32),
        "wq": inputs["L0_attn_wq"], "wk": inputs["L0_attn_wk"],
        "wv": inputs["L0_attn_wv"], "wo": inputs["L0_attn_wo"],
        "bq": np.ascontiguousarray(inputs["L0_attn_bq"].reshape(1, D)),
        "bk": np.ascontiguousarray(inputs["L0_attn_bk"].reshape(1, D)),
        "bv": np.ascontiguousarray(inputs["L0_attn_bv"].reshape(1, D)),
        "bo": np.ascontiguousarray(inputs["L0_attn_bo"].reshape(1, D)),
        "gw": inputs["L0_ffn_gate_w"], "uw": inputs["L0_ffn_up_w"],
        "dw": inputs["L0_ffn_down_w"],
        "s_in": np.ascontiguousarray(inputs["L0_ffn_s_in"].reshape(1)),
        "s_dn": np.ascontiguousarray(inputs["L0_ffn_s_in_down"].reshape(1)),
        "tk_sin_hi": tkh_s, "tk_sin_lo": tkl_s, "tk_cos_hi": tkh_c, "tk_cos_lo": tkl_c,
    }
    in_maps = []
    for c in range(8):
        b, sub = c // 2, c % 2
        q0s = Q0S[sub]
        qrows = np.concatenate([np.arange(q0, q0 + 512) for q0 in q0s])
        pos_q = qrows.astype(np.float64)
        tqh_s, tql_s = _theta_red_hilo(pos_q, 0.0)
        tqh_c, tql_c = _theta_red_hilo(pos_q, math.pi / 2)
        masks = np.zeros((QSTN * 8, 128, 512), dtype=np.float32)
        for sl in range(QSTN):
            q0 = q0s[sl]
            for j in range(8):
                kt = PC[sl] - 8 + j
                kpos = 128 * kt + np.arange(128)[:, None]
                qpos = q0 + np.arange(512)[None, :]
                masks[sl * 8 + j] = (kpos <= qpos)
        m = dict(shared)
        m.update({
            "x_q": np.ascontiguousarray(x[b][qrows]), "x_kv": np.ascontiguousarray(x[b]),
            "tq_sin_hi": tqh_s, "tq_sin_lo": tql_s, "tq_cos_hi": tqh_c, "tq_cos_lo": tql_c,
            "masks": masks.astype(ml_dtypes.bfloat16),
        })
        in_maps.append(m)
    return in_maps


def build_program():
    nc = bass.Bass("TRN2", target_bir_lowering=False, detect_race_conditions=False)
    es = ExitStack()
    d = {}

    def din(name, shape, dt=F32):
        d[name] = nc.dram_tensor(name, shape, dt, kind="ExternalInput").ap()

    din("x_q", [NQ, D]); din("x_kv", [S, D])
    for n in ("wq", "wk", "wv", "wo"): din(n, [D, D])
    for n in ("bq", "bk", "bv", "bo"): din(n, [1, D])
    din("gw", [4096, D]); din("uw", [4096, D]); din("dw", [D, 4096])
    din("s_in", [1]); din("s_dn", [1])
    din("ident", [128, 128], BF16); din("ident8", [128, 128], BF16)
    din("ones_row", [1, 128], BF16); din("ones_col", [128, 1], BF16)
    din("indA", [1, 128]); din("indB", [1, 128])
    din("ident_f", [128, 128]); din("ones_row_f", [1, 128])
    for n in ("tq_sin_hi", "tq_sin_lo", "tq_cos_hi", "tq_cos_lo"): din(n, [NQ, HD // 2])
    for n in ("tk_sin_hi", "tk_sin_lo", "tk_cos_hi", "tk_cos_lo"): din(n, [S, HD // 2])
    din("masks", [QSTN * 8, 128, 512], BF16)

    out_d = nc.dram_tensor("out", [NQ, D], F32, kind="ExternalOutput").ap()
    spill_kind = "ExternalOutput" if DEBUG else "Internal"
    kT_sp = nc.dram_tensor("kT_sp", [128, 8, S], BF16, kind=spill_kind).ap()
    v_sp = nc.dram_tensor("v_sp", [128, NKV, D], BF16, kind=spill_kind).ap()
    xqT_sp = nc.dram_tensor("xqT_sp", [128, 8, NQ], BF16, kind=spill_kind).ap()
    s_sp = nc.dram_tensor("s_sp", [NQ, D], F32, kind=spill_kind).ap()
    wgT_sp = nc.dram_tensor("wgT_sp", [128, 8, 4096], BF16, kind=spill_kind).ap()
    wuT_sp = nc.dram_tensor("wuT_sp", [128, 8, 4096], BF16, kind=spill_kind).ap()
    wdT_sp = nc.dram_tensor("wdT_sp", [128, 32, D], BF16, kind=spill_kind).ap()
    qT_sp = nc.dram_tensor("qT_sp", [128, 8, NQ], BF16, kind=spill_kind).ap()
    amax_sp = nc.dram_tensor("amax_sp", [128, 4096], F32, kind=spill_kind).ap()

    SB_BF = 62464
    bfp = es.enter_context(nc.sbuf_tensor("bfp", [128, SB_BF], BF16))

    def bfv(off, shape):
        n = int(np.prod(shape))
        apv = bfp[:, off:off + n]
        if len(shape) > 1:
            letters = "abcdefg"[: len(shape)]
            apv = apv.rearrange(f"p ({' '.join(letters)}) -> p {' '.join(letters)}",
                                **{letters[i]: shape[i] for i in range(len(shape))})
        return apv

    qT = bfv(0, [8, NQ])
    wT = {m: bfv(16384 + i * 8192, [8, 1024])
          for i, m in enumerate(("wq", "wk", "wv", "wo"))}
    xtile_a = bfv(49152, [1024])
    xtile_b = bfv(50176, [1024])
    xq_stage = bfv(51200, [1024])
    xq_par = [bfv(51200, [1024]), bfv(34816, [1024])]
    stage8 = bfv(52224, [2, 1024])
    wbig_bf = bfv(54272, [8, 1024])
    xqT = bfv(0, [8, NQ])
    wg_par = [bfv(16384, [8, 512]), bfv(41472, [8, 512])]
    wu_par = [bfv(20480, [8, 512]), bfv(45568, [8, 512])]
    hqT_t = [bfv(24576 + 4096 * i, [32, 128]) for i in range(4)]
    hqstage = bfv(40960, [512])
    wd_par = [bfv(54272, [4, 1024]), bfv(58368, [4, 1024])]
    kt_par = [bfv(16384, [4096]), bfv(54272, [4096])]
    v_par = [bfv(20480, [4096]), bfv(58368, [4096])]
    maskbuf = bfv(24576, [8, 512])
    accT = bfv(28672, [8, 512])
    expTb = bfv(32768, [2, 1024])

    SB_F32 = 18432
    fpool = es.enter_context(nc.sbuf_tensor("fpool", [128, SB_F32], F32))

    def fv(off, shape):
        n = int(np.prod(shape))
        apv = fpool[:, off:off + n]
        if len(shape) > 1:
            letters = "abcdefg"[: len(shape)]
            apv = apv.rearrange(f"p ({' '.join(letters)}) -> p {' '.join(letters)}",
                                **{letters[i]: shape[i] for i in range(len(shape))})
        return apv

    qs = [fv(i * 1024, [1024]) for i in range(4)]
    lnx_a = fv(4096, [1024]); lnx_b = fv(5120, [1024])
    sqt = fv(6144, [1024])
    xnf = fv(7168, [1024])
    ropet = fv(8192, [6, 512])
    tab_ks = fv(11264, [32, 32]); tab_kc = fv(12288, [32, 32])
    tab_qs = fv(13312, [16, 32]); tab_qc = fv(13824, [16, 32])
    dqsc = fv(14336, [8, 128])
    dqf = fv(14336, [1024])
    stile = fv(17408, [1024])
    amaxc = fv(8192, [2048])
    wf_a = fv(4096, [1024]); wf_b = fv(5120, [1024])
    wf4_a = fv(4096, [1024]); wf4_b = fv(5120, [1024])
    stile2 = fv(10240, [1024])
    xtile2 = fv(11264, [1024])
    gtile2 = fv(12288, [1024])
    hidf4 = fv(4096, [4, 512])
    hsg_par = [fv(6144, [512]), fv(6656, [512])]
    cmbt = fv(13312, [1024])
    sgn_buf = fv(15360, [1024])
    evacf = fv(16384, [1024])

    th_sb = fpool[:, 0:6144]
    bias_sb = fpool[0:1, 6144:6144 + 4 * D]
    scl = es.enter_context(nc.sbuf_tensor("scl", [128, 16], F32))

    def scol(i):
        return scl[:, i:i + 1]

    psf = es.enter_context(nc.psum_tensor("psf", [128, 3584], F32))
    psb = es.enter_context(nc.psum_tensor("psb", [128, 1024], BF16))

    identf_sb = es.enter_context(nc.sbuf_tensor("identf_sb", [128, 128], F32))
    onesrf_sb = es.enter_context(nc.sbuf_tensor("onesrf_sb", [1, 128], F32))
    ident_sb = es.enter_context(nc.sbuf_tensor("ident_sb", [128, 128], BF16))
    ident8_sb = es.enter_context(nc.sbuf_tensor("ident8_sb", [128, 128], BF16))
    onesr_sb = es.enter_context(nc.sbuf_tensor("onesr_sb", [1, 128], BF16))
    onesc_sb = es.enter_context(nc.sbuf_tensor("onesc_sb", [128, 1], BF16))
    indA_sb = es.enter_context(nc.sbuf_tensor("indA_sb", [1, 128], F32))
    indB_sb = es.enter_context(nc.sbuf_tensor("indB_sb", [1, 128], F32))
    biasb_sb = es.enter_context(nc.sbuf_tensor("biasb_sb", [1, 2 * D], BF16))
    beff_sb = es.enter_context(nc.sbuf_tensor("beff_sb", [1, D], BF16))
    bvcolf_sb = es.enter_context(nc.sbuf_tensor("bvcolf_sb", [128, 8], F32))
    bvcol_sb = es.enter_context(nc.sbuf_tensor("bvcol_sb", [128, 8], BF16))
    gs_sb = es.enter_context(nc.sbuf_tensor("gs_sb", [128, 16], F32))
    rl_sb = es.enter_context(nc.sbuf_tensor("rl_sb", [1, 1024], F32))

    sems = {}
    cnt = {}
    for n in ("ld", "gld", "mmA", "veA", "seA", "stB", "fin", "ldD", "mmD", "seD", "veD", "stD",
              "ldE", "mmE", "veE", "seE", "stE", "ldF", "mmF", "veF", "seF", "stF", "outS"):
        sems[n] = es.enter_context(nc.semaphore("s_" + n))
        cnt[n] = 0
    ck = {}

    def inc(name, instr, k=1):
        instr.then_inc(sems[name], k)
        cnt[name] += k
        return cnt[name]

    mmD_b, seD_b, veD_b, ldD_b = {}, {}, {}, {}
    mm_c = se_c = ve_c = ld_c = 0
    for qst in range(QSTN):
        cntk = PC[qst]
        ld_c += 16
        ldD_b[("mask", qst)] = ld_c
        for hg in range(8):
            ld_c += 32
            ldD_b[("kv", qst, hg)] = ld_c
            mmD_b[("base", qst, hg)] = mm_c
            mm_c += 2 * cntk
            mmD_b[("av", qst, hg)] = mm_c
            seD_b[("expb", qst, hg)] = se_c
            se_c += cntk
            seD_b[("exp", qst, hg)] = se_c
            veD_b[("mskb", qst, hg)] = ve_c
            ve_c += 8
            veD_b[("msk", qst, hg)] = ve_c
            ve_c += 1
            veD_b[("rcp", qst, hg)] = ve_c
            mm_c += 1
            mmD_b[("rl", qst, hg)] = mm_c
            ve_c += 1
            veD_b[("accbf", qst, hg)] = ve_c
        mm_c += 4
        mmD_b[("wo", qst)] = mm_c
        ve_c += 4
        veD_b[("sev", qst)] = ve_c
    MM_P4_END, SE_P4_END, VE_P4_END, LD_P4_END = mm_c, se_c, ve_c, ld_c
    ST_P4_END = 16 * 4 * QSTN

    NBLK = 8 * QSTN
    P5U = [("A", 0, r) for r in range(32)] + [("chain", 0)]
    bq = [("B", mi, r) for mi in range(3) for r in range(32)]
    bq_i = 0
    for mi_a in (1, 2):
        ar = 0
        while ar < 32:
            P5U.append(("A", mi_a, ar)); ar += 1
            if ar < 32:
                P5U.append(("A", mi_a, ar)); ar += 1
            if bq_i < len(bq) and bq[bq_i][1] != mi_a:
                P5U.append(bq[bq_i]); bq_i += 1
        P5U.append(("chain", mi_a))
    P5U += bq[bq_i:]

    def p5cost(u):
        if u[0] == "A":
            return 1.0
        if u[0] == "chain":
            return 4.0
        return 24.0 if u[2] % 16 == 0 else 17.0

    blkw = [float(PC[qst]) for qst in range(QSTN) for _ in range(8)]
    totw = sum(blkw); totc = sum(p5cost(u) for u in P5U)
    sched = [[] for _ in range(NBLK + 1)]
    bi = 0; used = 0.0; budget = blkw[0] / totw * totc
    nA = nB = 0
    for u in P5U:
        c = p5cost(u)
        while bi < NBLK and (used + c > budget or (u[0] == "A" and nA >= 8)
                             or (u[0] == "B" and nB >= 2)):
            bi += 1
            if bi < NBLK:
                budget += blkw[bi] / totw * totc
                nA = nB = 0
        sched[min(bi, NBLK)].append(u)
        used += c
        if u[0] == "A":
            nA += 1
        elif u[0] == "B":
            nB += 1

    par5 = {}
    lseq = [u for u in P5U if u[0] in ("A", "B")]
    for p, u in enumerate(lseq):
        par5[u] = p % 2
    border = [u for u in P5U if u[0] == "B"]
    bprev = {border[k]: (border[k - 1] if k >= 1 else None) for k in range(len(border))}
    bprev2 = {border[k]: (border[k - 2] if k >= 2 else None) for k in range(len(border))}

    vprog = [[] for _ in range(NBLK + 1)]
    pend = []
    for b in range(NBLK + 1):
        for p in pend:
            vprog[b].append(("ev", p[1], p[2]))
        pend = []
        tail_blk = (b < NBLK and b % 8 == 7)
        for u in sched[b]:
            vprog[b].append(u)
            if u[0] == "B":
                if pend and not tail_blk:
                    for p in pend:
                        vprog[b].append(("ev", p[1], p[2]))
                    pend = []
                pend.append(u)
    for p in pend:
        vprog[NBLK].append(("ev", p[1], p[2]))

    ldE_b, veE_b, mmE_b, seE_b, stE_b = {}, {}, {}, {}, {}
    vc = mc = sc5 = st5 = 0
    for b in range(NBLK + 1):
        for act in vprog[b]:
            vc += 1
            if act[0] == "A":
                veE_b[("A", act[1], act[2])] = vc
            elif act[0] == "chain":
                veE_b[("chain", act[1])] = vc
            elif act[0] == "B":
                veE_b[("q", act[1], act[2])] = vc
            else:
                veE_b[("ev", act[1], act[2])] = vc
    for mi in range(3):
        for r in range(32):
            sc5 += 1; seE_b[("sgn", mi, r)] = sc5
            mc += 1; mmE_b[("tr", mi, r)] = mc
            st5 += 16; stE_b[("st", mi, r)] = st5
    gldseq = []
    for u in P5U:
        if u[0] == "A":
            gldseq.append(u)
        elif u[0] == "chain":
            if u[1] < 2:
                gldseq.append(("spl", u[1]))
        elif u[0] == "B":
            if u[2] % 16 == 0 and u[1] < 2:
                gldseq.append(("rld", u[1], u[2] // 16))
            gldseq.append(u)
    lc = 0
    for it in gldseq:
        lc += 16
        ldE_b[it] = lc
    LDE_END, VEE_END, MME_END, SEE_END, STE_END = lc, vc, mc, sc5, st5
    batches = [(mi, h) for mi in range(3) for h in range(2)]

    stB_blocks = [[a for a in vprog[b] if a[0] == "ev"] for b in range(NBLK)]
    sgn_blocks = [[u for u in sched[b] if u[0] == "B"] for b in range(NBLK)]
    t_blocks = [[u for u in sched[b] if u[0] == "B"] for b in range(NBLK + 1)]
    vsplit = []
    for b in range(NBLK):
        acts = vprog[b]
        i1 = next((i for i, a in enumerate(acts) if a[0] == "B"), None)
        if i1 is None:
            vsplit.append((list(acts), []))
        else:
            vsplit.append((list(acts[:i1]), list(acts[i1:])))

    ldF_b, veF_b, mmF_b, seF_b = {}, {}, {}, {}
    lf = vf = mf = sf = 0
    lf += 16; ldF_b["xqT"] = lf
    for qd in range(4):
        for oc in range(8):
            lf += 32; ldF_b[("gu", qd, oc)] = lf
            for t4 in range(4):
                mf += 1; mmF_b[("g", qd, oc, t4)] = mf
                mf += 1; mmF_b[("u", qd, oc, t4)] = mf
                sf += 1; seF_b[("sil", qd, oc, t4)] = sf
                vf += 1; veF_b[("hid", qd, oc, t4)] = vf
                sf += 1; seF_b[("sgn", qd, oc, t4)] = sf
            vf += 1; veF_b[("dq", qd, oc)] = vf
            for t4 in range(4):
                vf += 1; veF_b[("q", qd, oc, t4)] = vf
                mf += 1; mmF_b[("tr", qd, oc, t4)] = mf
                vf += 1; veF_b[("ev", qd, oc, t4)] = vf
        for pr in range(2):
            for og in range(8):
                lf += 16; ldF_b[("wd", qd, pr, og)] = lf
                for tp in range(2):
                    mf += 1; mmF_b[("dn", qd, pr, og, tp)] = mf
            for tp in range(2):
                sf += 1; seF_b[("g", qd, pr, tp)] = sf
                lf += 32; ldF_b[("sx", qd, pr, tp)] = lf
                vf += 1; veF_b[("cmb", qd, pr, tp)] = vf
    LDF_END, VEF_END, MMF_END, SEF_END = lf, vf, mf, sf

    sgn4s = None
    def mm_sc_val(b, kt):
        return b + (kt + 1 if kt < 2 else 2 * kt)

    def mm_av_val(b, j, cntk):
        return b + (2 * j + 3 if j <= cntk - 2 else 2 * cntk)

    sgn4s = [sgn_buf[:, 0:512], sgn_buf[:, 512:1024], xnf[:, 0:512], xnf[:, 512:1024]]

    def p5_tr(t, mi, r):
        t.wait_ge(sems["veE"], veE_b[("q", mi, r)])
        pb = bprev[("B", mi, r)]
        if pb is not None:
            t.wait_ge(sems["veE"], veE_b[("ev", pb[1], pb[2])])
        xqv = xq_par[r % 2].rearrange("p (a b) -> p a b", a=8)
        for c in range(8):
            ins = t.transpose(psb[:, c * 128:(c + 1) * 128], xqv[:, c, :], ident_sb[:])
        inc("mmE", ins)

    def p5_sgn(sc_, mi, r):
        sc_.wait_ge(sems["ldE"], ldE_b[("B", mi, r)])
        pb = bprev[("B", mi, r)]
        if pb is not None:
            sc_.wait_ge(sems["veE"], veE_b[("q", pb[1], pb[2])])
        wf = wf_a if par5[("B", mi, r)] == 0 else wf_b
        ins = sc_.activation(sgn_buf[:], wf, AF.Sign)
        inc("seE", ins)

    def p5_st(sc_, mi, r):
        sc_.wait_ge(sems["veE"], veE_b[("ev", mi, r)])
        if mi < 2:
            dsp = (wgT_sp if mi == 0 else wuT_sp)[:, :, r * 128:(r + 1) * 128]
        else:
            dsp = wdT_sp[:, (r % 4) * 8:(r % 4 + 1) * 8, (r // 4) * 128:(r // 4 + 1) * 128]
        ins = sc_.dma_start(out=dsp,
                            in_=stage8[:, r % 2, :].rearrange("p (a b) -> p a b", a=8))
        inc("stE", ins, 16)

    blk = es.enter_context(nc.Block())

    @blk.sync
    def _(sy):
        for name, dst in (("ident", ident_sb[:]), ("ident8", ident8_sb[:]),
                          ("ones_row", onesr_sb[:]), ("ones_col", onesc_sb[:]),
                          ("indA", indA_sb[:]), ("indB", indB_sb[:]),
                          ("ident_f", identf_sb[:]), ("ones_row_f", onesrf_sb[:])):
            inc("ld", sy.dma_start(out=dst, in_=d[name]), 16)
        for i, name in enumerate(("bq", "bk", "bv", "bo")):
            inc("ld", sy.dma_start(out=bias_sb[:, i * D:(i + 1) * D], in_=d[name]), 16)
        inc("ld", sy.dma_start(out=bvcolf_sb[:],
                               in_=d["bv"].rearrange("o (a p) -> p o a", p=128).squeeze(1),
                               allow_slow_non_contiguous=True), 16)
        offs = 0
        for name in ("tk_sin_hi", "tk_sin_lo", "tk_cos_hi", "tk_cos_lo"):
            inc("ld", sy.dma_start(
                out=th_sb[:, offs:offs + 1024].rearrange("p (a b) -> p a b", b=32),
                in_=d[name].rearrange("(tl p) i -> p tl i", p=128)), 16)
            offs += 1024
        for name in ("tq_sin_hi", "tq_sin_lo", "tq_cos_hi", "tq_cos_lo"):
            inc("ld", sy.dma_start(
                out=th_sb[:, offs:offs + 512].rearrange("p (a b) -> p a b", b=32),
                in_=d[name].rearrange("(tl p) i -> p tl i", p=128)), 16)
            offs += 512
        inc("ld", sy.dma_start(out=gs_sb[:, 0:1],
                               in_=d["s_in"].partition_broadcast(128).unsqueeze(-1)), 16)
        inc("ld", sy.dma_start(out=gs_sb[:, 1:2],
                               in_=d["s_dn"].partition_broadcast(128).unsqueeze(-1)), 16)
        ck["ld_const"] = cnt["ld"]

        for i in range(NKV):
            if i < 2:
                sy.wait_ge(sems["veA"], VE_P0)
            else:
                sy.wait_ge(sems["veA"], VE_P1(i - 2, 2))
                sy.wait_ge(sems["seA"], SE_P1(i - 2, 1))
            dst = lnx_a if i % 2 == 0 else lnx_b
            inc("ld", sy.dma_start(out=dst, in_=d["x_kv"][i * 128:(i + 1) * 128, :]), 16)
            ck[f"p1_ld_{i}"] = cnt["ld"]
        for i in range(NQT):
            if i < 2:
                sy.wait_ge(sems["veA"], VE_P1(NKV - 2 + i, 2))
                sy.wait_ge(sems["seA"], SE_P1(NKV - 2 + i, 1))
            else:
                sy.wait_ge(sems["veA"], VE_P2(i - 2, 2))
                sy.wait_ge(sems["seA"], SE_P2(i - 2, 1))
            dst = lnx_a if i % 2 == 0 else lnx_b
            inc("ld", sy.dma_start(out=dst, in_=d["x_q"][i * 128:(i + 1) * 128, :]), 16)
            ck[f"p2_ld_{i}"] = cnt["ld"]

        sy.wait_ge(sems["stB"], 16 * 2 * NKV)
        for qst in range(QSTN):
            cntk = PC[qst]
            if qst >= 1:
                sy.wait_ge(sems["veD"], veD_b[("accbf", qst - 1, 7)])
            inc("ldD", sy.dma_start(out=maskbuf,
                                    in_=d["masks"][qst * 8:(qst + 1) * 8].transpose([1, 0, 2])), 16)
            for hg in range(8):
                bi = qst * 8 + hg
                if bi == 0:
                    sy.wait_ge(sems["mmA"], MM_P0)
                elif bi == 1:
                    sy.wait_ge(sems["veA"], VE_P2_END)
                if bi >= 2:
                    p2 = ((bi - 2) // 8, (bi - 2) % 8)
                    sy.wait_ge(sems["mmD"], mmD_b[("av", p2[0], p2[1])])
                par = (bi + 1) % 2
                inc("ldD", sy.dma_start(out=kt_par[par][:, 0:cntk * 128],
                                        in_=kT_sp[:, hg, 0:cntk * 128]), 16)
                inc("ldD", sy.dma_start(out=v_par[par][:, 0:cntk * 128].rearrange("p (a b) -> p a b", b=128),
                                        in_=v_sp[:, 0:cntk, hg * 128:(hg + 1) * 128]), 16)
        sy.wait_ge(sems["stD"], ST_P4_END)
        inc("ldF", sy.dma_start(out=xqT, in_=xqT_sp), 16)
        gu_seq = [(qd, oc) for qd in range(4) for oc in range(8)]
        wd_seq = [(qd, pr, og) for qd in range(4) for pr in range(2) for og in range(8)]
        for qd in range(4):
            for oc in range(8):
                gi = gu_seq.index((qd, oc))
                if gi == 0:
                    sy.wait_ge(sems["stE"], stE_b[("st", 0, 31)])
                    sy.wait_ge(sems["stE"], stE_b[("st", 1, 31)])
                elif gi >= 2:
                    p2 = gu_seq[gi - 2]
                    sy.wait_ge(sems["mmF"], mmF_b[("u", p2[0], p2[1], 3)])
                par = oc % 2
                inc("ldF", sy.dma_start(out=wg_par[par], in_=wgT_sp[:, :, oc * 512:(oc + 1) * 512]), 16)
                inc("ldF", sy.dma_start(out=wu_par[par], in_=wuT_sp[:, :, oc * 512:(oc + 1) * 512]), 16)
            for pr in range(2):
                for og in range(8):
                    wi = wd_seq.index((qd, pr, og))
                    if wi == 0:
                        sy.wait_ge(sems["stE"], stE_b[("st", 2, 31)])
                    elif wi >= 2:
                        pk = wd_seq[wi - 2]
                        sy.wait_ge(sems["mmF"], mmF_b[("dn", pk[0], pk[1], pk[2], 1)])
                    inc("ldF", sy.dma_start(out=wd_par[og % 2],
                                            in_=wdT_sp[:, og * 4:(og + 1) * 4, :]), 16)
                for tp in range(2):
                    j = 4 * qd + 2 * pr + tp
                    if j >= 1:
                        sy.wait_ge(sems["outS"], 16 * j)
                    inc("ldF", sy.dma_start(out=stile2, in_=s_sp[j * 128:(j + 1) * 128, :]), 16)
                    inc("ldF", sy.dma_start(out=xtile2, in_=d["x_q"][j * 128:(j + 1) * 128, :]), 16)
        sy.wait_ge(sems["stB"], 16 * (2 * NKV + NQT))
        if DEBUG:
            inc("ld", sy.dma_start(out=qT_sp, in_=qT), 16)
        sy.wait_ge(sems["outS"], 16 * NQT)

    @blk.gpsimd
    def _(g):
        for mi, m in enumerate(("wq", "wk", "wv", "wo")):
            if mi >= 1:
                g.wait_ge(sems["mmA"], 8 * mi)
            inc("gld", g.dma_start(out=wbig_bf, in_=d[m].rearrange("(r p) h -> p r h", p=128)), 16)
            ck[f"p0_gld_w_{mi}"] = cnt["gld"]

        def wslice(mi, r):
            mn = ("gw", "uw", "dw")[mi]
            if mn != "dw":
                return d[mn][r * 128:(r + 1) * 128, :]
            return d["dw"][(r // 4) * 128:(r // 4 + 1) * 128, (r % 4) * 1024:(r % 4 + 1) * 1024]
        g.wait_ge(sems["veA"], VE_P2_END)
        lpos = 0
        for it in gldseq:
            if it[0] == "spl":
                g.wait_ge(sems["veE"], veE_b[("chain", it[1])])
                inc("ldE", g.dma_start(out=amax_sp[:, it[1] * 2048:(it[1] + 1) * 2048],
                                       in_=amaxc[:]), 16)
            elif it[0] == "rld":
                kb = batches.index((it[1], it[2]))
                if kb >= 1:
                    pmi, ph = batches[kb - 1]
                    g.wait_ge(sems["veE"], veE_b[("q", pmi, ph * 16)])
                g.wait_ge(sems["ldE"], ldE_b[("spl", it[1])])
                off = it[1] * 2048 + it[2] * 1024
                inc("ldE", g.dma_start(out=dqf, in_=amax_sp[:, off:off + 1024]), 16)
            else:
                if lpos >= 2:
                    pu = lseq[lpos - 2]
                    if pu[0] == "A":
                        g.wait_ge(sems["veE"], veE_b[("A", pu[1], pu[2])])
                    else:
                        g.wait_ge(sems["veE"], veE_b[("q", pu[1], pu[2])])
                        g.wait_ge(sems["seE"], seE_b[("sgn", pu[1], pu[2])])
                dst = wf_a if lpos % 2 == 0 else wf_b
                inc("ldE", g.dma_start(out=dst, in_=wslice(it[1], it[2])), 16)
                lpos += 1

    @blk.tensor
    def _(t):
        t.wait_ge(sems["ld"], ck["ld_const"])
        for mi, m in enumerate(("wq", "wk", "wv", "wo")):
            t.wait_ge(sems["gld"], ck[f"p0_gld_w_{mi}"])
            idn = ident_sb
            for r in range(8):
                if mi * 8 + r >= 1:
                    t.wait_ge(sems["veA"], mi * 8 + r)
                for c in range(8):
                    ins = t.transpose(psb[:, c * 128:(c + 1) * 128],
                                      wbig_bf[:, r, c * 128:(c + 1) * 128], idn[:])
                inc("mmA", ins)
        t.wait_ge(sems["veA"], 33)
        for ch in range(2):
            for kt in range(8):
                ins = t.matmul(psf[0:1, 2048 + ch * 512: 2048 + (ch + 1) * 512],
                               bvcol_sb[:, kt:kt + 1], wT["wo"][:, kt, ch * 512:(ch + 1) * 512],
                               start=(kt == 0), stop=(kt == 7))
            inc("mmA", ins)
        assert cnt["mmA"] == MM_P0

        for i in range(NKV):
            xtv = (xtile_a if i % 2 == 0 else xtile_b).rearrange("p (a b) -> p a b", a=8)
            t.wait_ge(sems["veA"], VE_P1(i, 2))
            for c in range(8):
                ins = t.transpose(psb[:, c * 128:(c + 1) * 128], xtv[:, c, :], ident_sb[:])
            inc("mmA", ins)
            t.wait_ge(sems["veA"], VE_P1(i, 3))
            xnT = stage8[:, 0, :].rearrange("p (a b) -> p a b", a=8)
            for ch in range(2):
                for kt in range(8):
                    t.matmul(psf[:, ch * 512:(ch + 1) * 512], xnT[:, kt, :],
                             wT["wk"][:, kt, ch * 512:(ch + 1) * 512],
                             start=(kt == 0), stop=False)
                ins = t.matmul(psf[:, ch * 512:(ch + 1) * 512], onesr_sb[:],
                               biasb_sb[:, D + ch * 512: D + (ch + 1) * 512],
                               start=False, stop=True)
            inc("mmA", ins)
            for ch in range(2):
                for kt in range(8):
                    ins = t.matmul(psf[:, 1024 + ch * 512: 1024 + (ch + 1) * 512], xnT[:, kt, :],
                                   wT["wv"][:, kt, ch * 512:(ch + 1) * 512],
                                   start=(kt == 0), stop=(kt == 7))
            inc("mmA", ins)
            t.wait_ge(sems["veA"], VE_P1(i, 4))
            ktv = (xtile_a if i % 2 == 0 else xtile_b).rearrange("p (a b) -> p a b", a=8)
            for c in range(8):
                ins = t.transpose(psb[:, c * 128:(c + 1) * 128], ktv[:, c, :], ident_sb[:])
            inc("mmA", ins)
        assert cnt["mmA"] == MM_P1_END

        for i in range(NQT):
            xtv = (xtile_a if i % 2 == 0 else xtile_b).rearrange("p (a b) -> p a b", a=8)
            t.wait_ge(sems["veA"], VE_P2(i, 2))
            for c in range(8):
                ins = t.transpose(psb[:, c * 128:(c + 1) * 128], xtv[:, c, :], ident_sb[:])
            inc("mmA", ins)
            t.wait_ge(sems["veA"], VE_P2(i, 3))
            xnT = stage8[:, 0, :].rearrange("p (a b) -> p a b", a=8)
            for ch in range(2):
                for kt in range(8):
                    t.matmul(psf[:, ch * 512:(ch + 1) * 512], xnT[:, kt, :],
                             wT["wq"][:, kt, ch * 512:(ch + 1) * 512],
                             start=(kt == 0), stop=False)
                ins = t.matmul(psf[:, ch * 512:(ch + 1) * 512], onesr_sb[:],
                               biasb_sb[:, ch * 512:(ch + 1) * 512], start=False, stop=True)
            inc("mmA", ins)
            t.wait_ge(sems["veA"], VE_P2(i, 4))
            qtv = (xtile_a if i % 2 == 0 else xtile_b).rearrange("p (a b) -> p a b", a=8)
            for c in range(8):
                ins = t.transpose(psb[:, c * 128:(c + 1) * 128], qtv[:, c, :], ident_sb[:])
            inc("mmA", ins)
            t.wait_ge(sems["veA"], VE_P2(i, 6))
            xqv = xq_stage.rearrange("p (a b) -> p a b", a=8)
            for c in range(8):
                ins = t.transpose(psb[:, c * 128:(c + 1) * 128], xqv[:, c, :], ident_sb[:])
            inc("mmA", ins)
        assert cnt["mmA"] == MM_P2_END

        t.wait_ge(sems["veA"], VE_P2_END)
        for qst in range(QSTN):
            cntk = PC[qst]
            for hg in range(8):
                t.wait_ge(sems["ldD"], ldD_b[("kv", qst, hg)])
                if qst >= 1 or hg >= 1:
                    phg = (qst, hg - 1) if hg >= 1 else (qst - 1, 7)
                    t.wait_ge(sems["veD"], veD_b[("accbf", phg[0], phg[1])])
                if hg == 0 and qst >= 1:
                    t.wait_ge(sems["veD"], veD_b[("sev", qst - 1)])
                qA = qT[0:64, hg, qst * 512:(qst + 1) * 512]
                qB = qT[64:128, hg, qst * 512:(qst + 1) * 512]
                nmask_start = cntk - 8
                eb = seD_b[("expb", qst, hg)]
                mb = veD_b[("mskb", qst, hg)]
                par4 = (qst * 8 + hg + 1) % 2
                ktall = kt_par[par4]
                vt = v_par[par4][:, 0:cntk * 128].rearrange("p (a b) -> p a b", b=128)

                def emit_sc(kt):
                    buf = kt % 2
                    off = buf * 1024
                    if kt >= 2:
                        t.wait_ge(sems["seD"], eb + kt - 1)
                    ktile = ktall[:, kt * 128:(kt + 1) * 128]
                    t.matmul(psf[:, off:off + 512], ktile[0:64, :], qA,
                             start=True, stop=True, tile_position=(0, 0))
                    ins = t.matmul(psf[:, off + 512:off + 1024], ktile[64:128, :], qB,
                                   start=True, stop=True, tile_position=(64, 0))
                    inc("mmD", ins)

                def emit_av(kt):
                    buf = kt % 2
                    t.wait_ge(sems["seD"], eb + kt + 1)
                    if kt >= nmask_start:
                        t.wait_ge(sems["veD"], mb + (kt - nmask_start) + 1)
                    ex = expTb[:, buf, :]
                    t.matmul(psf[0:1, 2560:3072], onesc_sb[:], ex[:, 0:512],
                             start=(kt == 0), stop=(kt == cntk - 1))
                    t.matmul(psf[0:1, 3072:3584], onesc_sb[:], ex[:, 512:1024],
                             start=(kt == 0), stop=(kt == cntk - 1))
                    t.matmul(psf[0:64, 2048:2560], vt[:, kt, 0:64], ex[:, 0:512],
                             start=(kt == 0), stop=(kt == cntk - 1), tile_position=(0, 0),
                             skip_group_check=True)
                    ins = t.matmul(psf[64:128, 2048:2560], vt[:, kt, 64:128], ex[:, 512:1024],
                                   start=(kt == 0), stop=(kt == cntk - 1), tile_position=(0, 64),
                                   skip_group_check=True)
                    inc("mmD", ins)

                emit_sc(0)
                emit_sc(1)
                for kt in range(2, cntk):
                    emit_av(kt - 2)
                    emit_sc(kt)
                emit_av(cntk - 2)
                emit_av(cntk - 1)
                t.wait_ge(sems["veD"], veD_b[("rcp", qst, hg)])
                t.matmul(psf[:, 1024:1536], indA_sb[:], rl_sb[0:1, 0:512], start=True, stop=False)
                ins = t.matmul(psf[:, 1024:1536], indB_sb[:], rl_sb[0:1, 512:1024], start=False, stop=True)
                inc("mmD", ins)
                if hg < 7:
                    for u in t_blocks[qst * 8 + hg]:
                        p5_tr(t, u[1], u[2])
            t.wait_ge(sems["veD"], veD_b[("accbf", qst, 7)])
            for tq in range(4):
                if qst > 0 or tq > 0:
                    t.wait_ge(sems["veD"], veD_b[("sev", qst)] - 4 + tq if tq > 0
                              else veD_b[("sev", qst - 1)])
                for ch in range(2):
                    for et in range(8):
                        t.matmul(psf[:, ch * 512:(ch + 1) * 512],
                                 accT[:, et, tq * 128:(tq + 1) * 128],
                                 wT["wo"][:, et, ch * 512:(ch + 1) * 512],
                                 start=(et == 0), stop=False)
                    ins = t.matmul(psf[:, ch * 512:(ch + 1) * 512], onesr_sb[:],
                                   beff_sb[0:1, ch * 512:(ch + 1) * 512],
                                   start=False, stop=True)
                inc("mmD", ins)
            for u in t_blocks[qst * 8 + 7]:
                p5_tr(t, u[1], u[2])
        assert cnt["mmD"] == MM_P4_END

        for u in t_blocks[NBLK]:
            p5_tr(t, u[1], u[2])
        assert cnt["mmE"] == MME_END

        t.wait_ge(sems["ldF"], ldF_b["xqT"])
        for qd in range(4):
            for oc in range(8):
                t.wait_ge(sems["ldF"], ldF_b[("gu", qd, oc)])
                par = oc % 2
                for t4 in range(4):
                    j = 4 * qd + t4
                    gch = (qd * 8 + oc) * 4 + t4
                    if gch >= 2:
                        g2 = gch - 2
                        pk = (g2 // 32, (g2 // 4) % 8, g2 % 4)
                        t.wait_ge(sems["seF"], seF_b[("sil", pk[0], pk[1], pk[2])])
                        t.wait_ge(sems["veF"], veF_b[("hid", pk[0], pk[1], pk[2])])
                    if qd > 0 and oc == 0 and t4 == 1:
                        t.wait_ge(sems["seF"], seF_b[("g", qd - 1, 1, 0)])
                    gb = 1024 * (t4 % 2)
                    for kt in range(8):
                        ins = t.matmul(psf[:, gb:gb + 512], xqT[:, kt, j * 128:(j + 1) * 128],
                                       wg_par[par][:, kt, :], start=(kt == 0), stop=(kt == 7))
                    inc("mmF", ins)
                    for kt in range(8):
                        ins = t.matmul(psf[:, gb + 512:gb + 1024], xqT[:, kt, j * 128:(j + 1) * 128],
                                       wu_par[par][:, kt, :], start=(kt == 0), stop=(kt == 7))
                    inc("mmF", ins)
                for t4 in range(4):
                    t.wait_ge(sems["veF"], veF_b[("q", qd, oc, t4)])
                    hqv = hqstage.rearrange("p (a b) -> p a b", a=4)
                    for c in range(4):
                        ins = t.transpose(psb[:, c * 128:(c + 1) * 128], hqv[:, c, :], ident_sb[:])
                    inc("mmF", ins)
            for pr in range(2):
                for og in range(8):
                    t.wait_ge(sems["ldF"], ldF_b[("wd", qd, pr, og)])
                    for tp in range(2):
                        t4 = 2 * pr + tp
                        j = 4 * qd + t4
                        if og == 0:
                            t.wait_ge(sems["veF"], veF_b[("ev", qd, 7, t4)])
                            ppr = (qd, pr - 1) if pr > 0 else ((qd - 1, 1) if qd > 0 else None)
                            if ppr is not None:
                                t.wait_ge(sems["seF"], seF_b[("g", ppr[0], ppr[1], tp)])
                        base = 1024 + 1024 * tp
                        for ot in range(4):
                            go = og * 4 + ot
                            for ch in range(2):
                                ins = t.matmul(psf[:, base + ch * 512:base + (ch + 1) * 512],
                                               hqT_t[t4][:, go, :],
                                               wd_par[og % 2][:, ot, ch * 512:(ch + 1) * 512],
                                               start=(go == 0), stop=(go == 31))
                        inc("mmF", ins)
        assert cnt["mmF"] == MMF_END

    def emit_ln(v, se_sq, se_sqrt, lnx, out_bf, want_f32):
        v.tensor_reduce(scol(0), lnx, AX.X, ALU.add)
        v.wait_ge(sems["seA"], se_sq)
        v.tensor_reduce(scol(1), sqt[:], AX.X, ALU.add)
        v.drain()
        v.tensor_single_scalar(scol(2), scol(0), INV1024, ALU.mult)
        v.tensor_single_scalar(scol(4), scol(1), INV1024, ALU.mult)
        v.drain()
        v.tensor_scalar(sqt[:], lnx, scol(2), None, ALU.subtract)
        v.tensor_tensor(scol(3), scol(2), scol(2), ALU.mult)
        v.drain()
        v.scalar_tensor_tensor(scol(5), scol(3), -1.0, scol(4), ALU.mult, ALU.add)
        v.drain()
        v.tensor_single_scalar(scol(6), scol(5), EPS, ALU.add)
        v.drain()
        ins = v.reciprocal(scol(7), scol(6))
        inc("veA", ins)
        v.wait_ge(sems["seA"], se_sqrt)
        if want_f32:
            v.tensor_scalar(xnf[:], sqt[:], scol(8), None, ALU.mult)
        ins = v.tensor_scalar(out_bf, sqt[:], scol(8), None, ALU.mult)
        inc("veA", ins)

    def emit_rope(v, psrc_off, tabs, tabc, tabi, out_bf):
        v.tensor_copy(ropet[:, 0, :].rearrange("p (a b) -> p a b", b=32),
                      tabc[:, tabi, :].unsqueeze(1).broadcast_to([128, 16, 32]))
        v.tensor_copy(ropet[:, 1, :].rearrange("p (a b) -> p a b", b=32),
                      tabs[:, tabi, :].unsqueeze(1).broadcast_to([128, 16, 32]))
        v.tensor_copy(evacf[:], psf[:, psrc_off:psrc_off + 1024])
        v.drain()
        ev = evacf[:].rearrange("p (a two) -> p a two", two=2)
        e1 = ev[:, :, 0]; e2 = ev[:, :, 1]
        ob = out_bf.rearrange("p (a two) -> p a two", two=2)
        v.tensor_tensor(ropet[:, 2, :], e1, ropet[:, 0, :], ALU.mult)
        v.tensor_tensor(ropet[:, 3, :], e2, ropet[:, 1, :], ALU.mult)
        v.tensor_tensor(ropet[:, 4, :], e1, ropet[:, 1, :], ALU.mult)
        v.tensor_tensor(ropet[:, 5, :], e2, ropet[:, 0, :], ALU.mult)
        v.scalar_tensor_tensor(ob[:, :, 0], ropet[:, 3, :], -1.0, ropet[:, 2, :], ALU.mult, ALU.add)
        ins = v.scalar_tensor_tensor(ob[:, :, 1], ropet[:, 4, :], 1.0, ropet[:, 5, :], ALU.mult, ALU.add)
        return ins


    def emit_dq_chain(v, jobs):
        def both(f):
            for j in jobs:
                f(j)
        both(lambda j: v.tensor_scalar(j["sc"], j["am"], j["gs6"], 448.0, ALU.mult, ALU.min))
        both(lambda j: v.memset(j["pe"], 2.0 ** -6))
        for mlev in (8, 4, 2, 1):
            both(lambda j: v.tensor_single_scalar(j["th"], j["pe"], float(2.0 ** mlev), ALU.mult))
            both(lambda j: v.tensor_tensor(j["cc"], j["sc"], j["th"], ALU.is_ge))
            both(lambda j: v.tensor_scalar(j["fac"], j["cc"], float(2.0 ** mlev - 1), 1.0,
                                           ALU.mult, ALU.add))
            both(lambda j: v.tensor_tensor(j["pe"], j["pe"], j["fac"], ALU.mult))
        both(lambda j: v.reciprocal(j["th"], j["pe"]))
        both(lambda j: v.tensor_tensor(j["rr"], j["sc"], j["th"], ALU.mult))
        both(lambda j: v.tensor_single_scalar(j["f16"], j["rr"], 1.0 / 16.0, ALU.is_ge))
        both(lambda j: v.tensor_single_scalar(j["cc"], j["rr"], 3.0 / 16.0, ALU.is_ge))
        for jj in range(2, 16):
            tgt = "f16" if jj % 2 == 0 else "cc"
            both(lambda j, tgt=tgt, jj=jj: v.scalar_tensor_tensor(
                j[tgt], j["rr"], float((2 * jj + 1) / 16.0), j[tgt], ALU.is_ge, ALU.add))
        both(lambda j: v.tensor_tensor(j["f16"], j["f16"], j["cc"], ALU.add))
        both(lambda j: v.tensor_single_scalar(j["th"], j["pe"], 0.125, ALU.mult))
        both(lambda j: v.tensor_tensor(j["sc"], j["f16"], j["th"], ALU.mult))
        both(lambda j: v.tensor_scalar(j["am2"], j["sc"], j["invgs"], None, ALU.mult))
        both(lambda j: v.reciprocal(j["th"], j["am2"]))
        both(lambda j: v.tensor_single_scalar(j["rr"], j["am2"], 0.5, ALU.mult))
        ins = None
        for j in jobs:
            ins = v.tensor_single_scalar(j["th"], j["th"], 1e30, ALU.min)
        return ins

    def emit_main_chain(v, jobs, sign_wait):
        def both(f):
            for j in jobs:
                f(j)
        both(lambda j: v.tensor_tensor(j["a0"].rearrange("p (a b) -> p a b", b=16),
                                       j["src"].rearrange("p (a b) -> p a b", b=16),
                                       j["th"].unsqueeze(-1).broadcast_to(
                                           [128, j["nb"], 16]), ALU.mult))
        both(lambda j: v.scalar_tensor_tensor(j["ax"], j["a0"], -1.0, j["a0"], ALU.mult, ALU.max))
        both(lambda j: v.tensor_single_scalar(j["ca"], j["ax"], 0.25, ALU.is_ge))
        both(lambda j: v.scalar_tensor_tensor(j["ca"], j["ax"], 0.75, j["ca"], ALU.is_ge, ALU.add))
        both(lambda j: v.scalar_tensor_tensor(j["ca"], j["ax"], 1.25, j["ca"], ALU.is_ge, ALU.add))
        both(lambda j: v.scalar_tensor_tensor(j["ca"], j["ax"], 1.75, j["ca"], ALU.is_ge, ALU.add))
        both(lambda j: v.tensor_single_scalar(j["cb"], j["ax"], 2.5, ALU.is_ge))
        both(lambda j: v.scalar_tensor_tensor(j["cb"], j["ax"], 3.5, j["cb"], ALU.is_ge, ALU.add))
        both(lambda j: v.scalar_tensor_tensor(j["ca"], j["cb"], 2.0, j["ca"], ALU.mult, ALU.add))
        both(lambda j: v.tensor_single_scalar(j["cb"], j["ax"], 5.0, ALU.is_ge))
        both(lambda j: v.scalar_tensor_tensor(j["ca"], j["cb"], 4.0, j["ca"], ALU.mult, ALU.add))
        both(lambda j: v.tensor_tensor(j["ca"].rearrange("p (a b) -> p a b", b=16),
                                       j["ca"].rearrange("p (a b) -> p a b", b=16),
                                       j["rr"].unsqueeze(-1).broadcast_to(
                                           [128, j["nb"], 16]), ALU.mult))
        sign_wait()
        ins = None
        for j in jobs:
            ins = v.tensor_tensor(j["out"], j["ca"], j["sgn"], ALU.mult)
        return ins

    p5rows_s = dict(sc=sqt, pe=xnf, th=stile2, cc=xtile2, fac=gtile2, rr=cmbt, f16=evacf)

    def p5_vact(v, act):
        kind = act[0]
        if kind == "A":
            _, mi, r = act
            v.wait_ge(sems["ldE"], ldE_b[("A", mi, r)])
            if r == 0 and mi > 0:
                v.wait_ge(sems["ldE"], ldE_b[("spl", mi - 1)])
            wf = wf_a if par5[act] == 0 else wf_b
            ins = v.tensor_reduce(amaxc[:, r * 64:(r + 1) * 64],
                                  wf.rearrange("p (a b) -> p a b", b=16),
                                  AX.X, ALU.max, apply_absolute_value=True)
            inc("veE", ins)
        elif kind == "chain":
            mi = act[1]
            v.drain()
            v.tensor_reduce(sqt[:, 0:1], amaxc[:], AX.X, ALU.max)
            v.drain()
            v.transpose(sqt[:, 32:64], sqt[:, 0:32])
            v.drain()
            v.tensor_reduce(sqt[:, 64:65], sqt[:, 32:64], AX.X, ALU.max)
            v.drain()
            v.stream_shuffle(sqt[0:32, 65:66], sqt[0:32, 64:65], [0] * 32)
            v.stream_shuffle(sqt[0:32, 66:67], sqt[32:64, 64:65], [0] * 32)
            v.stream_shuffle(sqt[0:32, 67:68], sqt[64:96, 64:65], [0] * 32)
            v.stream_shuffle(sqt[0:32, 68:69], sqt[96:128, 64:65], [0] * 32)
            v.drain()
            v.tensor_reduce(sqt[:, 69:70], sqt[:, 65:69], AX.X, ALU.max)
            v.drain()
            v.stream_shuffle(sqt[32:64, 69:70], sqt[0:32, 69:70], list(range(32)))
            v.stream_shuffle(sqt[64:96, 69:70], sqt[0:32, 69:70], list(range(32)))
            v.stream_shuffle(sqt[96:128, 69:70], sqt[0:32, 69:70], list(range(32)))
            v.drain()
            v.reciprocal(sqt[:, 70:71], sqt[:, 69:70])
            v.drain()
            v.tensor_single_scalar(gs_sb[:, 6 + 2 * mi:7 + 2 * mi], sqt[:, 70:71],
                                   448.0, ALU.mult)
            ins = v.tensor_single_scalar(gs_sb[:, 7 + 2 * mi:8 + 2 * mi], sqt[:, 69:70],
                                         float(1.0 / 2688.0), ALU.mult)
            inc("veE", ins)
        elif kind == "B":
            _, mi, r = act
            if r % 16 == 0:
                half = r // 16
                if mi < 2:
                    v.wait_ge(sems["ldE"], ldE_b[("rld", mi, half)])
                jobs = []
                for ji in range(2):
                    co = ji * 512
                    jd = {k: ap[:, co:co + 512] for k, ap in p5rows_s.items()}
                    jd["am"] = (dqf[:, co:co + 512] if mi < 2
                                else amaxc[:, half * 1024 + co: half * 1024 + co + 512])
                    jd["am2"] = jd["fac"]
                    jd["gs6"] = gs_sb[:, 6 + 2 * mi:7 + 2 * mi]
                    jd["invgs"] = gs_sb[:, 7 + 2 * mi:8 + 2 * mi]
                    jobs.append(jd)
                emit_dq_chain(v, jobs)
            v.wait_ge(sems["ldE"], ldE_b[("B", mi, r)])
            p2 = bprev2[("B", mi, r)]
            if p2 is not None:
                v.wait_ge(sems["mmE"], mmE_b[("tr", p2[1], p2[2])])
            wf = wf_a if par5[act] == 0 else wf_b
            off = (r % 16) * 64
            mjobs = []
            for ji in range(2):
                sl = slice(ji * 512, (ji + 1) * 512)
                mjobs.append(dict(
                    src=wf[:, sl], out=xq_par[r % 2][:, sl], sgn=sgn_buf[:, sl],
                    th=p5rows_s["th"][:, off + ji * 32: off + (ji + 1) * 32],
                    rr=p5rows_s["rr"][:, off + ji * 32: off + (ji + 1) * 32], nb=32,
                    a0=qs[0][:, sl], ax=qs[1][:, sl], ca=qs[2][:, sl], cb=qs[3][:, sl],
                ))

            def _sw(mi=mi, r=r):
                v.wait_ge(sems["seE"], seE_b[("sgn", mi, r)])
            ins = emit_main_chain(v, mjobs, _sw)
            inc("veE", ins)
        else:
            _, mi, r = act
            v.wait_ge(sems["mmE"], mmE_b[("tr", mi, r)])
            p2 = bprev2[("B", mi, r)]
            if p2 is not None:
                v.wait_ge(sems["stE"], stE_b[("st", p2[1], p2[2])])
            ins = v.tensor_copy(stage8[:, r % 2, :], psb[:])
            inc("veE", ins)

    def emit_quant(v, src, out_bf, W, gs6_col, invgs_col, sign_wait, amax_ap=None,
                   sgn_ap=None):
        if sgn_ap is None:
            sgn_ap = sgn_buf[:, 0:W]
        Wh = W // 2
        nb = Wh // 16
        jobs = []
        for ji in range(2):
            co = ji * 64
            sl = slice(ji * Wh, (ji + 1) * Wh)
            qsl = slice(ji * Wh, (ji + 1) * Wh) if W <= 1024 else None
            jobs.append(dict(
                src=src[:, sl], out=out_bf[:, sl], sgn=sgn_ap[:, sl],
                am=(dqsc[:, 0, co:co + nb] if amax_ap is None else amax_ap[:, ji * nb:(ji + 1) * nb]),
                amr=dqsc[:, 0, co:co + nb],
                sc=dqsc[:, 1, co:co + nb], pe=dqsc[:, 2, co:co + nb],
                th=dqsc[:, 3, co:co + nb], cc=dqsc[:, 4, co:co + nb],
                fac=dqsc[:, 5, co:co + nb], rr=dqsc[:, 6, co:co + nb],
                f16=dqsc[:, 7, co:co + nb],
                a0=qs[0][:, sl], ax=qs[1][:, sl], ca=qs[2][:, sl], cb=qs[3][:, sl],
            ))

        def both(f):
            for j in jobs:
                f(j)

        if amax_ap is None:
            both(lambda j: v.tensor_reduce(j["am"], j["src"].rearrange("p (a b) -> p a b", b=16),
                                           AX.X, ALU.max, apply_absolute_value=True))
        both(lambda j: v.tensor_scalar(j["sc"], j["am"], gs6_col, 448.0, ALU.mult, ALU.min))
        both(lambda j: v.memset(j["pe"], 2.0 ** -6))
        for mlev in (8, 4, 2, 1):
            both(lambda j: v.tensor_single_scalar(j["th"], j["pe"], float(2.0 ** mlev), ALU.mult))
            both(lambda j: v.tensor_tensor(j["cc"], j["sc"], j["th"], ALU.is_ge))
            both(lambda j: v.tensor_scalar(j["fac"], j["cc"], float(2.0 ** mlev - 1), 1.0,
                                           ALU.mult, ALU.add))
            both(lambda j: v.tensor_tensor(j["pe"], j["pe"], j["fac"], ALU.mult))
        both(lambda j: v.reciprocal(j["th"], j["pe"]))
        both(lambda j: v.tensor_tensor(j["rr"], j["sc"], j["th"], ALU.mult))
        both(lambda j: v.tensor_single_scalar(j["f16"], j["rr"], 1.0 / 16.0, ALU.is_ge))
        both(lambda j: v.tensor_single_scalar(j["cc"], j["rr"], 3.0 / 16.0, ALU.is_ge))
        for jj in range(2, 16):
            tgt = "f16" if jj % 2 == 0 else "cc"
            both(lambda j, tgt=tgt, jj=jj: v.scalar_tensor_tensor(
                j[tgt], j["rr"], float((2 * jj + 1) / 16.0), j[tgt], ALU.is_ge, ALU.add))
        both(lambda j: v.tensor_tensor(j["f16"], j["f16"], j["cc"], ALU.add))
        both(lambda j: v.tensor_single_scalar(j["th"], j["pe"], 0.125, ALU.mult))
        both(lambda j: v.tensor_tensor(j["sc"], j["f16"], j["th"], ALU.mult))
        both(lambda j: v.tensor_scalar(j["amr"], j["sc"], invgs_col, None, ALU.mult))
        both(lambda j: v.reciprocal(j["th"], j["amr"]))
        both(lambda j: v.tensor_single_scalar(j["rr"], j["amr"], 0.5, ALU.mult))
        both(lambda j: v.tensor_single_scalar(j["th"], j["th"], 1e30, ALU.min))
        both(lambda j: v.tensor_tensor(j["a0"].rearrange("p (a b) -> p a b", b=16),
                                       j["src"].rearrange("p (a b) -> p a b", b=16),
                                       j["th"].unsqueeze(-1).broadcast_to([128, nb, 16]),
                                       ALU.mult))
        both(lambda j: v.scalar_tensor_tensor(j["ax"], j["a0"], -1.0, j["a0"], ALU.mult, ALU.max))
        both(lambda j: v.tensor_single_scalar(j["ca"], j["ax"], 0.25, ALU.is_ge))
        both(lambda j: v.scalar_tensor_tensor(j["ca"], j["ax"], 0.75, j["ca"], ALU.is_ge, ALU.add))
        both(lambda j: v.scalar_tensor_tensor(j["ca"], j["ax"], 1.25, j["ca"], ALU.is_ge, ALU.add))
        both(lambda j: v.scalar_tensor_tensor(j["ca"], j["ax"], 1.75, j["ca"], ALU.is_ge, ALU.add))
        both(lambda j: v.tensor_single_scalar(j["cb"], j["ax"], 2.5, ALU.is_ge))
        both(lambda j: v.scalar_tensor_tensor(j["cb"], j["ax"], 3.5, j["cb"], ALU.is_ge, ALU.add))
        both(lambda j: v.scalar_tensor_tensor(j["ca"], j["cb"], 2.0, j["ca"], ALU.mult, ALU.add))
        both(lambda j: v.tensor_single_scalar(j["cb"], j["ax"], 5.0, ALU.is_ge))
        both(lambda j: v.scalar_tensor_tensor(j["ca"], j["cb"], 4.0, j["ca"], ALU.mult, ALU.add))
        both(lambda j: v.tensor_tensor(j["ca"].rearrange("p (a b) -> p a b", b=16),
                                       j["ca"].rearrange("p (a b) -> p a b", b=16),
                                       j["rr"].unsqueeze(-1).broadcast_to([128, nb, 16]),
                                       ALU.mult))
        sign_wait()
        v.tensor_tensor(jobs[0]["out"], jobs[0]["ca"], jobs[0]["sgn"], ALU.mult)
        ins = v.tensor_tensor(jobs[1]["out"], jobs[1]["ca"], jobs[1]["sgn"], ALU.mult)
        return ins

    @blk.vector
    def _(v):
        v.wait_ge(sems["ld"], ck["ld_const"])
        for mi in range(4):
            for r in range(8):
                v.wait_ge(sems["mmA"], mi * 8 + r + 1)
                ins = v.tensor_copy(wT[("wq", "wk", "wv", "wo")[mi]][:, :, r * 128:(r + 1) * 128],
                                    psb[:].rearrange("p (a b) -> p a b", a=8))
                inc("veA", ins)
        v.tensor_copy(biasb_sb[:, 0:D], bias_sb[:, 0:D])
        v.tensor_copy(biasb_sb[:, D:2 * D], bias_sb[:, D:2 * D])
        v.reciprocal(gs_sb[:, 3:4], gs_sb[:, 0:1])
        v.reciprocal(gs_sb[:, 5:6], gs_sb[:, 1:2])
        ins = v.tensor_copy(bvcol_sb[:], bvcolf_sb[:])
        inc("veA", ins)
        v.wait_ge(sems["mmA"], MM_P0)
        ins = v.tensor_tensor(beff_sb[:], psf[0:1, 2048:3072], bias_sb[:, 3 * D:4 * D], ALU.add)
        inc("veA", ins)
        v.tensor_tensor(tab_ks[:].rearrange("p a b -> p (a b)"),
                        th_sb[:, 0:1024], th_sb[:, 1024:2048], ALU.add)
        v.tensor_tensor(tab_kc[:].rearrange("p a b -> p (a b)"),
                        th_sb[:, 2048:3072], th_sb[:, 3072:4096], ALU.add)
        v.tensor_tensor(tab_qs[:].rearrange("p a b -> p (a b)"),
                        th_sb[:, 4096:4608], th_sb[:, 4608:5120], ALU.add)
        ins = v.tensor_tensor(tab_qc[:].rearrange("p a b -> p (a b)"),
                              th_sb[:, 5120:5632], th_sb[:, 5632:6144], ALU.add)
        inc("veA", ins)
        assert cnt["veA"] == VE_P0

        for i in range(NKV):
            lnx = lnx_a if i % 2 == 0 else lnx_b
            xt = xtile_a if i % 2 == 0 else xtile_b
            v.wait_ge(sems["ld"], ck[f"p1_ld_{i}"])
            v.wait_ge(sems["seA"], 1)
            emit_ln(v, SE_P1(i, 1), SE_P1(i, 2), lnx, xt[:], want_f32=False)
            v.wait_ge(sems["mmA"], MM_P1(i, 1))
            v.wait_ge(sems["stB"], 32 * i)
            ins = v.tensor_copy(stage8[:, 0, :], psb[:])
            inc("veA", ins)
            v.wait_ge(sems["mmA"], MM_P1(i, 2))
            ins = emit_rope(v, 0, tab_ks, tab_kc, i, xt[:])
            inc("veA", ins)
            v.wait_ge(sems["mmA"], MM_P1(i, 3))
            v.tensor_copy(stage8[:, 1, :], psf[:, 1024:2048])
            v.wait_ge(sems["mmA"], MM_P1(i, 4))
            ins = v.tensor_copy(stage8[:, 0, :], psb[:])
            inc("veA", ins)
        assert cnt["veA"] == VE_P1_END

        for i in range(NQT):
            lnx = lnx_a if i % 2 == 0 else lnx_b
            xt = xtile_a if i % 2 == 0 else xtile_b
            v.wait_ge(sems["ld"], ck[f"p2_ld_{i}"])
            emit_ln(v, SE_P2(i, 1), SE_P2(i, 2), lnx, xt[:], want_f32=True)
            v.wait_ge(sems["mmA"], MM_P2(i, 1))
            v.wait_ge(sems["stB"], ST_P1_END + 16 * i)
            ins = v.tensor_copy(stage8[:, 0, :], psb[:])
            inc("veA", ins)
            v.wait_ge(sems["mmA"], MM_P2(i, 2))
            ins = emit_rope(v, 0, tab_qs, tab_qc, i, xt[:])
            inc("veA", ins)
            v.wait_ge(sems["mmA"], MM_P2(i, 3))
            ins = v.tensor_copy(qT[:, :, i * 128:(i + 1) * 128], psb[:].rearrange("p (a b) -> p a b", a=8))
            inc("veA", ins)
            def _sw(i=i):
                v.wait_ge(sems["seA"], SE_P2(i, 3))
            ins = emit_quant(v, xnf, xq_stage, 1024, gs_sb[:, 2:3], gs_sb[:, 3:4], _sw,
                             sgn_ap=sgn_buf)
            inc("veA", ins)
            v.wait_ge(sems["mmA"], MM_P2(i, 4))
            ins = v.tensor_copy(stage8[:, 0, :], psb[:])
            inc("veA", ins)
        assert cnt["veA"] == VE_P2_END

        for qst in range(QSTN):
            cntk = PC[qst]
            pump_tail = None
            for hg in range(8):
                b = qst * 8 + hg
                part1, part2 = vsplit[b]
                state = {"p1": list(part1), "p2": list(part2), "k": 0, "npts": 10}
                while state["p1"] and state["p1"][0][0] == "ev":
                    p5_vact(v, state["p1"].pop(0))

                def pump(state=state):
                    state["k"] += 1
                    k = state["k"]
                    if k <= 7:
                        src, rem = state["p1"], 7 - k
                    else:
                        if state["p1"]:
                            state["p2"] = state["p1"] + state["p2"]
                            state["p1"] = []
                        src, rem = state["p2"], state["npts"] - k
                    n = len(src)
                    take = n if rem <= 0 else -(-n // (rem + 1))
                    for _ in range(take):
                        p5_vact(v, src.pop(0))
                if hg == 7:
                    pump_tail = pump
                nmask_start = cntk - 8
                for j in range(8):
                    kt = nmask_start + j
                    buf = kt % 2
                    v.wait_ge(sems["seD"], seD_b[("expb", qst, hg)] + kt + 1)
                    ex = expTb[:, buf, :]
                    v.tensor_tensor(ex[:, 0:512], ex[:, 0:512], maskbuf[:, j, :], ALU.mult)
                    ins = v.tensor_tensor(ex[:, 512:1024], ex[:, 512:1024], maskbuf[:, j, :], ALU.mult)
                    inc("veD", ins)
                    pump()
                v.wait_ge(sems["mmD"], mmD_b[("av", qst, hg)])
                v.reciprocal(rl_sb[0:1, 0:512], psf[0:1, 2560:3072])
                ins = v.reciprocal(rl_sb[0:1, 512:1024], psf[0:1, 3072:3584])
                inc("veD", ins)
                pump()
                v.wait_ge(sems["mmD"], mmD_b[("rl", qst, hg)])
                v.tensor_copy(evacf[:, 0:512], psf[:, 1024:1536])
                v.drain()
                ins = v.tensor_tensor(accT[:, hg, :], psf[:, 2048:2560],
                                      evacf[:, 0:512], ALU.mult)
                inc("veD", ins)
                pump()
            for tq in range(4):
                v.wait_ge(sems["mmD"], mmD_b[("wo", qst)] - 4 + tq + 1)
                v.wait_ge(sems["stD"], 16 * (4 * qst + tq))
                ins = v.tensor_copy(stile[:], psf[:, 0:1024])
                inc("veD", ins)
                pump_tail()
        assert cnt["veD"] == VE_P4_END
        for act in vprog[NBLK]:
            p5_vact(v, act)
        assert cnt["veE"] == VEE_END


        for qd in range(4):
            for oc in range(8):
                pend_amax = []
                for t4 in range(4):
                    v.wait_ge(sems["mmF"], mmF_b[("u", qd, oc, t4)])
                    v.wait_ge(sems["seF"], seF_b[("sil", qd, oc, t4)])
                    ins = v.tensor_tensor(hidf4[:, t4, :], hsg_par[t4 % 2][:],
                                          psf[:, 1024 * (t4 % 2) + 512:1024 * (t4 % 2) + 1024],
                                          ALU.mult)
                    inc("veF", ins)
                    pend_amax.append(t4)
                    if len(pend_amax) >= 2:
                        ta = pend_amax.pop(0)
                        v.tensor_reduce(dqsc[:, 0, ta * 32:(ta + 1) * 32],
                                        hidf4[:, ta, :].rearrange("p (a b) -> p a b", b=16),
                                        AX.X, ALU.max, apply_absolute_value=True)
                for ta in pend_amax:
                    v.tensor_reduce(dqsc[:, 0, ta * 32:(ta + 1) * 32],
                                    hidf4[:, ta, :].rearrange("p (a b) -> p a b", b=16),
                                    AX.X, ALU.max, apply_absolute_value=True)
                jobs = []
                for ji in range(2):
                    co = ji * 64
                    jobs.append(dict(
                        am=dqsc[:, 0, co:co + 64], am2=dqsc[:, 0, co:co + 64],
                        sc=dqsc[:, 1, co:co + 64], pe=dqsc[:, 2, co:co + 64],
                        th=dqsc[:, 3, co:co + 64], cc=dqsc[:, 4, co:co + 64],
                        fac=dqsc[:, 5, co:co + 64], rr=dqsc[:, 6, co:co + 64],
                        f16=dqsc[:, 7, co:co + 64],
                        gs6=gs_sb[:, 4:5], invgs=gs_sb[:, 5:6],
                    ))
                ins = emit_dq_chain(v, jobs)
                inc("veF", ins)
                sgn4 = [sgn_buf[:, 0:512], sgn_buf[:, 512:1024], xnf[:, 0:512], xnf[:, 512:1024]]
                for t4 in range(4):
                    if (qd, oc, t4) != (0, 0, 0):
                        pk = (qd, oc, t4 - 1) if t4 > 0 else ((qd, oc - 1, 3) if oc > 0
                                                              else (qd - 1, 7, 3))
                        v.wait_ge(sems["mmF"], mmF_b[("tr", pk[0], pk[1], pk[2])])
                    mjobs = []
                    for ji in range(2):
                        co = t4 * 32 + ji * 16
                        sl = slice(ji * 256, (ji + 1) * 256)
                        mjobs.append(dict(
                            src=hidf4[:, t4, sl], out=hqstage[:, sl], sgn=sgn4[t4][:, sl],
                            th=dqsc[:, 3, co:co + 16], rr=dqsc[:, 6, co:co + 16], nb=16,
                            a0=qs[0][:, sl], ax=qs[1][:, sl], ca=qs[2][:, sl], cb=qs[3][:, sl],
                        ))

                    def _sw2(qd=qd, oc=oc, t4=t4):
                        v.wait_ge(sems["seF"], seF_b[("sgn", qd, oc, t4)])
                    ins = emit_main_chain(v, mjobs, _sw2)
                    inc("veF", ins)
                    v.wait_ge(sems["mmF"], mmF_b[("tr", qd, oc, t4)])
                    ins = v.tensor_copy(hqT_t[t4][:, oc * 4:(oc + 1) * 4, :],
                                        psb[:, 0:512].rearrange("p (a b) -> p a b", a=4))
                    inc("veF", ins)
            for pr in range(2):
                for tp in range(2):
                    v.wait_ge(sems["seF"], seF_b[("g", qd, pr, tp)])
                    v.wait_ge(sems["ldF"], ldF_b[("sx", qd, pr, tp)])
                    v.tensor_tensor(cmbt[:, 0:512], stile2[:, 0:512], gtile2[:, 0:512], ALU.mult)
                    v.tensor_tensor(cmbt[:, 512:1024], stile2[:, 512:1024], gtile2[:, 512:1024], ALU.mult)
                    v.tensor_tensor(stile2[:, 0:512], cmbt[:, 0:512], xtile2[:, 0:512], ALU.add)
                    ins = v.tensor_tensor(stile2[:, 512:1024], cmbt[:, 512:1024], xtile2[:, 512:1024], ALU.add)
                    inc("veF", ins)
        assert cnt["veF"] == VEF_END

    @blk.scalar
    def _(sc_):
        sc_.wait_ge(sems["ld"], ck["ld_const"])
        sc_.mul(gs_sb[:, 2:3], gs_sb[:, 0:1], 1.0 / 6.0)
        sc_.mul(gs_sb[:, 4:5], gs_sb[:, 1:2], 1.0 / 6.0)
        sc_.wait_ge(sems["veA"], VE_P0)
        sc_.activation(tab_ks[:].rearrange("p a b -> p (a b)"),
                       tab_ks[:].rearrange("p a b -> p (a b)"), AF.Sin)
        sc_.activation(tab_kc[:].rearrange("p a b -> p (a b)"),
                       tab_kc[:].rearrange("p a b -> p (a b)"), AF.Sin)
        sc_.activation(tab_qs[:].rearrange("p a b -> p (a b)"),
                       tab_qs[:].rearrange("p a b -> p (a b)"), AF.Sin)
        sc_.activation(tab_qc[:].rearrange("p a b -> p (a b)"),
                       tab_qc[:].rearrange("p a b -> p (a b)"), AF.Sin)
        sc_.mul(tab_qs[:].rearrange("p a b -> p (a b)"),
                tab_qs[:].rearrange("p a b -> p (a b)"), 0.125)
        ins = sc_.mul(tab_qc[:].rearrange("p a b -> p (a b)"),
                      tab_qc[:].rearrange("p a b -> p (a b)"), 0.125)
        inc("seA", ins)
        for i in range(NKV):
            sc_.wait_ge(sems["ld"], ck[f"p1_ld_{i}"])
            if i >= 1:
                sc_.wait_ge(sems["veA"], VE_P1(i - 1, 2))
            lnx = lnx_a if i % 2 == 0 else lnx_b
            ins = sc_.activation(sqt[:], lnx, AF.Square)
            inc("seA", ins)
            sc_.wait_ge(sems["veA"], VE_P1(i, 1))
            ins = sc_.activation(scol(8), scol(7), AF.Sqrt)
            inc("seA", ins)
            sc_.wait_ge(sems["veA"], VE_P1(i, 5))
            inc("stB", sc_.dma_start(out=kT_sp[:, :, i * 128:(i + 1) * 128],
                                     in_=stage8[:, 0, :].rearrange("p (a b) -> p a b", a=8)), 16)
            inc("stB", sc_.dma_start(out=v_sp[:, i, :], in_=stage8[:, 1, :]), 16)
        for i in range(NQT):
            sc_.wait_ge(sems["ld"], ck[f"p2_ld_{i}"])
            sc_.wait_ge(sems["veA"], VE_P1(NKV - 1, 2) if i == 0 else VE_P2(i - 1, 2))
            lnx = lnx_a if i % 2 == 0 else lnx_b
            ins = sc_.activation(sqt[:], lnx, AF.Square)
            inc("seA", ins)
            sc_.wait_ge(sems["veA"], VE_P2(i, 1))
            ins = sc_.activation(scol(8), scol(7), AF.Sqrt)
            inc("seA", ins)
            sc_.wait_ge(sems["veA"], VE_P2(i, 2))
            ins = sc_.activation(sgn_buf[:], xnf[:], AF.Sign)
            inc("seA", ins)
            sc_.wait_ge(sems["veA"], VE_P2(i, 7))
            inc("stB", sc_.dma_start(out=xqT_sp[:, :, i * 128:(i + 1) * 128],
                                     in_=stage8[:, 0, :].rearrange("p (a b) -> p a b", a=8)), 16)

        for qst in range(QSTN):
            cntk = PC[qst]
            for hg in range(8):
                b = qst * 8 + hg
                nmask_start = cntk - 8
                bqh = mmD_b[("base", qst, hg)]
                for kt in range(cntk):
                    buf = kt % 2
                    sc_.wait_ge(sems["mmD"], mm_sc_val(bqh, kt))
                    if kt >= 2:
                        sc_.wait_ge(sems["mmD"], mm_av_val(bqh, kt - 2, cntk))
                        if kt - 2 >= nmask_start:
                            sc_.wait_ge(sems["veD"],
                                        veD_b[("mskb", qst, hg)] + (kt - 2 - nmask_start) + 1)
                    ins = sc_.activation(expTb[:, buf, :], psf[:, buf * 1024:(buf + 1) * 1024], AF.Exp)
                    inc("seD", ins)
                for act in vprog[b]:
                    if act[0] == "B":
                        p5_sgn(sc_, act[1], act[2])
                    elif act[0] == "ev":
                        p5_st(sc_, act[1], act[2])
            for tq in range(4):
                sc_.wait_ge(sems["veD"], veD_b[("sev", qst)] - 4 + tq + 1)
                row = qst * 512 + tq * 128
                inc("stD", sc_.dma_start(out=s_sp[row:row + 128, :], in_=stile[:]), 16)

        for act in vprog[NBLK]:
            if act[0] == "B":
                p5_sgn(sc_, act[1], act[2])
            elif act[0] == "ev":
                p5_st(sc_, act[1], act[2])
        assert cnt["seE"] == SEE_END and cnt["stE"] == STE_END

        for qd in range(4):
            for oc in range(8):
                for t4 in range(4):
                    sc_.wait_ge(sems["mmF"], mmF_b[("g", qd, oc, t4)])
                    if (qd, oc, t4) != (0, 0, 0):
                        pk = (qd, oc, t4 - 2) if t4 >= 2 else ((qd, oc - 1, t4 + 2) if oc > 0
                                                               else (qd - 1, 7, t4 + 2))
                        if pk[2] <= 3 and pk[0] >= 0:
                            sc_.wait_ge(sems["veF"], veF_b[("hid", pk[0], pk[1], pk[2])])
                    ins = sc_.activation(hsg_par[t4 % 2][:],
                                         psf[:, 1024 * (t4 % 2):1024 * (t4 % 2) + 512], AF.Silu)
                    inc("seF", ins)
                    sc_.wait_ge(sems["veF"], veF_b[("hid", qd, oc, t4)])
                    ins = sc_.activation(sgn4s[t4], hidf4[:, t4, :], AF.Sign)
                    inc("seF", ins)
            for pr in range(2):
                for tp in range(2):
                    t4 = 2 * pr + tp
                    j = 4 * qd + t4
                    sc_.wait_ge(sems["mmF"], mmF_b[("dn", qd, pr, 7, tp)])
                    ins = sc_.activation(gtile2[:], psf[:, 1024 + 1024 * tp:2048 + 1024 * tp], AF.Silu)
                    inc("seF", ins)
                    sc_.wait_ge(sems["veF"], veF_b[("cmb", qd, pr, tp)])
                    ins = sc_.dma_start(out=out_d[j * 128:(j + 1) * 128, :], in_=stile2[:])
                    inc("outS", ins, 16)

        es.close()
    return nc




_PROGRAM_CACHE = {}


def kernel(**inputs):
    in_maps = host_prep(inputs)
    if "nc" not in _PROGRAM_CACHE:
        _PROGRAM_CACHE["nc"] = build_program()
    nc = _PROGRAM_CACHE["nc"]
    res = run_bass_kernel_spmd(nc, in_maps, core_ids=list(range(8)))
    out = np.empty((B, S, D), dtype=np.float32)
    for c in range(8):
        b, sub = c // 2, c % 2
        q0s = Q0S[sub]
        oc = np.asarray(res.results[c]["out"])
        for sl, q0 in enumerate(q0s):
            out[b, q0:q0 + 512, :] = oc[sl * 512:(sl + 1) * 512, :]
    return out


# version 17
# speedup vs baseline: 1.0691x; 1.0199x over previous
import math
import os
import numpy as np
import ml_dtypes
from contextlib import ExitStack
import concourse.bass as bass
import concourse.mybir as mybir
from concourse.bass_utils import run_bass_kernel_spmd

F32 = mybir.dt.float32
BF16 = mybir.dt.bfloat16
ALU = mybir.AluOpType
AF = mybir.ActivationFunctionType
AX = mybir.AxisListType

B, S, D, H, HD = 4, 4096, 1024, 16, 64
NKV, NQT, NQ, QSTN = 32, 16, 2048, 4
PC = [32, 24, 16, 8]
Q0S = {0: [3584, 2048, 1536, 0], 1: [3072, 2560, 1024, 512]}
EPS = 1e-5
INV1024 = float(np.float32(1.0 / 1024.0))
DEBUG = 0

VE_P0 = 35
VE_P1 = lambda i, st: VE_P0 + 5 * i + st
VE_P1_END = VE_P0 + 5 * NKV
VE_P2 = lambda i, st: VE_P1_END + 7 * i + st
VE_P2_END = VE_P1_END + 7 * NQT
SE_P1 = lambda i, st: 1 + 2 * i + st
SE_P1_END = 1 + 2 * NKV
SE_P2 = lambda i, st: SE_P1_END + 3 * i + st
MM_P0 = 34
MM_P1 = lambda i, st: MM_P0 + 4 * i + st
MM_P1_END = MM_P0 + 4 * NKV
MM_P2 = lambda i, st: MM_P1_END + 4 * i + st
MM_P2_END = MM_P1_END + 4 * NQT
ST_P1_END = 32 * NKV


def _theta_red_hilo(pos, shift):
    i = np.arange(HD // 2, dtype=np.float64)[None, :]
    inv = np.exp(-2.0 * i / HD * math.log(10000.0))
    th = pos[:, None] * inv + shift
    k = np.floor((th + math.pi) / (2 * math.pi))
    thr = th - 2 * math.pi * k
    hi = thr.astype(np.float32)
    lo = (thr - hi.astype(np.float64)).astype(np.float32)
    return hi, lo


def host_prep(inputs):
    x = np.ascontiguousarray(inputs["L0_attn_x"], dtype=np.float32)
    ident = np.eye(128, dtype=ml_dtypes.bfloat16)
    ident8 = (np.eye(128, dtype=np.float32) * 0.125).astype(ml_dtypes.bfloat16)
    pos_kv = np.arange(S, dtype=np.float64)
    tkh_s, tkl_s = _theta_red_hilo(pos_kv, 0.0)
    tkh_c, tkl_c = _theta_red_hilo(pos_kv, math.pi / 2)
    indA = np.zeros((1, 128), dtype=np.float32); indA[0, :64] = 1.0
    indB = np.zeros((1, 128), dtype=np.float32); indB[0, 64:] = 1.0
    shared = {
        "ident": ident, "ident8": ident8,
        "ones_row": np.ones((1, 128), dtype=ml_dtypes.bfloat16),
        "ones_col": np.ones((128, 1), dtype=ml_dtypes.bfloat16),
        "indA": indA, "indB": indB,
        "ident_f": np.eye(128, dtype=np.float32),
        "ones_row_f": np.ones((1, 128), dtype=np.float32),
        "wq": inputs["L0_attn_wq"], "wk": inputs["L0_attn_wk"],
        "wv": inputs["L0_attn_wv"], "wo": inputs["L0_attn_wo"],
        "bq": np.ascontiguousarray(inputs["L0_attn_bq"].reshape(1, D)),
        "bk": np.ascontiguousarray(inputs["L0_attn_bk"].reshape(1, D)),
        "bv": np.ascontiguousarray(inputs["L0_attn_bv"].reshape(1, D)),
        "bo": np.ascontiguousarray(inputs["L0_attn_bo"].reshape(1, D)),
        "gw": inputs["L0_ffn_gate_w"], "uw": inputs["L0_ffn_up_w"],
        "dw": inputs["L0_ffn_down_w"],
        "s_in": np.ascontiguousarray(inputs["L0_ffn_s_in"].reshape(1)),
        "s_dn": np.ascontiguousarray(inputs["L0_ffn_s_in_down"].reshape(1)),
        "tk_sin_hi": tkh_s, "tk_sin_lo": tkl_s, "tk_cos_hi": tkh_c, "tk_cos_lo": tkl_c,
    }
    in_maps = []
    for c in range(8):
        b, sub = c // 2, c % 2
        q0s = Q0S[sub]
        qrows = np.concatenate([np.arange(q0, q0 + 512) for q0 in q0s])
        pos_q = qrows.astype(np.float64)
        tqh_s, tql_s = _theta_red_hilo(pos_q, 0.0)
        tqh_c, tql_c = _theta_red_hilo(pos_q, math.pi / 2)
        masks = np.zeros((QSTN * 8, 128, 512), dtype=np.float32)
        for sl in range(QSTN):
            q0 = q0s[sl]
            for j in range(8):
                kt = PC[sl] - 8 + j
                kpos = 128 * kt + np.arange(128)[:, None]
                qpos = q0 + np.arange(512)[None, :]
                masks[sl * 8 + j] = (kpos <= qpos)
        m = dict(shared)
        m.update({
            "x_q": np.ascontiguousarray(x[b][qrows]), "x_kv": np.ascontiguousarray(x[b]),
            "tq_sin_hi": tqh_s, "tq_sin_lo": tql_s, "tq_cos_hi": tqh_c, "tq_cos_lo": tql_c,
            "masks": masks.astype(ml_dtypes.bfloat16),
        })
        in_maps.append(m)
    return in_maps


def build_program():
    nc = bass.Bass("TRN2", target_bir_lowering=False, detect_race_conditions=False)
    es = ExitStack()
    d = {}

    def din(name, shape, dt=F32):
        d[name] = nc.dram_tensor(name, shape, dt, kind="ExternalInput").ap()

    din("x_q", [NQ, D]); din("x_kv", [S, D])
    for n in ("wq", "wk", "wv", "wo"): din(n, [D, D])
    for n in ("bq", "bk", "bv", "bo"): din(n, [1, D])
    din("gw", [4096, D]); din("uw", [4096, D]); din("dw", [D, 4096])
    din("s_in", [1]); din("s_dn", [1])
    din("ident", [128, 128], BF16); din("ident8", [128, 128], BF16)
    din("ones_row", [1, 128], BF16); din("ones_col", [128, 1], BF16)
    din("indA", [1, 128]); din("indB", [1, 128])
    din("ident_f", [128, 128]); din("ones_row_f", [1, 128])
    for n in ("tq_sin_hi", "tq_sin_lo", "tq_cos_hi", "tq_cos_lo"): din(n, [NQ, HD // 2])
    for n in ("tk_sin_hi", "tk_sin_lo", "tk_cos_hi", "tk_cos_lo"): din(n, [S, HD // 2])
    din("masks", [QSTN * 8, 128, 512], BF16)

    out_d = nc.dram_tensor("out", [NQ, D], F32, kind="ExternalOutput").ap()
    spill_kind = "ExternalOutput" if DEBUG else "Internal"
    kT_sp = nc.dram_tensor("kT_sp", [128, 8, S], BF16, kind=spill_kind).ap()
    v_sp = nc.dram_tensor("v_sp", [128, NKV, D], BF16, kind=spill_kind).ap()
    xqT_sp = nc.dram_tensor("xqT_sp", [128, 8, NQ], BF16, kind=spill_kind).ap()
    s_sp = nc.dram_tensor("s_sp", [NQ, D], F32, kind=spill_kind).ap()
    wgT_sp = nc.dram_tensor("wgT_sp", [128, 8, 4096], BF16, kind=spill_kind).ap()
    wuT_sp = nc.dram_tensor("wuT_sp", [128, 8, 4096], BF16, kind=spill_kind).ap()
    wdT_sp = nc.dram_tensor("wdT_sp", [128, 32, D], BF16, kind=spill_kind).ap()
    qT_sp = nc.dram_tensor("qT_sp", [128, 8, NQ], BF16, kind=spill_kind).ap()
    amax_sp = nc.dram_tensor("amax_sp", [128, 4096], F32, kind=spill_kind).ap()

    SB_BF = 62464
    bfp = es.enter_context(nc.sbuf_tensor("bfp", [128, SB_BF], BF16))

    def bfv(off, shape):
        n = int(np.prod(shape))
        apv = bfp[:, off:off + n]
        if len(shape) > 1:
            letters = "abcdefg"[: len(shape)]
            apv = apv.rearrange(f"p ({' '.join(letters)}) -> p {' '.join(letters)}",
                                **{letters[i]: shape[i] for i in range(len(shape))})
        return apv

    qT = bfv(0, [8, NQ])
    wT = {m: bfv(16384 + i * 8192, [8, 1024])
          for i, m in enumerate(("wq", "wk", "wv", "wo"))}
    xtile_a = bfv(49152, [1024])
    xtile_b = bfv(50176, [1024])
    xq_stage = bfv(51200, [1024])
    xq_par = [bfv(51200, [1024]), bfv(34816, [1024])]
    stage8 = bfv(52224, [2, 1024])
    wbig_bf = bfv(54272, [8, 1024])
    xqT = bfv(0, [8, NQ])
    wg_par = [bfv(16384, [8, 512]), bfv(41472, [8, 512])]
    wu_par = [bfv(20480, [8, 512]), bfv(45568, [8, 512])]
    hqT_t = [bfv(24576 + 4096 * i, [32, 128]) for i in range(4)]
    hqstage = bfv(40960, [512])
    wd_par = [bfv(54272, [4, 1024]), bfv(58368, [4, 1024])]
    kt_par = [bfv(16384, [4096]), bfv(54272, [4096])]
    v_par = [bfv(20480, [4096]), bfv(58368, [4096])]
    maskbuf = bfv(24576, [8, 512])
    accT = bfv(28672, [8, 512])
    expTb = bfv(32768, [2, 1024])

    SB_F32 = 18432
    fpool = es.enter_context(nc.sbuf_tensor("fpool", [128, SB_F32], F32))

    def fv(off, shape):
        n = int(np.prod(shape))
        apv = fpool[:, off:off + n]
        if len(shape) > 1:
            letters = "abcdefg"[: len(shape)]
            apv = apv.rearrange(f"p ({' '.join(letters)}) -> p {' '.join(letters)}",
                                **{letters[i]: shape[i] for i in range(len(shape))})
        return apv

    qs = [fv(i * 1024, [1024]) for i in range(4)]
    lnx_a = fv(4096, [1024]); lnx_b = fv(5120, [1024])
    sqt = fv(6144, [1024])
    xnf = fv(7168, [1024])
    ropet = fv(8192, [6, 512])
    tab_ks = fv(11264, [32, 32]); tab_kc = fv(12288, [32, 32])
    tab_qs = fv(13312, [16, 32]); tab_qc = fv(13824, [16, 32])
    dqsc = fv(14336, [8, 128])
    dqf = fv(14336, [1024])
    stile = fv(17408, [1024])
    amaxc = fv(8192, [2048])
    wf_a = fv(4096, [1024]); wf_b = fv(5120, [1024])
    wf4_a = fv(4096, [1024]); wf4_b = fv(5120, [1024])
    stile2 = fv(10240, [1024])
    xtile2 = fv(11264, [1024])
    gtile2 = fv(12288, [1024])
    hidf4 = fv(4096, [4, 512])
    hsg_par = [fv(6144, [512]), fv(6656, [512])]
    cmbt = fv(13312, [1024])
    sgn_buf = fv(15360, [1024])
    evacf = fv(16384, [1024])

    th_sb = fpool[:, 0:6144]
    bias_sb = fpool[0:1, 6144:6144 + 4 * D]
    scl = es.enter_context(nc.sbuf_tensor("scl", [128, 16], F32))

    def scol(i):
        return scl[:, i:i + 1]

    psf = es.enter_context(nc.psum_tensor("psf", [128, 3584], F32))
    psb = es.enter_context(nc.psum_tensor("psb", [128, 1024], BF16))

    identf_sb = es.enter_context(nc.sbuf_tensor("identf_sb", [128, 128], F32))
    onesrf_sb = es.enter_context(nc.sbuf_tensor("onesrf_sb", [1, 128], F32))
    ident_sb = es.enter_context(nc.sbuf_tensor("ident_sb", [128, 128], BF16))
    ident8_sb = es.enter_context(nc.sbuf_tensor("ident8_sb", [128, 128], BF16))
    onesr_sb = es.enter_context(nc.sbuf_tensor("onesr_sb", [1, 128], BF16))
    onesc_sb = es.enter_context(nc.sbuf_tensor("onesc_sb", [128, 1], BF16))
    indA_sb = es.enter_context(nc.sbuf_tensor("indA_sb", [1, 128], F32))
    indB_sb = es.enter_context(nc.sbuf_tensor("indB_sb", [1, 128], F32))
    biasb_sb = es.enter_context(nc.sbuf_tensor("biasb_sb", [1, 2 * D], BF16))
    beff_sb = es.enter_context(nc.sbuf_tensor("beff_sb", [1, D], BF16))
    bvcolf_sb = es.enter_context(nc.sbuf_tensor("bvcolf_sb", [128, 8], F32))
    bvcol_sb = es.enter_context(nc.sbuf_tensor("bvcol_sb", [128, 8], BF16))
    gs_sb = es.enter_context(nc.sbuf_tensor("gs_sb", [128, 16], F32))
    rl_sb = es.enter_context(nc.sbuf_tensor("rl_sb", [1, 1024], F32))

    sems = {}
    cnt = {}
    for n in ("ld", "gld", "mmA", "veA", "seA", "stB", "fin", "ldD", "mmD", "seD", "veD", "stD",
              "ldE", "mmE", "veE", "seE", "stE", "ldF", "mmF", "veF", "seF", "stF", "outS"):
        sems[n] = es.enter_context(nc.semaphore("s_" + n))
        cnt[n] = 0
    ck = {}

    def inc(name, instr, k=1):
        instr.then_inc(sems[name], k)
        cnt[name] += k
        return cnt[name]

    mmD_b, seD_b, veD_b, ldD_b = {}, {}, {}, {}
    mm_c = se_c = ve_c = ld_c = 0
    for qst in range(QSTN):
        cntk = PC[qst]
        ld_c += 16
        ldD_b[("mask", qst)] = ld_c
        for hg in range(8):
            ld_c += 32
            ldD_b[("kv", qst, hg)] = ld_c
            mmD_b[("base", qst, hg)] = mm_c
            mm_c += 2 * cntk
            mmD_b[("av", qst, hg)] = mm_c
            seD_b[("expb", qst, hg)] = se_c
            se_c += cntk
            seD_b[("exp", qst, hg)] = se_c
            veD_b[("mskb", qst, hg)] = ve_c
            ve_c += 8
            veD_b[("msk", qst, hg)] = ve_c
            ve_c += 1
            veD_b[("rcp", qst, hg)] = ve_c
            mm_c += 1
            mmD_b[("rl", qst, hg)] = mm_c
            ve_c += 1
            veD_b[("accbf", qst, hg)] = ve_c
        mm_c += 4
        mmD_b[("wo", qst)] = mm_c
        ve_c += 4
        veD_b[("sev", qst)] = ve_c
    MM_P4_END, SE_P4_END, VE_P4_END, LD_P4_END = mm_c, se_c, ve_c, ld_c
    ST_P4_END = 16 * 4 * QSTN

    NBLK = 8 * QSTN
    P5U = [("A", 0, r) for r in range(32)] + [("chain", 0)]
    bq = [("B", mi, r) for mi in range(3) for r in range(32)]
    bq_i = 0
    for mi_a in (1, 2):
        ar = 0
        while ar < 32:
            P5U.append(("A", mi_a, ar)); ar += 1
            if ar < 32:
                P5U.append(("A", mi_a, ar)); ar += 1
            if bq_i < len(bq) and bq[bq_i][1] != mi_a:
                P5U.append(bq[bq_i]); bq_i += 1
        P5U.append(("chain", mi_a))
    P5U += bq[bq_i:]

    def p5cost(u):
        if u[0] == "A":
            return 1.0
        if u[0] == "chain":
            return 4.0
        return 24.0 if u[2] % 16 == 0 else 17.0

    blkw = [float(PC[qst]) for qst in range(QSTN) for _ in range(8)]
    totw = sum(blkw); totc = sum(p5cost(u) for u in P5U)
    sched = [[] for _ in range(NBLK + 1)]
    bi = 0; used = 0.0; budget = blkw[0] / totw * totc
    nA = nB = 0
    for u in P5U:
        c = p5cost(u)
        while bi < NBLK and (used + c > budget or (u[0] == "A" and nA >= 8)
                             or (u[0] == "B" and nB >= 2)):
            bi += 1
            if bi < NBLK:
                budget += blkw[bi] / totw * totc
                nA = nB = 0
        sched[min(bi, NBLK)].append(u)
        used += c
        if u[0] == "A":
            nA += 1
        elif u[0] == "B":
            nB += 1

    par5 = {}
    lseq = [u for u in P5U if u[0] in ("A", "B")]
    for p, u in enumerate(lseq):
        par5[u] = p % 2
    border = [u for u in P5U if u[0] == "B"]
    bprev = {border[k]: (border[k - 1] if k >= 1 else None) for k in range(len(border))}
    bprev2 = {border[k]: (border[k - 2] if k >= 2 else None) for k in range(len(border))}

    vprog = [[] for _ in range(NBLK + 1)]
    pend = []
    for b in range(NBLK + 1):
        for p in pend:
            vprog[b].append(("ev", p[1], p[2]))
        pend = []
        tail_blk = (b < NBLK and b % 8 == 7)
        for u in sched[b]:
            vprog[b].append(u)
            if u[0] == "B":
                if pend and not tail_blk:
                    for p in pend:
                        vprog[b].append(("ev", p[1], p[2]))
                    pend = []
                pend.append(u)
    for p in pend:
        vprog[NBLK].append(("ev", p[1], p[2]))

    ldE_b, veE_b, mmE_b, seE_b, stE_b = {}, {}, {}, {}, {}
    vc = mc = sc5 = st5 = 0
    for b in range(NBLK + 1):
        for act in vprog[b]:
            vc += 1
            if act[0] == "A":
                veE_b[("A", act[1], act[2])] = vc
            elif act[0] == "chain":
                veE_b[("chain", act[1])] = vc
            elif act[0] == "B":
                veE_b[("q", act[1], act[2])] = vc
            else:
                veE_b[("ev", act[1], act[2])] = vc
    for mi in range(3):
        for r in range(32):
            sc5 += 1; seE_b[("sgn", mi, r)] = sc5
            mc += 1; mmE_b[("tr", mi, r)] = mc
            st5 += 16; stE_b[("st", mi, r)] = st5
    gldseq = []
    for u in P5U:
        if u[0] == "A":
            gldseq.append(u)
        elif u[0] == "chain":
            if u[1] < 2:
                gldseq.append(("spl", u[1]))
        elif u[0] == "B":
            if u[2] % 16 == 0 and u[1] < 2:
                gldseq.append(("rld", u[1], u[2] // 16))
            gldseq.append(u)
    lc = 0
    for it in gldseq:
        lc += 16
        ldE_b[it] = lc
    LDE_END, VEE_END, MME_END, SEE_END, STE_END = lc, vc, mc, sc5, st5
    batches = [(mi, h) for mi in range(3) for h in range(2)]

    stB_blocks = [[a for a in vprog[b] if a[0] == "ev"] for b in range(NBLK)]
    sgn_blocks = [[u for u in sched[b] if u[0] == "B"] for b in range(NBLK)]
    t_blocks = [[u for u in sched[b] if u[0] == "B"] for b in range(NBLK + 1)]
    vsplit = []
    for b in range(NBLK):
        acts = vprog[b]
        i1 = next((i for i, a in enumerate(acts) if a[0] == "B"), None)
        if i1 is None:
            vsplit.append((list(acts), []))
        else:
            vsplit.append((list(acts[:i1]), list(acts[i1:])))

    ldF_b, veF_b, mmF_b, seF_b = {}, {}, {}, {}
    lf = vf = mf = sf = 0
    lf += 16; ldF_b["xqT"] = lf
    for qd in range(4):
        for oc in range(8):
            lf += 32; ldF_b[("gu", qd, oc)] = lf
            for t4 in range(4):
                mf += 1; mmF_b[("g", qd, oc, t4)] = mf
                mf += 1; mmF_b[("u", qd, oc, t4)] = mf
                sf += 1; seF_b[("sil", qd, oc, t4)] = sf
                vf += 1; veF_b[("hid", qd, oc, t4)] = vf
                sf += 1; seF_b[("sgn", qd, oc, t4)] = sf
            vf += 1; veF_b[("dq", qd, oc)] = vf
            for p4_ in range(2):
                vf += 1; veF_b[("q", qd, oc, p4_)] = vf
                mf += 1; mmF_b[("tr", qd, oc, p4_)] = mf
                vf += 1; veF_b[("ev", qd, oc, p4_)] = vf
        for pr in range(2):
            for og in range(8):
                lf += 16; ldF_b[("wd", qd, pr, og)] = lf
                for tp in range(2):
                    mf += 1; mmF_b[("dn", qd, pr, og, tp)] = mf
            for tp in range(2):
                sf += 1; seF_b[("g", qd, pr, tp)] = sf
                lf += 32; ldF_b[("sx", qd, pr, tp)] = lf
                vf += 1; veF_b[("cmb", qd, pr, tp)] = vf
    LDF_END, VEF_END, MMF_END, SEF_END = lf, vf, mf, sf

    sgn4s = None
    def mm_sc_val(b, kt):
        return b + (kt + 1 if kt < 2 else 2 * kt)

    def mm_av_val(b, j, cntk):
        return b + (2 * j + 3 if j <= cntk - 2 else 2 * cntk)

    sgn4s = [sgn_buf[:, 0:512], sgn_buf[:, 512:1024], xnf[:, 0:512], xnf[:, 512:1024]]

    def p5_tr(t, mi, r):
        t.wait_ge(sems["veE"], veE_b[("q", mi, r)])
        pb = bprev[("B", mi, r)]
        if pb is not None:
            t.wait_ge(sems["veE"], veE_b[("ev", pb[1], pb[2])])
        xqv = xq_par[r % 2].rearrange("p (a b) -> p a b", a=8)
        for c in range(8):
            ins = t.transpose(psb[:, c * 128:(c + 1) * 128], xqv[:, c, :], ident_sb[:])
        inc("mmE", ins)

    def p5_sgn(sc_, mi, r):
        sc_.wait_ge(sems["ldE"], ldE_b[("B", mi, r)])
        pb = bprev[("B", mi, r)]
        if pb is not None:
            sc_.wait_ge(sems["veE"], veE_b[("q", pb[1], pb[2])])
        wf = wf_a if par5[("B", mi, r)] == 0 else wf_b
        ins = sc_.activation(sgn_buf[:], wf, AF.Sign)
        inc("seE", ins)

    def p5_st(sc_, mi, r):
        sc_.wait_ge(sems["veE"], veE_b[("ev", mi, r)])
        if mi < 2:
            dsp = (wgT_sp if mi == 0 else wuT_sp)[:, :, r * 128:(r + 1) * 128]
        else:
            dsp = wdT_sp[:, (r % 4) * 8:(r % 4 + 1) * 8, (r // 4) * 128:(r // 4 + 1) * 128]
        ins = sc_.dma_start(out=dsp,
                            in_=stage8[:, r % 2, :].rearrange("p (a b) -> p a b", a=8))
        inc("stE", ins, 16)

    blk = es.enter_context(nc.Block())

    @blk.sync
    def _(sy):
        for name, dst in (("ident", ident_sb[:]), ("ident8", ident8_sb[:]),
                          ("ones_row", onesr_sb[:]), ("ones_col", onesc_sb[:]),
                          ("indA", indA_sb[:]), ("indB", indB_sb[:]),
                          ("ident_f", identf_sb[:]), ("ones_row_f", onesrf_sb[:])):
            inc("ld", sy.dma_start(out=dst, in_=d[name]), 16)
        for i, name in enumerate(("bq", "bk", "bv", "bo")):
            inc("ld", sy.dma_start(out=bias_sb[:, i * D:(i + 1) * D], in_=d[name]), 16)
        inc("ld", sy.dma_start(out=bvcolf_sb[:],
                               in_=d["bv"].rearrange("o (a p) -> p o a", p=128).squeeze(1),
                               allow_slow_non_contiguous=True), 16)
        offs = 0
        for name in ("tk_sin_hi", "tk_sin_lo", "tk_cos_hi", "tk_cos_lo"):
            inc("ld", sy.dma_start(
                out=th_sb[:, offs:offs + 1024].rearrange("p (a b) -> p a b", b=32),
                in_=d[name].rearrange("(tl p) i -> p tl i", p=128)), 16)
            offs += 1024
        for name in ("tq_sin_hi", "tq_sin_lo", "tq_cos_hi", "tq_cos_lo"):
            inc("ld", sy.dma_start(
                out=th_sb[:, offs:offs + 512].rearrange("p (a b) -> p a b", b=32),
                in_=d[name].rearrange("(tl p) i -> p tl i", p=128)), 16)
            offs += 512
        inc("ld", sy.dma_start(out=gs_sb[:, 0:1],
                               in_=d["s_in"].partition_broadcast(128).unsqueeze(-1)), 16)
        inc("ld", sy.dma_start(out=gs_sb[:, 1:2],
                               in_=d["s_dn"].partition_broadcast(128).unsqueeze(-1)), 16)
        ck["ld_const"] = cnt["ld"]

        for i in range(NKV):
            if i < 2:
                sy.wait_ge(sems["veA"], VE_P0)
            else:
                sy.wait_ge(sems["veA"], VE_P1(i - 2, 2))
                sy.wait_ge(sems["seA"], SE_P1(i - 2, 1))
            dst = lnx_a if i % 2 == 0 else lnx_b
            inc("ld", sy.dma_start(out=dst, in_=d["x_kv"][i * 128:(i + 1) * 128, :]), 16)
            ck[f"p1_ld_{i}"] = cnt["ld"]
        for i in range(NQT):
            if i < 2:
                sy.wait_ge(sems["veA"], VE_P1(NKV - 2 + i, 2))
                sy.wait_ge(sems["seA"], SE_P1(NKV - 2 + i, 1))
            else:
                sy.wait_ge(sems["veA"], VE_P2(i - 2, 2))
                sy.wait_ge(sems["seA"], SE_P2(i - 2, 1))
            dst = lnx_a if i % 2 == 0 else lnx_b
            inc("ld", sy.dma_start(out=dst, in_=d["x_q"][i * 128:(i + 1) * 128, :]), 16)
            ck[f"p2_ld_{i}"] = cnt["ld"]

        sy.wait_ge(sems["stB"], 16 * 2 * NKV)
        for qst in range(QSTN):
            cntk = PC[qst]
            if qst >= 1:
                sy.wait_ge(sems["veD"], veD_b[("accbf", qst - 1, 7)])
            inc("ldD", sy.dma_start(out=maskbuf,
                                    in_=d["masks"][qst * 8:(qst + 1) * 8].transpose([1, 0, 2])), 16)
            for hg in range(8):
                bi = qst * 8 + hg
                if bi == 0:
                    sy.wait_ge(sems["mmA"], MM_P0)
                elif bi == 1:
                    sy.wait_ge(sems["veA"], VE_P2_END)
                if bi >= 2:
                    p2 = ((bi - 2) // 8, (bi - 2) % 8)
                    sy.wait_ge(sems["mmD"], mmD_b[("av", p2[0], p2[1])])
                par = (bi + 1) % 2
                inc("ldD", sy.dma_start(out=kt_par[par][:, 0:cntk * 128],
                                        in_=kT_sp[:, hg, 0:cntk * 128]), 16)
                inc("ldD", sy.dma_start(out=v_par[par][:, 0:cntk * 128].rearrange("p (a b) -> p a b", b=128),
                                        in_=v_sp[:, 0:cntk, hg * 128:(hg + 1) * 128]), 16)
        sy.wait_ge(sems["stD"], ST_P4_END)
        inc("ldF", sy.dma_start(out=xqT, in_=xqT_sp), 16)
        gu_seq = [(qd, oc) for qd in range(4) for oc in range(8)]
        wd_seq = [(qd, pr, og) for qd in range(4) for pr in range(2) for og in range(8)]
        for qd in range(4):
            for oc in range(8):
                gi = gu_seq.index((qd, oc))
                if gi == 0:
                    sy.wait_ge(sems["stE"], stE_b[("st", 0, 31)])
                    sy.wait_ge(sems["stE"], stE_b[("st", 1, 31)])
                elif gi >= 2:
                    p2 = gu_seq[gi - 2]
                    sy.wait_ge(sems["mmF"], mmF_b[("u", p2[0], p2[1], 3)])
                par = oc % 2
                inc("ldF", sy.dma_start(out=wg_par[par], in_=wgT_sp[:, :, oc * 512:(oc + 1) * 512]), 16)
                inc("ldF", sy.dma_start(out=wu_par[par], in_=wuT_sp[:, :, oc * 512:(oc + 1) * 512]), 16)
            for pr in range(2):
                for og in range(8):
                    wi = wd_seq.index((qd, pr, og))
                    if wi == 0:
                        sy.wait_ge(sems["stE"], stE_b[("st", 2, 31)])
                    elif wi >= 2:
                        pk = wd_seq[wi - 2]
                        sy.wait_ge(sems["mmF"], mmF_b[("dn", pk[0], pk[1], pk[2], 1)])
                    inc("ldF", sy.dma_start(out=wd_par[og % 2],
                                            in_=wdT_sp[:, og * 4:(og + 1) * 4, :]), 16)
                for tp in range(2):
                    j = 4 * qd + 2 * pr + tp
                    if j >= 1:
                        sy.wait_ge(sems["outS"], 16 * j)
                    inc("ldF", sy.dma_start(out=stile2, in_=s_sp[j * 128:(j + 1) * 128, :]), 16)
                    inc("ldF", sy.dma_start(out=xtile2, in_=d["x_q"][j * 128:(j + 1) * 128, :]), 16)
        sy.wait_ge(sems["stB"], 16 * (2 * NKV + NQT))
        if DEBUG:
            inc("ld", sy.dma_start(out=qT_sp, in_=qT), 16)
        sy.wait_ge(sems["outS"], 16 * NQT)

    @blk.gpsimd
    def _(g):
        for mi, m in enumerate(("wq", "wk", "wv", "wo")):
            if mi >= 1:
                g.wait_ge(sems["mmA"], 8 * mi)
            inc("gld", g.dma_start(out=wbig_bf, in_=d[m].rearrange("(r p) h -> p r h", p=128)), 16)
            ck[f"p0_gld_w_{mi}"] = cnt["gld"]

        def wslice(mi, r):
            mn = ("gw", "uw", "dw")[mi]
            if mn != "dw":
                return d[mn][r * 128:(r + 1) * 128, :]
            return d["dw"][(r // 4) * 128:(r // 4 + 1) * 128, (r % 4) * 1024:(r % 4 + 1) * 1024]
        g.wait_ge(sems["veA"], VE_P2_END)
        lpos = 0
        for it in gldseq:
            if it[0] == "spl":
                g.wait_ge(sems["veE"], veE_b[("chain", it[1])])
                inc("ldE", g.dma_start(out=amax_sp[:, it[1] * 2048:(it[1] + 1) * 2048],
                                       in_=amaxc[:]), 16)
            elif it[0] == "rld":
                kb = batches.index((it[1], it[2]))
                if kb >= 1:
                    pmi, ph = batches[kb - 1]
                    g.wait_ge(sems["veE"], veE_b[("q", pmi, ph * 16)])
                g.wait_ge(sems["ldE"], ldE_b[("spl", it[1])])
                off = it[1] * 2048 + it[2] * 1024
                inc("ldE", g.dma_start(out=dqf, in_=amax_sp[:, off:off + 1024]), 16)
            else:
                if lpos >= 2:
                    pu = lseq[lpos - 2]
                    if pu[0] == "A":
                        g.wait_ge(sems["veE"], veE_b[("A", pu[1], pu[2])])
                    else:
                        g.wait_ge(sems["veE"], veE_b[("q", pu[1], pu[2])])
                        g.wait_ge(sems["seE"], seE_b[("sgn", pu[1], pu[2])])
                dst = wf_a if lpos % 2 == 0 else wf_b
                inc("ldE", g.dma_start(out=dst, in_=wslice(it[1], it[2])), 16)
                lpos += 1

    @blk.tensor
    def _(t):
        t.wait_ge(sems["ld"], ck["ld_const"])
        for mi, m in enumerate(("wq", "wk", "wv", "wo")):
            t.wait_ge(sems["gld"], ck[f"p0_gld_w_{mi}"])
            idn = ident_sb
            for r in range(8):
                if mi * 8 + r >= 1:
                    t.wait_ge(sems["veA"], mi * 8 + r)
                for c in range(8):
                    ins = t.transpose(psb[:, c * 128:(c + 1) * 128],
                                      wbig_bf[:, r, c * 128:(c + 1) * 128], idn[:])
                inc("mmA", ins)
        t.wait_ge(sems["veA"], 33)
        for ch in range(2):
            for kt in range(8):
                ins = t.matmul(psf[0:1, 2048 + ch * 512: 2048 + (ch + 1) * 512],
                               bvcol_sb[:, kt:kt + 1], wT["wo"][:, kt, ch * 512:(ch + 1) * 512],
                               start=(kt == 0), stop=(kt == 7))
            inc("mmA", ins)
        assert cnt["mmA"] == MM_P0

        for i in range(NKV):
            xtv = (xtile_a if i % 2 == 0 else xtile_b).rearrange("p (a b) -> p a b", a=8)
            t.wait_ge(sems["veA"], VE_P1(i, 2))
            for c in range(8):
                ins = t.transpose(psb[:, c * 128:(c + 1) * 128], xtv[:, c, :], ident_sb[:])
            inc("mmA", ins)
            t.wait_ge(sems["veA"], VE_P1(i, 3))
            xnT = stage8[:, 0, :].rearrange("p (a b) -> p a b", a=8)
            for ch in range(2):
                for kt in range(8):
                    t.matmul(psf[:, ch * 512:(ch + 1) * 512], xnT[:, kt, :],
                             wT["wk"][:, kt, ch * 512:(ch + 1) * 512],
                             start=(kt == 0), stop=False)
                ins = t.matmul(psf[:, ch * 512:(ch + 1) * 512], onesr_sb[:],
                               biasb_sb[:, D + ch * 512: D + (ch + 1) * 512],
                               start=False, stop=True)
            inc("mmA", ins)
            for ch in range(2):
                for kt in range(8):
                    ins = t.matmul(psf[:, 1024 + ch * 512: 1024 + (ch + 1) * 512], xnT[:, kt, :],
                                   wT["wv"][:, kt, ch * 512:(ch + 1) * 512],
                                   start=(kt == 0), stop=(kt == 7))
            inc("mmA", ins)
            t.wait_ge(sems["veA"], VE_P1(i, 4))
            ktv = (xtile_a if i % 2 == 0 else xtile_b).rearrange("p (a b) -> p a b", a=8)
            for c in range(8):
                ins = t.transpose(psb[:, c * 128:(c + 1) * 128], ktv[:, c, :], ident_sb[:])
            inc("mmA", ins)
        assert cnt["mmA"] == MM_P1_END

        for i in range(NQT):
            xtv = (xtile_a if i % 2 == 0 else xtile_b).rearrange("p (a b) -> p a b", a=8)
            t.wait_ge(sems["veA"], VE_P2(i, 2))
            for c in range(8):
                ins = t.transpose(psb[:, c * 128:(c + 1) * 128], xtv[:, c, :], ident_sb[:])
            inc("mmA", ins)
            t.wait_ge(sems["veA"], VE_P2(i, 3))
            xnT = stage8[:, 0, :].rearrange("p (a b) -> p a b", a=8)
            for ch in range(2):
                for kt in range(8):
                    t.matmul(psf[:, ch * 512:(ch + 1) * 512], xnT[:, kt, :],
                             wT["wq"][:, kt, ch * 512:(ch + 1) * 512],
                             start=(kt == 0), stop=False)
                ins = t.matmul(psf[:, ch * 512:(ch + 1) * 512], onesr_sb[:],
                               biasb_sb[:, ch * 512:(ch + 1) * 512], start=False, stop=True)
            inc("mmA", ins)
            t.wait_ge(sems["veA"], VE_P2(i, 4))
            qtv = (xtile_a if i % 2 == 0 else xtile_b).rearrange("p (a b) -> p a b", a=8)
            for c in range(8):
                ins = t.transpose(psb[:, c * 128:(c + 1) * 128], qtv[:, c, :], ident_sb[:])
            inc("mmA", ins)
            t.wait_ge(sems["veA"], VE_P2(i, 6))
            xqv = xq_stage.rearrange("p (a b) -> p a b", a=8)
            for c in range(8):
                ins = t.transpose(psb[:, c * 128:(c + 1) * 128], xqv[:, c, :], ident_sb[:])
            inc("mmA", ins)
        assert cnt["mmA"] == MM_P2_END

        t.wait_ge(sems["veA"], VE_P2_END)
        for qst in range(QSTN):
            cntk = PC[qst]
            for hg in range(8):
                t.wait_ge(sems["ldD"], ldD_b[("kv", qst, hg)])
                if qst >= 1 or hg >= 1:
                    phg = (qst, hg - 1) if hg >= 1 else (qst - 1, 7)
                    t.wait_ge(sems["veD"], veD_b[("accbf", phg[0], phg[1])])
                if hg == 0 and qst >= 1:
                    t.wait_ge(sems["veD"], veD_b[("sev", qst - 1)])
                qA = qT[0:64, hg, qst * 512:(qst + 1) * 512]
                qB = qT[64:128, hg, qst * 512:(qst + 1) * 512]
                nmask_start = cntk - 8
                eb = seD_b[("expb", qst, hg)]
                mb = veD_b[("mskb", qst, hg)]
                par4 = (qst * 8 + hg + 1) % 2
                ktall = kt_par[par4]
                vt = v_par[par4][:, 0:cntk * 128].rearrange("p (a b) -> p a b", b=128)

                def emit_sc(kt):
                    buf = kt % 2
                    off = buf * 1024
                    if kt >= 2:
                        t.wait_ge(sems["seD"], eb + kt - 1)
                    ktile = ktall[:, kt * 128:(kt + 1) * 128]
                    t.matmul(psf[:, off:off + 512], ktile[0:64, :], qA,
                             start=True, stop=True, tile_position=(0, 0))
                    ins = t.matmul(psf[:, off + 512:off + 1024], ktile[64:128, :], qB,
                                   start=True, stop=True, tile_position=(64, 0))
                    inc("mmD", ins)

                def emit_av(kt):
                    buf = kt % 2
                    t.wait_ge(sems["seD"], eb + kt + 1)
                    if kt >= nmask_start:
                        t.wait_ge(sems["veD"], mb + (kt - nmask_start) + 1)
                    ex = expTb[:, buf, :]
                    t.matmul(psf[0:1, 2560:3072], onesc_sb[:], ex[:, 0:512],
                             start=(kt == 0), stop=(kt == cntk - 1))
                    t.matmul(psf[0:1, 3072:3584], onesc_sb[:], ex[:, 512:1024],
                             start=(kt == 0), stop=(kt == cntk - 1))
                    t.matmul(psf[0:64, 2048:2560], vt[:, kt, 0:64], ex[:, 0:512],
                             start=(kt == 0), stop=(kt == cntk - 1), tile_position=(0, 0),
                             skip_group_check=True)
                    ins = t.matmul(psf[64:128, 2048:2560], vt[:, kt, 64:128], ex[:, 512:1024],
                                   start=(kt == 0), stop=(kt == cntk - 1), tile_position=(0, 64),
                                   skip_group_check=True)
                    inc("mmD", ins)

                emit_sc(0)
                emit_sc(1)
                for kt in range(2, cntk):
                    emit_av(kt - 2)
                    emit_sc(kt)
                emit_av(cntk - 2)
                emit_av(cntk - 1)
                t.wait_ge(sems["veD"], veD_b[("rcp", qst, hg)])
                t.matmul(psf[:, 1024:1536], indA_sb[:], rl_sb[0:1, 0:512], start=True, stop=False)
                ins = t.matmul(psf[:, 1024:1536], indB_sb[:], rl_sb[0:1, 512:1024], start=False, stop=True)
                inc("mmD", ins)
                if hg < 7:
                    for u in t_blocks[qst * 8 + hg]:
                        p5_tr(t, u[1], u[2])
            t.wait_ge(sems["veD"], veD_b[("accbf", qst, 7)])
            for tq in range(4):
                if qst > 0 or tq > 0:
                    t.wait_ge(sems["veD"], veD_b[("sev", qst)] - 4 + tq if tq > 0
                              else veD_b[("sev", qst - 1)])
                for ch in range(2):
                    for et in range(8):
                        t.matmul(psf[:, ch * 512:(ch + 1) * 512],
                                 accT[:, et, tq * 128:(tq + 1) * 128],
                                 wT["wo"][:, et, ch * 512:(ch + 1) * 512],
                                 start=(et == 0), stop=False)
                    ins = t.matmul(psf[:, ch * 512:(ch + 1) * 512], onesr_sb[:],
                                   beff_sb[0:1, ch * 512:(ch + 1) * 512],
                                   start=False, stop=True)
                inc("mmD", ins)
            for u in t_blocks[qst * 8 + 7]:
                p5_tr(t, u[1], u[2])
        assert cnt["mmD"] == MM_P4_END

        for u in t_blocks[NBLK]:
            p5_tr(t, u[1], u[2])
        assert cnt["mmE"] == MME_END

        t.wait_ge(sems["ldF"], ldF_b["xqT"])
        for qd in range(4):
            for oc in range(8):
                t.wait_ge(sems["ldF"], ldF_b[("gu", qd, oc)])
                par = oc % 2
                for t4 in range(4):
                    j = 4 * qd + t4
                    gch = (qd * 8 + oc) * 4 + t4
                    if gch >= 2:
                        g2 = gch - 2
                        pk = (g2 // 32, (g2 // 4) % 8, g2 % 4)
                        t.wait_ge(sems["seF"], seF_b[("sil", pk[0], pk[1], pk[2])])
                        t.wait_ge(sems["veF"], veF_b[("hid", pk[0], pk[1], pk[2])])
                    if qd > 0 and oc == 0 and t4 == 1:
                        t.wait_ge(sems["seF"], seF_b[("g", qd - 1, 1, 0)])
                    gb = 1024 * (t4 % 2)
                    for kt in range(8):
                        ins = t.matmul(psf[:, gb:gb + 512], xqT[:, kt, j * 128:(j + 1) * 128],
                                       wg_par[par][:, kt, :], start=(kt == 0), stop=(kt == 7))
                    inc("mmF", ins)
                    for kt in range(8):
                        ins = t.matmul(psf[:, gb + 512:gb + 1024], xqT[:, kt, j * 128:(j + 1) * 128],
                                       wu_par[par][:, kt, :], start=(kt == 0), stop=(kt == 7))
                    inc("mmF", ins)
                for p4_ in range(2):
                    t.wait_ge(sems["veF"], veF_b[("q", qd, oc, p4_)])
                    hqv0 = hqstage.rearrange("p (a b) -> p a b", a=4)
                    hqv1 = xq_stage[:, 0:512].rearrange("p (a b) -> p a b", a=4)
                    for c in range(4):
                        t.transpose(psb[:, c * 128:(c + 1) * 128], hqv0[:, c, :], ident_sb[:])
                    for c in range(4):
                        ins = t.transpose(psb[:, 512 + c * 128:640 + c * 128],
                                          hqv1[:, c, :], ident_sb[:])
                    inc("mmF", ins)
            for pr in range(2):
                for og in range(8):
                    t.wait_ge(sems["ldF"], ldF_b[("wd", qd, pr, og)])
                    for tp in range(2):
                        t4 = 2 * pr + tp
                        j = 4 * qd + t4
                        if og == 0:
                            t.wait_ge(sems["veF"], veF_b[("ev", qd, 7, pr)])
                            ppr = (qd, pr - 1) if pr > 0 else ((qd - 1, 1) if qd > 0 else None)
                            if ppr is not None:
                                t.wait_ge(sems["seF"], seF_b[("g", ppr[0], ppr[1], tp)])
                        base = 1024 + 1024 * tp
                        for ot in range(4):
                            go = og * 4 + ot
                            for ch in range(2):
                                ins = t.matmul(psf[:, base + ch * 512:base + (ch + 1) * 512],
                                               hqT_t[t4][:, go, :],
                                               wd_par[og % 2][:, ot, ch * 512:(ch + 1) * 512],
                                               start=(go == 0), stop=(go == 31))
                        inc("mmF", ins)
        assert cnt["mmF"] == MMF_END

    def emit_ln(v, se_sq, se_sqrt, lnx, out_bf, want_f32):
        v.tensor_reduce(scol(0), lnx, AX.X, ALU.add)
        v.wait_ge(sems["seA"], se_sq)
        v.tensor_reduce(scol(1), sqt[:], AX.X, ALU.add)
        v.drain()
        v.tensor_single_scalar(scol(2), scol(0), INV1024, ALU.mult)
        v.tensor_single_scalar(scol(4), scol(1), INV1024, ALU.mult)
        v.drain()
        v.tensor_scalar(sqt[:], lnx, scol(2), None, ALU.subtract)
        v.tensor_tensor(scol(3), scol(2), scol(2), ALU.mult)
        v.drain()
        v.scalar_tensor_tensor(scol(5), scol(3), -1.0, scol(4), ALU.mult, ALU.add)
        v.drain()
        v.tensor_single_scalar(scol(6), scol(5), EPS, ALU.add)
        v.drain()
        ins = v.reciprocal(scol(7), scol(6))
        inc("veA", ins)
        v.wait_ge(sems["seA"], se_sqrt)
        if want_f32:
            v.tensor_scalar(xnf[:], sqt[:], scol(8), None, ALU.mult)
        ins = v.tensor_scalar(out_bf, sqt[:], scol(8), None, ALU.mult)
        inc("veA", ins)

    def emit_rope(v, psrc_off, tabs, tabc, tabi, out_bf):
        v.tensor_copy(ropet[:, 0, :].rearrange("p (a b) -> p a b", b=32),
                      tabc[:, tabi, :].unsqueeze(1).broadcast_to([128, 16, 32]))
        v.tensor_copy(ropet[:, 1, :].rearrange("p (a b) -> p a b", b=32),
                      tabs[:, tabi, :].unsqueeze(1).broadcast_to([128, 16, 32]))
        v.tensor_copy(evacf[:], psf[:, psrc_off:psrc_off + 1024])
        v.drain()
        ev = evacf[:].rearrange("p (a two) -> p a two", two=2)
        e1 = ev[:, :, 0]; e2 = ev[:, :, 1]
        ob = out_bf.rearrange("p (a two) -> p a two", two=2)
        v.tensor_tensor(ropet[:, 2, :], e1, ropet[:, 0, :], ALU.mult)
        v.tensor_tensor(ropet[:, 3, :], e2, ropet[:, 1, :], ALU.mult)
        v.tensor_tensor(ropet[:, 4, :], e1, ropet[:, 1, :], ALU.mult)
        v.tensor_tensor(ropet[:, 5, :], e2, ropet[:, 0, :], ALU.mult)
        v.scalar_tensor_tensor(ob[:, :, 0], ropet[:, 3, :], -1.0, ropet[:, 2, :], ALU.mult, ALU.add)
        ins = v.scalar_tensor_tensor(ob[:, :, 1], ropet[:, 4, :], 1.0, ropet[:, 5, :], ALU.mult, ALU.add)
        return ins


    def emit_dq_chain(v, jobs):
        def both(f):
            for j in jobs:
                f(j)
        both(lambda j: v.tensor_scalar(j["sc"], j["am"], j["gs6"], 448.0, ALU.mult, ALU.min))
        both(lambda j: v.memset(j["pe"], 2.0 ** -6))
        for mlev in (8, 4, 2, 1):
            both(lambda j: v.tensor_single_scalar(j["th"], j["pe"], float(2.0 ** mlev), ALU.mult))
            both(lambda j: v.tensor_tensor(j["cc"], j["sc"], j["th"], ALU.is_ge))
            both(lambda j: v.tensor_scalar(j["fac"], j["cc"], float(2.0 ** mlev - 1), 1.0,
                                           ALU.mult, ALU.add))
            both(lambda j: v.tensor_tensor(j["pe"], j["pe"], j["fac"], ALU.mult))
        both(lambda j: v.reciprocal(j["th"], j["pe"]))
        both(lambda j: v.tensor_tensor(j["rr"], j["sc"], j["th"], ALU.mult))
        both(lambda j: v.tensor_single_scalar(j["f16"], j["rr"], 1.0 / 16.0, ALU.is_ge))
        both(lambda j: v.tensor_single_scalar(j["cc"], j["rr"], 3.0 / 16.0, ALU.is_ge))
        for jj in range(2, 16):
            tgt = "f16" if jj % 2 == 0 else "cc"
            both(lambda j, tgt=tgt, jj=jj: v.scalar_tensor_tensor(
                j[tgt], j["rr"], float((2 * jj + 1) / 16.0), j[tgt], ALU.is_ge, ALU.add))
        both(lambda j: v.tensor_tensor(j["f16"], j["f16"], j["cc"], ALU.add))
        both(lambda j: v.tensor_single_scalar(j["th"], j["pe"], 0.125, ALU.mult))
        both(lambda j: v.tensor_tensor(j["sc"], j["f16"], j["th"], ALU.mult))
        both(lambda j: v.tensor_scalar(j["am2"], j["sc"], j["invgs"], None, ALU.mult))
        both(lambda j: v.reciprocal(j["th"], j["am2"]))
        both(lambda j: v.tensor_single_scalar(j["rr"], j["am2"], 0.5, ALU.mult))
        ins = None
        for j in jobs:
            ins = v.tensor_single_scalar(j["th"], j["th"], 1e30, ALU.min)
        return ins

    def emit_main_chain(v, jobs, sign_wait):
        def both(f):
            for j in jobs:
                f(j)
        both(lambda j: v.tensor_tensor(j["a0"].rearrange("p (a b) -> p a b", b=16),
                                       j["src"].rearrange("p (a b) -> p a b", b=16),
                                       j["th"].unsqueeze(-1).broadcast_to(
                                           [128, j["nb"], 16]), ALU.mult))
        both(lambda j: v.scalar_tensor_tensor(j["ax"], j["a0"], -1.0, j["a0"], ALU.mult, ALU.max))
        both(lambda j: v.tensor_single_scalar(j["ca"], j["ax"], 0.25, ALU.is_ge))
        both(lambda j: v.scalar_tensor_tensor(j["ca"], j["ax"], 0.75, j["ca"], ALU.is_ge, ALU.add))
        both(lambda j: v.scalar_tensor_tensor(j["ca"], j["ax"], 1.25, j["ca"], ALU.is_ge, ALU.add))
        both(lambda j: v.scalar_tensor_tensor(j["ca"], j["ax"], 1.75, j["ca"], ALU.is_ge, ALU.add))
        both(lambda j: v.tensor_single_scalar(j["cb"], j["ax"], 2.5, ALU.is_ge))
        both(lambda j: v.scalar_tensor_tensor(j["cb"], j["ax"], 3.5, j["cb"], ALU.is_ge, ALU.add))
        both(lambda j: v.scalar_tensor_tensor(j["ca"], j["cb"], 2.0, j["ca"], ALU.mult, ALU.add))
        both(lambda j: v.tensor_single_scalar(j["cb"], j["ax"], 5.0, ALU.is_ge))
        both(lambda j: v.scalar_tensor_tensor(j["ca"], j["cb"], 4.0, j["ca"], ALU.mult, ALU.add))
        both(lambda j: v.tensor_tensor(j["ca"].rearrange("p (a b) -> p a b", b=16),
                                       j["ca"].rearrange("p (a b) -> p a b", b=16),
                                       j["rr"].unsqueeze(-1).broadcast_to(
                                           [128, j["nb"], 16]), ALU.mult))
        sign_wait()
        ins = None
        for j in jobs:
            ins = v.tensor_tensor(j["out"], j["ca"], j["sgn"], ALU.mult)
        return ins

    p5rows_s = dict(sc=sqt, pe=xnf, th=stile2, cc=xtile2, fac=gtile2, rr=cmbt, f16=evacf)

    def p5_vact(v, act):
        kind = act[0]
        if kind == "A":
            _, mi, r = act
            v.wait_ge(sems["ldE"], ldE_b[("A", mi, r)])
            if r == 0 and mi > 0:
                v.wait_ge(sems["ldE"], ldE_b[("spl", mi - 1)])
            wf = wf_a if par5[act] == 0 else wf_b
            ins = v.tensor_reduce(amaxc[:, r * 64:(r + 1) * 64],
                                  wf.rearrange("p (a b) -> p a b", b=16),
                                  AX.X, ALU.max, apply_absolute_value=True)
            inc("veE", ins)
        elif kind == "chain":
            mi = act[1]
            v.drain()
            v.tensor_reduce(sqt[:, 0:1], amaxc[:], AX.X, ALU.max)
            v.drain()
            v.transpose(sqt[:, 32:64], sqt[:, 0:32])
            v.drain()
            v.tensor_reduce(sqt[:, 64:65], sqt[:, 32:64], AX.X, ALU.max)
            v.drain()
            v.stream_shuffle(sqt[0:32, 65:66], sqt[0:32, 64:65], [0] * 32)
            v.stream_shuffle(sqt[0:32, 66:67], sqt[32:64, 64:65], [0] * 32)
            v.stream_shuffle(sqt[0:32, 67:68], sqt[64:96, 64:65], [0] * 32)
            v.stream_shuffle(sqt[0:32, 68:69], sqt[96:128, 64:65], [0] * 32)
            v.drain()
            v.tensor_reduce(sqt[:, 69:70], sqt[:, 65:69], AX.X, ALU.max)
            v.drain()
            v.stream_shuffle(sqt[32:64, 69:70], sqt[0:32, 69:70], list(range(32)))
            v.stream_shuffle(sqt[64:96, 69:70], sqt[0:32, 69:70], list(range(32)))
            v.stream_shuffle(sqt[96:128, 69:70], sqt[0:32, 69:70], list(range(32)))
            v.drain()
            v.reciprocal(sqt[:, 70:71], sqt[:, 69:70])
            v.drain()
            v.tensor_single_scalar(gs_sb[:, 6 + 2 * mi:7 + 2 * mi], sqt[:, 70:71],
                                   448.0, ALU.mult)
            ins = v.tensor_single_scalar(gs_sb[:, 7 + 2 * mi:8 + 2 * mi], sqt[:, 69:70],
                                         float(1.0 / 2688.0), ALU.mult)
            inc("veE", ins)
        elif kind == "B":
            _, mi, r = act
            if r % 16 == 0:
                half = r // 16
                if mi < 2:
                    v.wait_ge(sems["ldE"], ldE_b[("rld", mi, half)])
                jobs = []
                for ji in range(2):
                    co = ji * 512
                    jd = {k: ap[:, co:co + 512] for k, ap in p5rows_s.items()}
                    jd["am"] = (dqf[:, co:co + 512] if mi < 2
                                else amaxc[:, half * 1024 + co: half * 1024 + co + 512])
                    jd["am2"] = jd["fac"]
                    jd["gs6"] = gs_sb[:, 6 + 2 * mi:7 + 2 * mi]
                    jd["invgs"] = gs_sb[:, 7 + 2 * mi:8 + 2 * mi]
                    jobs.append(jd)
                emit_dq_chain(v, jobs)
            v.wait_ge(sems["ldE"], ldE_b[("B", mi, r)])
            p2 = bprev2[("B", mi, r)]
            if p2 is not None:
                v.wait_ge(sems["mmE"], mmE_b[("tr", p2[1], p2[2])])
            wf = wf_a if par5[act] == 0 else wf_b
            off = (r % 16) * 64
            mjobs = []
            for ji in range(2):
                sl = slice(ji * 512, (ji + 1) * 512)
                mjobs.append(dict(
                    src=wf[:, sl], out=xq_par[r % 2][:, sl], sgn=sgn_buf[:, sl],
                    th=p5rows_s["th"][:, off + ji * 32: off + (ji + 1) * 32],
                    rr=p5rows_s["rr"][:, off + ji * 32: off + (ji + 1) * 32], nb=32,
                    a0=qs[0][:, sl], ax=qs[1][:, sl], ca=qs[2][:, sl], cb=qs[3][:, sl],
                ))

            def _sw(mi=mi, r=r):
                v.wait_ge(sems["seE"], seE_b[("sgn", mi, r)])
            ins = emit_main_chain(v, mjobs, _sw)
            inc("veE", ins)
        else:
            _, mi, r = act
            v.wait_ge(sems["mmE"], mmE_b[("tr", mi, r)])
            p2 = bprev2[("B", mi, r)]
            if p2 is not None:
                v.wait_ge(sems["stE"], stE_b[("st", p2[1], p2[2])])
            ins = v.tensor_copy(stage8[:, r % 2, :], psb[:])
            inc("veE", ins)

    def emit_quant(v, src, out_bf, W, gs6_col, invgs_col, sign_wait, amax_ap=None,
                   sgn_ap=None):
        if sgn_ap is None:
            sgn_ap = sgn_buf[:, 0:W]
        Wh = W // 2
        nb = Wh // 16
        jobs = []
        for ji in range(2):
            co = ji * 64
            sl = slice(ji * Wh, (ji + 1) * Wh)
            qsl = slice(ji * Wh, (ji + 1) * Wh) if W <= 1024 else None
            jobs.append(dict(
                src=src[:, sl], out=out_bf[:, sl], sgn=sgn_ap[:, sl],
                am=(dqsc[:, 0, co:co + nb] if amax_ap is None else amax_ap[:, ji * nb:(ji + 1) * nb]),
                amr=dqsc[:, 0, co:co + nb],
                sc=dqsc[:, 1, co:co + nb], pe=dqsc[:, 2, co:co + nb],
                th=dqsc[:, 3, co:co + nb], cc=dqsc[:, 4, co:co + nb],
                fac=dqsc[:, 5, co:co + nb], rr=dqsc[:, 6, co:co + nb],
                f16=dqsc[:, 7, co:co + nb],
                a0=qs[0][:, sl], ax=qs[1][:, sl], ca=qs[2][:, sl], cb=qs[3][:, sl],
            ))

        def both(f):
            for j in jobs:
                f(j)

        if amax_ap is None:
            both(lambda j: v.tensor_reduce(j["am"], j["src"].rearrange("p (a b) -> p a b", b=16),
                                           AX.X, ALU.max, apply_absolute_value=True))
        both(lambda j: v.tensor_scalar(j["sc"], j["am"], gs6_col, 448.0, ALU.mult, ALU.min))
        both(lambda j: v.memset(j["pe"], 2.0 ** -6))
        for mlev in (8, 4, 2, 1):
            both(lambda j: v.tensor_single_scalar(j["th"], j["pe"], float(2.0 ** mlev), ALU.mult))
            both(lambda j: v.tensor_tensor(j["cc"], j["sc"], j["th"], ALU.is_ge))
            both(lambda j: v.tensor_scalar(j["fac"], j["cc"], float(2.0 ** mlev - 1), 1.0,
                                           ALU.mult, ALU.add))
            both(lambda j: v.tensor_tensor(j["pe"], j["pe"], j["fac"], ALU.mult))
        both(lambda j: v.reciprocal(j["th"], j["pe"]))
        both(lambda j: v.tensor_tensor(j["rr"], j["sc"], j["th"], ALU.mult))
        both(lambda j: v.tensor_single_scalar(j["f16"], j["rr"], 1.0 / 16.0, ALU.is_ge))
        both(lambda j: v.tensor_single_scalar(j["cc"], j["rr"], 3.0 / 16.0, ALU.is_ge))
        for jj in range(2, 16):
            tgt = "f16" if jj % 2 == 0 else "cc"
            both(lambda j, tgt=tgt, jj=jj: v.scalar_tensor_tensor(
                j[tgt], j["rr"], float((2 * jj + 1) / 16.0), j[tgt], ALU.is_ge, ALU.add))
        both(lambda j: v.tensor_tensor(j["f16"], j["f16"], j["cc"], ALU.add))
        both(lambda j: v.tensor_single_scalar(j["th"], j["pe"], 0.125, ALU.mult))
        both(lambda j: v.tensor_tensor(j["sc"], j["f16"], j["th"], ALU.mult))
        both(lambda j: v.tensor_scalar(j["amr"], j["sc"], invgs_col, None, ALU.mult))
        both(lambda j: v.reciprocal(j["th"], j["amr"]))
        both(lambda j: v.tensor_single_scalar(j["rr"], j["amr"], 0.5, ALU.mult))
        both(lambda j: v.tensor_single_scalar(j["th"], j["th"], 1e30, ALU.min))
        both(lambda j: v.tensor_tensor(j["a0"].rearrange("p (a b) -> p a b", b=16),
                                       j["src"].rearrange("p (a b) -> p a b", b=16),
                                       j["th"].unsqueeze(-1).broadcast_to([128, nb, 16]),
                                       ALU.mult))
        both(lambda j: v.scalar_tensor_tensor(j["ax"], j["a0"], -1.0, j["a0"], ALU.mult, ALU.max))
        both(lambda j: v.tensor_single_scalar(j["ca"], j["ax"], 0.25, ALU.is_ge))
        both(lambda j: v.scalar_tensor_tensor(j["ca"], j["ax"], 0.75, j["ca"], ALU.is_ge, ALU.add))
        both(lambda j: v.scalar_tensor_tensor(j["ca"], j["ax"], 1.25, j["ca"], ALU.is_ge, ALU.add))
        both(lambda j: v.scalar_tensor_tensor(j["ca"], j["ax"], 1.75, j["ca"], ALU.is_ge, ALU.add))
        both(lambda j: v.tensor_single_scalar(j["cb"], j["ax"], 2.5, ALU.is_ge))
        both(lambda j: v.scalar_tensor_tensor(j["cb"], j["ax"], 3.5, j["cb"], ALU.is_ge, ALU.add))
        both(lambda j: v.scalar_tensor_tensor(j["ca"], j["cb"], 2.0, j["ca"], ALU.mult, ALU.add))
        both(lambda j: v.tensor_single_scalar(j["cb"], j["ax"], 5.0, ALU.is_ge))
        both(lambda j: v.scalar_tensor_tensor(j["ca"], j["cb"], 4.0, j["ca"], ALU.mult, ALU.add))
        both(lambda j: v.tensor_tensor(j["ca"].rearrange("p (a b) -> p a b", b=16),
                                       j["ca"].rearrange("p (a b) -> p a b", b=16),
                                       j["rr"].unsqueeze(-1).broadcast_to([128, nb, 16]),
                                       ALU.mult))
        sign_wait()
        v.tensor_tensor(jobs[0]["out"], jobs[0]["ca"], jobs[0]["sgn"], ALU.mult)
        ins = v.tensor_tensor(jobs[1]["out"], jobs[1]["ca"], jobs[1]["sgn"], ALU.mult)
        return ins

    @blk.vector
    def _(v):
        v.wait_ge(sems["ld"], ck["ld_const"])
        for mi in range(4):
            for r in range(8):
                v.wait_ge(sems["mmA"], mi * 8 + r + 1)
                ins = v.tensor_copy(wT[("wq", "wk", "wv", "wo")[mi]][:, :, r * 128:(r + 1) * 128],
                                    psb[:].rearrange("p (a b) -> p a b", a=8))
                inc("veA", ins)
        v.tensor_copy(biasb_sb[:, 0:D], bias_sb[:, 0:D])
        v.tensor_copy(biasb_sb[:, D:2 * D], bias_sb[:, D:2 * D])
        v.reciprocal(gs_sb[:, 3:4], gs_sb[:, 0:1])
        v.reciprocal(gs_sb[:, 5:6], gs_sb[:, 1:2])
        ins = v.tensor_copy(bvcol_sb[:], bvcolf_sb[:])
        inc("veA", ins)
        v.wait_ge(sems["mmA"], MM_P0)
        ins = v.tensor_tensor(beff_sb[:], psf[0:1, 2048:3072], bias_sb[:, 3 * D:4 * D], ALU.add)
        inc("veA", ins)
        v.tensor_tensor(tab_ks[:].rearrange("p a b -> p (a b)"),
                        th_sb[:, 0:1024], th_sb[:, 1024:2048], ALU.add)
        v.tensor_tensor(tab_kc[:].rearrange("p a b -> p (a b)"),
                        th_sb[:, 2048:3072], th_sb[:, 3072:4096], ALU.add)
        v.tensor_tensor(tab_qs[:].rearrange("p a b -> p (a b)"),
                        th_sb[:, 4096:4608], th_sb[:, 4608:5120], ALU.add)
        ins = v.tensor_tensor(tab_qc[:].rearrange("p a b -> p (a b)"),
                              th_sb[:, 5120:5632], th_sb[:, 5632:6144], ALU.add)
        inc("veA", ins)
        assert cnt["veA"] == VE_P0

        for i in range(NKV):
            lnx = lnx_a if i % 2 == 0 else lnx_b
            xt = xtile_a if i % 2 == 0 else xtile_b
            v.wait_ge(sems["ld"], ck[f"p1_ld_{i}"])
            v.wait_ge(sems["seA"], 1)
            emit_ln(v, SE_P1(i, 1), SE_P1(i, 2), lnx, xt[:], want_f32=False)
            v.wait_ge(sems["mmA"], MM_P1(i, 1))
            v.wait_ge(sems["stB"], 32 * i)
            ins = v.tensor_copy(stage8[:, 0, :], psb[:])
            inc("veA", ins)
            v.wait_ge(sems["mmA"], MM_P1(i, 2))
            ins = emit_rope(v, 0, tab_ks, tab_kc, i, xt[:])
            inc("veA", ins)
            v.wait_ge(sems["mmA"], MM_P1(i, 3))
            v.tensor_copy(stage8[:, 1, :], psf[:, 1024:2048])
            v.wait_ge(sems["mmA"], MM_P1(i, 4))
            ins = v.tensor_copy(stage8[:, 0, :], psb[:])
            inc("veA", ins)
        assert cnt["veA"] == VE_P1_END

        for i in range(NQT):
            lnx = lnx_a if i % 2 == 0 else lnx_b
            xt = xtile_a if i % 2 == 0 else xtile_b
            v.wait_ge(sems["ld"], ck[f"p2_ld_{i}"])
            emit_ln(v, SE_P2(i, 1), SE_P2(i, 2), lnx, xt[:], want_f32=True)
            v.wait_ge(sems["mmA"], MM_P2(i, 1))
            v.wait_ge(sems["stB"], ST_P1_END + 16 * i)
            ins = v.tensor_copy(stage8[:, 0, :], psb[:])
            inc("veA", ins)
            v.wait_ge(sems["mmA"], MM_P2(i, 2))
            ins = emit_rope(v, 0, tab_qs, tab_qc, i, xt[:])
            inc("veA", ins)
            v.wait_ge(sems["mmA"], MM_P2(i, 3))
            ins = v.tensor_copy(qT[:, :, i * 128:(i + 1) * 128], psb[:].rearrange("p (a b) -> p a b", a=8))
            inc("veA", ins)
            def _sw(i=i):
                v.wait_ge(sems["seA"], SE_P2(i, 3))
            ins = emit_quant(v, xnf, xq_stage, 1024, gs_sb[:, 2:3], gs_sb[:, 3:4], _sw,
                             sgn_ap=sgn_buf)
            inc("veA", ins)
            v.wait_ge(sems["mmA"], MM_P2(i, 4))
            ins = v.tensor_copy(stage8[:, 0, :], psb[:])
            inc("veA", ins)
        assert cnt["veA"] == VE_P2_END

        for qst in range(QSTN):
            cntk = PC[qst]
            pump_tail = None
            for hg in range(8):
                b = qst * 8 + hg
                part1, part2 = vsplit[b]
                state = {"p1": list(part1), "p2": list(part2), "k": 0, "npts": 10}
                while state["p1"] and state["p1"][0][0] == "ev":
                    p5_vact(v, state["p1"].pop(0))

                def pump(state=state):
                    state["k"] += 1
                    k = state["k"]
                    if k <= 7:
                        src, rem = state["p1"], 7 - k
                    else:
                        if state["p1"]:
                            state["p2"] = state["p1"] + state["p2"]
                            state["p1"] = []
                        src, rem = state["p2"], state["npts"] - k
                    n = len(src)
                    take = n if rem <= 0 else -(-n // (rem + 1))
                    for _ in range(take):
                        p5_vact(v, src.pop(0))
                if hg == 7:
                    pump_tail = pump
                nmask_start = cntk - 8
                for j in range(8):
                    kt = nmask_start + j
                    buf = kt % 2
                    v.wait_ge(sems["seD"], seD_b[("expb", qst, hg)] + kt + 1)
                    ex = expTb[:, buf, :]
                    v.tensor_tensor(ex[:, 0:512], ex[:, 0:512], maskbuf[:, j, :], ALU.mult)
                    ins = v.tensor_tensor(ex[:, 512:1024], ex[:, 512:1024], maskbuf[:, j, :], ALU.mult)
                    inc("veD", ins)
                    pump()
                v.wait_ge(sems["mmD"], mmD_b[("av", qst, hg)])
                v.reciprocal(rl_sb[0:1, 0:512], psf[0:1, 2560:3072])
                ins = v.reciprocal(rl_sb[0:1, 512:1024], psf[0:1, 3072:3584])
                inc("veD", ins)
                pump()
                v.wait_ge(sems["mmD"], mmD_b[("rl", qst, hg)])
                v.tensor_copy(evacf[:, 0:512], psf[:, 1024:1536])
                v.drain()
                ins = v.tensor_tensor(accT[:, hg, :], psf[:, 2048:2560],
                                      evacf[:, 0:512], ALU.mult)
                inc("veD", ins)
                pump()
            for tq in range(4):
                v.wait_ge(sems["mmD"], mmD_b[("wo", qst)] - 4 + tq + 1)
                v.wait_ge(sems["stD"], 16 * (4 * qst + tq))
                ins = v.tensor_copy(stile[:], psf[:, 0:1024])
                inc("veD", ins)
                pump_tail()
        assert cnt["veD"] == VE_P4_END
        for act in vprog[NBLK]:
            p5_vact(v, act)
        assert cnt["veE"] == VEE_END


        for qd in range(4):
            for oc in range(8):
                pend_amax = []
                for t4 in range(4):
                    v.wait_ge(sems["mmF"], mmF_b[("u", qd, oc, t4)])
                    v.wait_ge(sems["seF"], seF_b[("sil", qd, oc, t4)])
                    ins = v.tensor_tensor(hidf4[:, t4, :], hsg_par[t4 % 2][:],
                                          psf[:, 1024 * (t4 % 2) + 512:1024 * (t4 % 2) + 1024],
                                          ALU.mult)
                    inc("veF", ins)
                    pend_amax.append(t4)
                    if len(pend_amax) >= 2:
                        ta = pend_amax.pop(0)
                        v.tensor_reduce(dqsc[:, 0, ta * 32:(ta + 1) * 32],
                                        hidf4[:, ta, :].rearrange("p (a b) -> p a b", b=16),
                                        AX.X, ALU.max, apply_absolute_value=True)
                for ta in pend_amax:
                    v.tensor_reduce(dqsc[:, 0, ta * 32:(ta + 1) * 32],
                                    hidf4[:, ta, :].rearrange("p (a b) -> p a b", b=16),
                                    AX.X, ALU.max, apply_absolute_value=True)
                jobs = []
                for ji in range(2):
                    co = ji * 64
                    jobs.append(dict(
                        am=dqsc[:, 0, co:co + 64], am2=dqsc[:, 0, co:co + 64],
                        sc=dqsc[:, 1, co:co + 64], pe=dqsc[:, 2, co:co + 64],
                        th=dqsc[:, 3, co:co + 64], cc=dqsc[:, 4, co:co + 64],
                        fac=dqsc[:, 5, co:co + 64], rr=dqsc[:, 6, co:co + 64],
                        f16=dqsc[:, 7, co:co + 64],
                        gs6=gs_sb[:, 4:5], invgs=gs_sb[:, 5:6],
                    ))
                ins = emit_dq_chain(v, jobs)
                inc("veF", ins)
                for p4_ in range(2):
                    if (qd, oc, p4_) != (0, 0, 0):
                        pk = (qd, oc, p4_ - 1) if p4_ > 0 else ((qd, oc - 1, 1) if oc > 0
                                                                else (qd - 1, 7, 1))
                        v.wait_ge(sems["mmF"], mmF_b[("tr", pk[0], pk[1], pk[2])])
                    mjobs = []
                    for ji in range(2):
                        t4 = 2 * p4_ + ji
                        co = t4 * 32
                        sl = slice(ji * 512, (ji + 1) * 512)
                        mjobs.append(dict(
                            src=hidf4[:, t4, :],
                            out=(hqstage[:, 0:512] if ji == 0 else xq_stage[:, 0:512]),
                            sgn=sgn4s[t4],
                            th=dqsc[:, 3, co:co + 32], rr=dqsc[:, 6, co:co + 32], nb=32,
                            a0=qs[0][:, sl], ax=qs[1][:, sl], ca=qs[2][:, sl], cb=qs[3][:, sl],
                        ))

                    def _sw2(qd=qd, oc=oc, p4_=p4_):
                        v.wait_ge(sems["seF"], seF_b[("sgn", qd, oc, 2 * p4_ + 1)])
                    ins = emit_main_chain(v, mjobs, _sw2)
                    inc("veF", ins)
                    v.wait_ge(sems["mmF"], mmF_b[("tr", qd, oc, p4_)])
                    v.tensor_copy(hqT_t[2 * p4_][:, oc * 4:(oc + 1) * 4, :],
                                  psb[:, 0:512].rearrange("p (a b) -> p a b", a=4))
                    ins = v.tensor_copy(hqT_t[2 * p4_ + 1][:, oc * 4:(oc + 1) * 4, :],
                                        psb[:, 512:1024].rearrange("p (a b) -> p a b", a=4))
                    inc("veF", ins)
            for pr in range(2):
                for tp in range(2):
                    v.wait_ge(sems["seF"], seF_b[("g", qd, pr, tp)])
                    v.wait_ge(sems["ldF"], ldF_b[("sx", qd, pr, tp)])
                    v.tensor_tensor(cmbt[:, 0:512], stile2[:, 0:512], gtile2[:, 0:512], ALU.mult)
                    v.tensor_tensor(cmbt[:, 512:1024], stile2[:, 512:1024], gtile2[:, 512:1024], ALU.mult)
                    v.tensor_tensor(stile2[:, 0:512], cmbt[:, 0:512], xtile2[:, 0:512], ALU.add)
                    ins = v.tensor_tensor(stile2[:, 512:1024], cmbt[:, 512:1024], xtile2[:, 512:1024], ALU.add)
                    inc("veF", ins)
        assert cnt["veF"] == VEF_END

    @blk.scalar
    def _(sc_):
        sc_.wait_ge(sems["ld"], ck["ld_const"])
        sc_.mul(gs_sb[:, 2:3], gs_sb[:, 0:1], 1.0 / 6.0)
        sc_.mul(gs_sb[:, 4:5], gs_sb[:, 1:2], 1.0 / 6.0)
        sc_.wait_ge(sems["veA"], VE_P0)
        sc_.activation(tab_ks[:].rearrange("p a b -> p (a b)"),
                       tab_ks[:].rearrange("p a b -> p (a b)"), AF.Sin)
        sc_.activation(tab_kc[:].rearrange("p a b -> p (a b)"),
                       tab_kc[:].rearrange("p a b -> p (a b)"), AF.Sin)
        sc_.activation(tab_qs[:].rearrange("p a b -> p (a b)"),
                       tab_qs[:].rearrange("p a b -> p (a b)"), AF.Sin)
        sc_.activation(tab_qc[:].rearrange("p a b -> p (a b)"),
                       tab_qc[:].rearrange("p a b -> p (a b)"), AF.Sin)
        sc_.mul(tab_qs[:].rearrange("p a b -> p (a b)"),
                tab_qs[:].rearrange("p a b -> p (a b)"), 0.125)
        ins = sc_.mul(tab_qc[:].rearrange("p a b -> p (a b)"),
                      tab_qc[:].rearrange("p a b -> p (a b)"), 0.125)
        inc("seA", ins)
        for i in range(NKV):
            sc_.wait_ge(sems["ld"], ck[f"p1_ld_{i}"])
            if i >= 1:
                sc_.wait_ge(sems["veA"], VE_P1(i - 1, 2))
            lnx = lnx_a if i % 2 == 0 else lnx_b
            ins = sc_.activation(sqt[:], lnx, AF.Square)
            inc("seA", ins)
            sc_.wait_ge(sems["veA"], VE_P1(i, 1))
            ins = sc_.activation(scol(8), scol(7), AF.Sqrt)
            inc("seA", ins)
            sc_.wait_ge(sems["veA"], VE_P1(i, 5))
            inc("stB", sc_.dma_start(out=kT_sp[:, :, i * 128:(i + 1) * 128],
                                     in_=stage8[:, 0, :].rearrange("p (a b) -> p a b", a=8)), 16)
            inc("stB", sc_.dma_start(out=v_sp[:, i, :], in_=stage8[:, 1, :]), 16)
        for i in range(NQT):
            sc_.wait_ge(sems["ld"], ck[f"p2_ld_{i}"])
            sc_.wait_ge(sems["veA"], VE_P1(NKV - 1, 2) if i == 0 else VE_P2(i - 1, 2))
            lnx = lnx_a if i % 2 == 0 else lnx_b
            ins = sc_.activation(sqt[:], lnx, AF.Square)
            inc("seA", ins)
            sc_.wait_ge(sems["veA"], VE_P2(i, 1))
            ins = sc_.activation(scol(8), scol(7), AF.Sqrt)
            inc("seA", ins)
            sc_.wait_ge(sems["veA"], VE_P2(i, 2))
            ins = sc_.activation(sgn_buf[:], xnf[:], AF.Sign)
            inc("seA", ins)
            sc_.wait_ge(sems["veA"], VE_P2(i, 7))
            inc("stB", sc_.dma_start(out=xqT_sp[:, :, i * 128:(i + 1) * 128],
                                     in_=stage8[:, 0, :].rearrange("p (a b) -> p a b", a=8)), 16)

        for qst in range(QSTN):
            cntk = PC[qst]
            for hg in range(8):
                b = qst * 8 + hg
                nmask_start = cntk - 8
                bqh = mmD_b[("base", qst, hg)]
                for kt in range(cntk):
                    buf = kt % 2
                    sc_.wait_ge(sems["mmD"], mm_sc_val(bqh, kt))
                    if kt >= 2:
                        sc_.wait_ge(sems["mmD"], mm_av_val(bqh, kt - 2, cntk))
                        if kt - 2 >= nmask_start:
                            sc_.wait_ge(sems["veD"],
                                        veD_b[("mskb", qst, hg)] + (kt - 2 - nmask_start) + 1)
                    ins = sc_.activation(expTb[:, buf, :], psf[:, buf * 1024:(buf + 1) * 1024], AF.Exp)
                    inc("seD", ins)
                for act in vprog[b]:
                    if act[0] == "B":
                        p5_sgn(sc_, act[1], act[2])
                    elif act[0] == "ev":
                        p5_st(sc_, act[1], act[2])
            for tq in range(4):
                sc_.wait_ge(sems["veD"], veD_b[("sev", qst)] - 4 + tq + 1)
                row = qst * 512 + tq * 128
                inc("stD", sc_.dma_start(out=s_sp[row:row + 128, :], in_=stile[:]), 16)

        for act in vprog[NBLK]:
            if act[0] == "B":
                p5_sgn(sc_, act[1], act[2])
            elif act[0] == "ev":
                p5_st(sc_, act[1], act[2])
        assert cnt["seE"] == SEE_END and cnt["stE"] == STE_END

        for qd in range(4):
            for oc in range(8):
                for t4 in range(4):
                    sc_.wait_ge(sems["mmF"], mmF_b[("g", qd, oc, t4)])
                    if (qd, oc, t4) != (0, 0, 0):
                        pk = (qd, oc, t4 - 2) if t4 >= 2 else ((qd, oc - 1, t4 + 2) if oc > 0
                                                               else (qd - 1, 7, t4 + 2))
                        if pk[2] <= 3 and pk[0] >= 0:
                            sc_.wait_ge(sems["veF"], veF_b[("hid", pk[0], pk[1], pk[2])])
                    ins = sc_.activation(hsg_par[t4 % 2][:],
                                         psf[:, 1024 * (t4 % 2):1024 * (t4 % 2) + 512], AF.Silu)
                    inc("seF", ins)
                    sc_.wait_ge(sems["veF"], veF_b[("hid", qd, oc, t4)])
                    ins = sc_.activation(sgn4s[t4], hidf4[:, t4, :], AF.Sign)
                    inc("seF", ins)
            for pr in range(2):
                for tp in range(2):
                    t4 = 2 * pr + tp
                    j = 4 * qd + t4
                    sc_.wait_ge(sems["mmF"], mmF_b[("dn", qd, pr, 7, tp)])
                    ins = sc_.activation(gtile2[:], psf[:, 1024 + 1024 * tp:2048 + 1024 * tp], AF.Silu)
                    inc("seF", ins)
                    sc_.wait_ge(sems["veF"], veF_b[("cmb", qd, pr, tp)])
                    ins = sc_.dma_start(out=out_d[j * 128:(j + 1) * 128, :], in_=stile2[:])
                    inc("outS", ins, 16)

        es.close()
    return nc




_PROGRAM_CACHE = {}


def kernel(**inputs):
    in_maps = host_prep(inputs)
    if "nc" not in _PROGRAM_CACHE:
        _PROGRAM_CACHE["nc"] = build_program()
    nc = _PROGRAM_CACHE["nc"]
    res = run_bass_kernel_spmd(nc, in_maps, core_ids=list(range(8)))
    out = np.empty((B, S, D), dtype=np.float32)
    for c in range(8):
        b, sub = c // 2, c % 2
        q0s = Q0S[sub]
        oc = np.asarray(res.results[c]["out"])
        for sl, q0 in enumerate(q0s):
            out[b, q0:q0 + 512, :] = oc[sl * 512:(sl + 1) * 512, :]
    return out
